# version 04
# baseline (speedup 1.0000x reference)
import contextlib
import numpy as np
import ml_dtypes
import concourse.bass as bass
import concourse.mybir as mybir
from concourse.bass_utils import run_bass_kernel_spmd

F32 = mybir.dt.float32
BF16 = mybir.dt.bfloat16
ALU = mybir.AluOpType
ACT = mybir.ActivationFunctionType
AX = mybir.AxisListType.X
BF = ml_dtypes.bfloat16

N_CORES = 8
D = 1024
DC = 8
HFF = 512
NEXP = 10
NROUTED = 8
B = 4
S = 4096
TOWN = 2048
TT = 512
NTILES = TOWN // TT
NQT = S // 128
NEG = -1.0e9

PHASES = {"ffn1": True, "attn": True, "ffn2": True}
DEBUG = False
ATT = {"qkv": True, "passa": True, "passb": True, "summary": True,
       "strided": 3, "norm": True, "ag2": True, "wo": True}


def _masks_np():
    r = np.arange(128)[:, None]
    c = np.arange(256)[None, :]
    okA = (c - r >= 65) & (c - r <= 128)
    okA0 = (r - c >= 0) & (r - c < 64)
    cc = c - 128
    okB = (c >= 128) & (cc // 64 == r // 64) & (cc <= r)
    okB0 = (c < 128) & (c // 64 == r // 64) & (c <= r)

    def addm(ok):
        return np.where(ok, 0.0, NEG).astype(np.float32)

    def mulT(ok):
        m = np.zeros((128, 256), dtype=BF)
        m[:, 0:128] = ok[:, 0:128].T.astype(BF)
        m[:, 128:256] = ok[:, 128:256].T.astype(BF)
        return m

    msum = np.zeros((64, NQT * 128), dtype=BF)
    cc64 = np.arange(64)[:, None]
    rr = np.arange(128)[None, :]
    for qt in range(NQT):
        msum[:, qt * 128:(qt + 1) * 128] = (cc64 < 2 * qt + (rr >= 64)).astype(BF)
    bj = np.arange(64)[:, None]
    qi = np.arange(64)[None, :]
    m1 = (bj < qi).astype(BF)
    mstr = np.zeros((128, 512), dtype=BF)
    for g in range(8):
        mstr[0:64, g * 64:(g + 1) * 64] = m1
        mstr[64:128, g * 64:(g + 1) * 64] = m1
    return (addm(okA), addm(okA0), addm(okB), addm(okB0),
            mulT(okA), mulT(okA0), mulT(okB), mulT(okB0), msum, mstr)


class Ctx:
    def __init__(self, nc, es):
        self.nc = nc
        self.es = es
        self.sems = {}
        self.lastw = {}
        self.marks = {}

    def sem(self, name):
        if name not in self.sems:
            h = self.es.enter_context(self.nc.semaphore(name))
            s = type("S", (), {})()
            s.h, s.n, s.name = h, 0, name
            self.sems[name] = s
        return self.sems[name]

    def W(self, eng, s, val):
        if val is None or val <= 0:
            return
        key = (eng, s.name)
        if self.lastw.get(key, -1) >= val:
            return
        getattr(self.nc, eng).wait_ge(s.h, val)
        self.lastw[key] = val

    def inc(self, instr, s, k=1):
        instr.then_inc(s.h, k)
        s.n += k
        return s.n

    def barrier(self, engines=("tensor", "vector", "scalar", "sync", "gpsimd"), skip=()):
        for e in engines:
            for s in self.sems.values():
                if s.name in skip:
                    continue
                self.W(e, s, s.n)


def build_nc():
    nc = bass.Bass("TRN2", target_bir_lowering=False)
    T, V, SC, SY, GP = nc.tensor, nc.vector, nc.scalar, nc.sync, nc.gpsimd

    xT = nc.dram_tensor("xT", [D, TOWN], F32, kind="ExternalInput").ap()
    wqT = nc.dram_tensor("wqT", [D, 128], F32, kind="ExternalInput").ap()
    wkT = nc.dram_tensor("wkT", [D, 128], F32, kind="ExternalInput").ap()
    wvT = nc.dram_tensor("wvT", [D, 128], F32, kind="ExternalInput").ap()
    woT = nc.dram_tensor("woT", [D, D], F32, kind="ExternalInput").ap()
    rw = nc.dram_tensor("rw", [D, NROUTED], F32, kind="ExternalInput").ap()
    rbc = nc.dram_tensor("rbc", [NROUTED, 1], F32, kind="ExternalInput").ap()
    w1 = nc.dram_tensor("w1", [NROUTED, D, HFF], F32, kind="ExternalInput").ap()
    w3 = nc.dram_tensor("w3", [NROUTED, D, HFF], F32, kind="ExternalInput").ap()
    w2 = nc.dram_tensor("w2", [NROUTED, HFF, D], F32, kind="ExternalInput").ap()
    s1 = nc.dram_tensor("s1", [2, D, HFF], F32, kind="ExternalInput").ap()
    s3 = nc.dram_tensor("s3", [2, D, HFF], F32, kind="ExternalInput").ap()
    s2 = nc.dram_tensor("s2", [2, HFF, D], F32, kind="ExternalInput").ap()
    yT = nc.dram_tensor("yT", [D, TOWN], F32, kind="ExternalOutput").ap()

    if DEBUG:
        dbg_t = {
            "dbg_xln": nc.dram_tensor("dbg_xln", [128, DC * TT], BF16),
            "dbg_l": nc.dram_tensor("dbg_l", [NROUTED, TT], BF16),
            "dbg_g": nc.dram_tensor("dbg_g", [NROUTED, TT], BF16),
            "dbg_hh": nc.dram_tensor("dbg_hh", [128, NEXP * 4 * TT], BF16),
        }
    cc_xa_in = nc.dram_tensor("cc_xa_in", [D, TOWN], BF16)
    cc_xa_out = nc.dram_tensor("cc_xa_out", [N_CORES, D, TOWN], BF16, addr_space="Shared")
    vperm_dr = nc.dram_tensor("vperm_dr", [S, 130], BF16)
    cc_o_in = nc.dram_tensor("cc_o_in", [B, 2, 64, S], BF16)
    cc_o_out = nc.dram_tensor("cc_o_out", [B, N_CORES, 2, 64, S], BF16, addr_space="Shared")

    (mA_np, mA0_np, mB_np, mB0_np, mAT_np, mA0T_np, mBT_np, mB0T_np,
     msum_np, mstr_np) = _masks_np()
    oneh_np = np.zeros((NROUTED, NROUTED * 128), dtype=BF)
    for e in range(NROUTED):
        oneh_np[e, e * 128:(e + 1) * 128] = 1.0
    consts = [
        ("c_ident", np.eye(128, dtype=BF)), ("c_ones", np.ones((128, 128), dtype=BF)),
        ("c_oneh", oneh_np),
        ("c_onesf", np.ones((128, 64), dtype=np.float32)),
        ("c_mA", mA_np), ("c_mA0", mA0_np), ("c_mB", mB_np), ("c_mB0", mB0_np),
        ("c_mAT", mAT_np), ("c_mA0T", mA0T_np), ("c_mBT", mBT_np), ("c_mB0T", mB0T_np),
        ("c_msum", msum_np), ("c_mstr", mstr_np),
    ]
    c_handles = {n: nc.inline_tensor(a, name=n) for n, a in consts}
    c_shapes = {n: a for n, a in consts}

    es_top = contextlib.ExitStack()
    with es_top:
        ctx = Ctx(nc, es_top)
        sb = lambda name, shape, dt: es_top.enter_context(nc.sbuf_tensor(name, shape, dt))

        x_res = sb("x_res", [128, DC * TOWN], F32)
        csb = {}
        for n, a in consts:
            dt = BF16 if a.dtype == BF else F32
            csb[n] = sb(n[2:], list(a.shape), dt)
        ident, ones = csb["c_ident"], csb["c_ones"]
        mA, mA0, mB, mB0 = csb["c_mA"], csb["c_mA0"], csb["c_mB"], csb["c_mB0"]
        mAT, mA0T, mBT, mB0T = csb["c_mAT"], csb["c_mA0T"], csb["c_mBT"], csb["c_mB0T"]
        msum, mstr = csb["c_msum"], csb["c_mstr"]
        oneh = csb["c_oneh"]
        onesf = csb["c_onesf"]
        rw_sb = sb("rw_sb", [128, DC * NROUTED], BF16)
        rb_sb = sb("rb_sb", [NROUTED, 1], F32)

        pb = [es_top.enter_context(nc.psum_tensor(f"pb{i}", [128, 512], F32)) for i in range(7)]
        pbT = es_top.enter_context(nc.psum_tensor("pbT", [128, 1024], BF16))

        et, ev, esc = ctx.sem("et"), ctx.sem("ev"), ctx.sem("esc")
        dsy = ctx.sem("dsy")
        dgp = ctx.sem("dgp")
        dw = [ctx.sem("dw0"), ctx.sem("dw1")]
        dwx = [ctx.sem("dwa"), ctx.sem("dwb")]
        dx = [ctx.sem("dx0"), ctx.sem("dx1")]
        dmr = ctx.sem("dmr")
        dvp = ctx.sem("dvp")
        dvs = ctx.sem("dvs")
        dco = ctx.sem("dco")
        dxa = ctx.sem("dxa")
        dcc = ctx.sem("dcc")
        dol = [ctx.sem("dol0"), ctx.sem("dol1")]
        dst = ctx.sem("dst")

        SY.dma_start(out=x_res[:, :].rearrange("p (c t) -> p c t", c=DC),
                     in_=xT.rearrange("(c p) t -> p c t", p=128)).then_inc(dsy.h, 16)
        dsy.n += 16
        for n, _ in consts:
            SY.dma_start(out=csb[n][:, :], in_=c_handles[n].ap()).then_inc(dsy.h, 16)
            dsy.n += 16
        SY.dma_start(out=rb_sb[:, :], in_=rbc).then_inc(dsy.h, 16)
        dsy.n += 16
        GP.dma_start(out=rw_sb[:, :].rearrange("p (c e) -> p c e", c=DC),
                     in_=rw.rearrange("(c p) e -> p c e", p=128)).then_inc(dgp.h, 16)
        dgp.n += 16
        base_sy = dsy.n
        base_gp = dgp.n

        def emit_ffn(tag):
            with contextlib.ExitStack() as es:
                fsb = lambda name, shape, dt: es.enter_context(
                    nc.sbuf_tensor(f"{tag}_{name}", shape, dt))
                xbf8 = fsb("xbf8", [128, DC * TT], BF16)
                sq = fsb("sq", [128, TT], BF16)
                xln = xbf8
                mbc = fsb("mbc", [128, TT], F32)
                ex2 = fsb("ex2", [128, TT], F32)
                rbcst = fsb("rbcst", [128, TT], F32)
                lnt = fsb("lnt", [128, TT], F32)
                l_sb = fsb("l_sb", [NROUTED, TT], BF16)
                topf = fsb("topf", [128, 48], F32)
                ge_sb = fsb("ge_sb", [128, NROUTED], F32)
                gbf = fsb("gbf", [128, NROUTED], BF16)
                g_sb = fsb("g_sb", [NROUTED, TT], BF16)
                silu_sb = fsb("silu_sb", [128, 2 * TT], BF16)
                hh = fsb("hh", [128, NEXP * 4 * TT], BF16)
                w13 = [fsb("w13_0", [128, 2 * DC * HFF], BF16),
                       fsb("w13_1", [128, 2 * DC * HFF], BF16)]
                w2b = [fsb("w2b_0", [128, NEXP * 4 * 128], BF16),
                       fsb("w2b_1", [128, NEXP * 4 * 128], BF16)]

                w13_expect = [0, 0]

                def issue_w13(e, slot, wait_et):
                    ctx.W("gpsimd", et, wait_et)
                    srcs = (w1[e], w3[e]) if e < NROUTED else (s1[e - NROUTED], s3[e - NROUTED])
                    for j, src in enumerate(srcs):
                        GP.dma_start(
                            out=w13[slot][:, j * DC * HFF:(j + 1) * DC * HFF]
                                .rearrange("p (c h) -> p c h", c=DC),
                            in_=src.rearrange("(c p) h -> p c h", p=128),
                        ).then_inc(dw[slot].h, 16)
                        dw[slot].n += 16
                    w13_expect[slot] = dw[slot].n

                w2_expect = [0, 0]

                def issue_w2(dcm, slot, wait_et):
                    ctx.W("gpsimd", et, wait_et)
                    GP.dma_start(
                        out=w2b[slot][:, 0:NROUTED * 4 * 128]
                            .rearrange("p (e hc x) -> p e hc x", e=NROUTED, hc=4),
                        in_=w2[:, :, dcm * 128:(dcm + 1) * 128]
                            .rearrange("e (hc p) x -> p e hc x", p=128),
                    ).then_inc(dwx[slot].h, 16)
                    dwx[slot].n += 16
                    GP.dma_start(
                        out=w2b[slot][:, NROUTED * 4 * 128:NEXP * 4 * 128]
                            .rearrange("p (e hc x) -> p e hc x", e=2, hc=4),
                        in_=s2[:, :, dcm * 128:(dcm + 1) * 128]
                            .rearrange("e (hc p) x -> p e hc x", p=128),
                    ).then_inc(dwx[slot].h, 16)
                    dwx[slot].n += 16
                    w2_expect[slot] = dwx[slot].n

                issue_w13(0, 0, 0)
                issue_w13(1, 1, 0)
                w13_loaded = {0: w13_expect[0], 1: w13_expect[1]}

                prev_stage2_mm = 0
                for tile in range(NTILES):
                    t0 = tile * TT

                    ctx.W("vector", dsy, base_sy)
                    ctx.W("vector", et, prev_stage2_mm)
                    for dcm in range(DC):
                        xs = x_res[:, dcm * TOWN + t0: dcm * TOWN + t0 + TT]
                        ins = V.tensor_copy(xbf8[:, dcm * TT:(dcm + 1) * TT], xs)
                    cast_ms = ctx.inc(ins, ev)
                    ctx.W("tensor", ev, cast_ms)
                    for dcm in range(DC):
                        ins = T.matmul(pb[4][:, 0:TT], ones[:, :], xbf8[:, dcm * TT:(dcm + 1) * TT],
                                       start=(dcm == 0), stop=(dcm == DC - 1))
                    ctx.inc(ins, et)
                    sq_read_ms = None
                    for dcm in range(DC):
                        if sq_read_ms is not None:
                            ctx.W("vector", et, sq_read_ms)
                        sqv = ctx.inc(
                            V.tensor_mul(sq[:, :], x_res[:, dcm * TOWN + t0:dcm * TOWN + t0 + TT],
                                         x_res[:, dcm * TOWN + t0:dcm * TOWN + t0 + TT]), ev)
                        ctx.W("tensor", ev, sqv)
                        sq_read_ms = ctx.inc(
                            T.matmul(pb[5][:, 0:TT], ones[:, :], sq[:, :],
                                     start=(dcm == 0), stop=(dcm == DC - 1)), et)
                    sums_ms = sq_read_ms
                    ctx.W("vector", et, sums_ms)
                    V.tensor_scalar(mbc[:, :], pb[4][:, 0:TT], 1.0 / D, None, ALU.mult)
                    mb_ms = ctx.inc(V.tensor_scalar(ex2[:, :], pb[5][:, 0:TT], 1.0 / D, None,
                                                    ALU.mult), ev)
                    ctx.W("vector", ev, mb_ms)
                    v1 = ctx.inc(V.tensor_mul(lnt[:, :], mbc[:, :], mbc[:, :]), ev)
                    ctx.W("vector", ev, v1)
                    v2 = ctx.inc(V.scalar_tensor_tensor(lnt[:, :], ex2[:, :], 1e-5,
                                                        lnt[:, :], ALU.add, ALU.subtract), ev)
                    ctx.W("scalar", ev, v2)
                    sc1 = ctx.inc(SC.activation(lnt[:, :], lnt[:, :], ACT.Ln), esc)
                    ctx.W("scalar", esc, sc1)
                    sc2 = ctx.inc(SC.activation(rbcst[:, :], lnt[:, :], ACT.Exp, scale=-0.5), esc)
                    for dcm in range(DC):
                        xs = x_res[:, dcm * TOWN + t0: dcm * TOWN + t0 + TT]
                        ins = V.tensor_sub(xbf8[:, dcm * TT:(dcm + 1) * TT], xs, mbc[:, :])
                    sub_ms = ctx.inc(ins, ev)
                    ctx.W("vector", ev, sub_ms)
                    ctx.W("vector", esc, sc2)
                    for dcm in range(DC):
                        ins = V.tensor_mul(xln[:, dcm * TT:(dcm + 1) * TT],
                                           xbf8[:, dcm * TT:(dcm + 1) * TT], rbcst[:, :])
                    xln_ms = ctx.inc(ins, ev)

                    ctx.W("tensor", ev, xln_ms)
                    ctx.W("tensor", dgp, base_gp)
                    for dcm in range(DC):
                        ins = T.matmul(pb[6][0:NROUTED, 0:TT],
                                       rw_sb[:, dcm * NROUTED:(dcm + 1) * NROUTED],
                                       xln[:, dcm * TT:(dcm + 1) * TT],
                                       start=(dcm == 0), stop=(dcm == DC - 1))
                    rt_ms = ctx.inc(ins, et)
                    ctx.W("vector", et, rt_ms)
                    lg_ms = ctx.inc(V.tensor_scalar(l_sb[:, :], pb[6][0:NROUTED, 0:TT],
                                                    rb_sb[:, :], None, ALU.add), ev)
                    ctx.W("tensor", ev, lg_ms)
                    for g in range(4):
                        ins = T.transpose(pbT[:, g * 8:(g + 1) * 8],
                                          l_sb[:, g * 128:(g + 1) * 128], ident[0:8, 0:8])
                    tr_ms = ctx.inc(ins, et)
                    prev_gt = tr_ms
                    for g in range(4):
                        lt = pbT[:, g * 8:(g + 1) * 8]
                        ctx.W("vector", et, prev_gt)
                        a1 = ctx.inc(V.reduce_max(topf[:, 0:1], lt, axis=AX, negate=True), ev)
                        ctx.W("vector", ev, a1)
                        a2 = ctx.inc(V.tensor_scalar(topf[:, 8:16], lt, topf[:, 0:1], 0.0,
                                                     ALU.add, ALU.is_lt), ev)
                        ctx.W("vector", ev, a2)
                        a2b = ctx.inc(V.tensor_scalar(topf[:, 40:48], topf[:, 8:16], 1.0, 1.0e30,
                                                      ALU.subtract, ALU.mult), ev)
                        a3a = ctx.inc(V.tensor_mul(topf[:, 16:24], lt, topf[:, 8:16]), ev)
                        ctx.W("vector", ev, a3a)
                        a3 = ctx.inc(V.tensor_add(topf[:, 16:24], topf[:, 16:24], topf[:, 40:48]), ev)
                        ctx.W("vector", ev, a3)
                        a4 = ctx.inc(V.reduce_max(topf[:, 1:2], topf[:, 16:24], axis=AX), ev)
                        ctx.W("scalar", ev, a1)
                        ctx.W("scalar", et, tr_ms)
                        e1 = ctx.inc(SC.activation(topf[:, 24:32], lt, ACT.Exp, bias=topf[:, 0:1]),
                                     esc)
                        ctx.W("vector", ev, a4)
                        a5 = ctx.inc(V.tensor_single_scalar(topf[:, 32:40], lt, topf[:, 1:2],
                                                            ALU.is_ge), ev)
                        ctx.W("vector", esc, e1)
                        ctx.W("vector", ev, a5)
                        a6 = ctx.inc(V.tensor_mul(ge_sb[:, :], topf[:, 24:32], topf[:, 32:40]), ev)
                        ctx.W("vector", ev, a6)
                        a7 = ctx.inc(V.reduce_sum(topf[:, 2:3], ge_sb[:, :], axis=AX), ev)
                        ctx.W("vector", ev, a7)
                        a8 = ctx.inc(V.reciprocal(topf[:, 3:4], topf[:, 2:3]), ev)
                        ctx.W("vector", ev, a8)
                        a9 = ctx.inc(V.tensor_scalar(gbf[:, :], ge_sb[:, :], topf[:, 3:4], None,
                                                     ALU.mult), ev)
                        ctx.W("tensor", ev, a9)
                        gt = ctx.inc(T.transpose(pbT[0:NROUTED, 512 + g * 128:512 + (g + 1) * 128],
                                                 gbf[:, :], ident[:, :]), et)
                        prev_gt = gt
                        ctx.W("vector", et, gt)
                        ctx.inc(V.tensor_copy(g_sb[:, g * 128:(g + 1) * 128],
                                              pbT[0:NROUTED, 512 + g * 128:512 + (g + 1) * 128]),
                                ev)
                    gates_ms = ev.n

                    evac_done = {0: 0, 1: 0}
                    silu_done = [0, 0]
                    gb_read = 0
                    next_load_e = 2
                    for e in range(NEXP):
                        slot = e % 2
                        ctx.W("tensor", dw[slot], w13_loaded[slot])
                        ctx.W("tensor", ev, xln_ms)
                        if e < NROUTED:
                            ctx.W("tensor", ev, gates_ms)
                            ctx.W("tensor", ev, gb_read)
                            gbc = ctx.inc(T.matmul(pb[6][:, 0:TT],
                                                   oneh[:, e * 128:(e + 1) * 128],
                                                   g_sb[:, :], start=True, stop=True), et)
                        for hc in range(4):
                            par = hc % 2
                            bh1, bh3 = pb[0 + 2 * par], pb[1 + 2 * par]
                            ctx.W("tensor", ev, evac_done[par])
                            for dcm in range(DC):
                                T.matmul(bh1[:, 0:TT],
                                         w13[slot][:, dcm * HFF + hc * 128:
                                                   dcm * HFF + hc * 128 + 128],
                                         xln[:, dcm * TT:(dcm + 1) * TT],
                                         start=(dcm == 0), stop=(dcm == DC - 1))
                            for dcm in range(DC):
                                ins = T.matmul(bh3[:, 0:TT],
                                               w13[slot][:, DC * HFF + dcm * HFF + hc * 128:
                                                         DC * HFF + dcm * HFF + hc * 128 + 128],
                                               xln[:, dcm * TT:(dcm + 1) * TT],
                                               start=(dcm == 0), stop=(dcm == DC - 1))
                            mm_ms = ctx.inc(ins, et)
                            sl = silu_sb[:, par * TT:(par + 1) * TT]
                            ctx.W("scalar", et, mm_ms)
                            ctx.W("scalar", ev, silu_done[par])
                            sl_ms = ctx.inc(SC.activation(sl, bh1[:, 0:TT], ACT.Silu), esc)
                            hho = hh[:, (e * 4 + hc) * TT:(e * 4 + hc + 1) * TT]
                            ctx.W("vector", esc, sl_ms)
                            if e < NROUTED:
                                tv = ctx.inc(V.tensor_mul(sl, sl, bh3[:, 0:TT]), ev)
                                ctx.W("vector", ev, tv)
                                ctx.W("vector", et, gbc)
                                silu_done[par] = ctx.inc(
                                    V.tensor_mul(hho, sl, pb[6][:, 0:TT]), ev)
                                gb_read = silu_done[par]
                            else:
                                silu_done[par] = ctx.inc(V.tensor_mul(hho, sl, bh3[:, 0:TT]), ev)
                            evac_done[par] = silu_done[par]
                        if next_load_e < NEXP:
                            issue_w13(next_load_e, slot, et.n)
                            w13_loaded[slot] = w13_expect[slot]
                            next_load_e += 1
                    hh_ms = ev.n
                    if tile + 1 < NTILES:
                        issue_w13(0, 0, et.n)
                        w13_loaded[0] = w13_expect[0]
                        issue_w13(1, 1, et.n)
                        w13_loaded[1] = w13_expect[1]

                    issue_w2(0, 0, et.n)
                    issue_w2(1, 1, et.n)
                    w2_loaded = {0: w2_expect[0], 1: w2_expect[1]}
                    out_evac = [0, 0]
                    out_evac_v = [0, 0]
                    for dcm in range(DC):
                        slot = dcm % 2
                        bank = pb[4 + slot]
                        ctx.W("tensor", dwx[slot], w2_loaded[slot])
                        ctx.W("tensor", ev, hh_ms)
                        ctx.W("tensor", ev, out_evac_v[slot])
                        for k in range(NEXP * 4):
                            ins = T.matmul(bank[:, 0:TT], w2b[slot][:, k * 128:(k + 1) * 128],
                                           hh[:, k * TT:(k + 1) * TT],
                                           start=(k == 0), stop=(k == NEXP * 4 - 1))
                        o_ms = ctx.inc(ins, et)
                        prev_stage2_mm = o_ms
                        ctx.W("vector", et, o_ms)
                        upd = ctx.inc(V.scalar_tensor_tensor(
                            x_res[:, dcm * TOWN + t0: dcm * TOWN + t0 + TT],
                            bank[:, 0:TT], 0.5,
                            x_res[:, dcm * TOWN + t0: dcm * TOWN + t0 + TT],
                            ALU.mult, ALU.add), ev)
                        out_evac_v[slot] = upd
                        if dcm + 2 < DC:
                            issue_w2(dcm + 2, slot, et.n)
                            w2_loaded[slot] = w2_expect[slot]
                if DEBUG and tag == "f1":
                    ctx.barrier()
                    for nm, t_sb in (("dbg_xln", xln), ("dbg_l", l_sb),
                                     ("dbg_g", g_sb), ("dbg_hh", hh)):
                        SY.dma_start(out=dbg_t[nm].ap(), in_=t_sb[:, :]).then_inc(dsy.h, 16)
                        dsy.n += 16
                    ctx.W("sync", dsy, dsy.n)
                ctx.barrier()

        def emit_ln2_and_ag1():
            with contextlib.ExitStack() as es:
                fsb = lambda name, shape, dt: es.enter_context(
                    nc.sbuf_tensor(f"ln2_{name}", shape, dt))
                xbf8 = fsb("xbf8", [128, DC * TT], BF16)
                sq = fsb("sq", [128, TT], BF16)
                xln = fsb("xln", [128, DC * TT], BF16)
                mbc = fsb("mbc", [128, TT], F32)
                ex2 = fsb("ex2", [128, TT], F32)
                rbcst = fsb("rbcst", [128, TT], F32)
                lnt = fsb("lnt", [128, TT], F32)
                for tile in range(NTILES):
                    t0 = tile * TT
                    ctx.W("vector", dsy, base_sy)
                    ctx.W("vector", dxa, dxa.n)
                    for dcm in range(DC):
                        xs = x_res[:, dcm * TOWN + t0: dcm * TOWN + t0 + TT]
                        ins = V.tensor_copy(xbf8[:, dcm * TT:(dcm + 1) * TT], xs)
                    cast_ms = ctx.inc(ins, ev)
                    ctx.W("tensor", ev, cast_ms)
                    for dcm in range(DC):
                        ins = T.matmul(pb[4][:, 0:TT], ones[:, :], xbf8[:, dcm * TT:(dcm + 1) * TT],
                                       start=(dcm == 0), stop=(dcm == DC - 1))
                    ctx.inc(ins, et)
                    sq_read_ms = None
                    for dcm in range(DC):
                        if sq_read_ms is not None:
                            ctx.W("vector", et, sq_read_ms)
                        sqv = ctx.inc(
                            V.tensor_mul(sq[:, :], x_res[:, dcm * TOWN + t0:dcm * TOWN + t0 + TT],
                                         x_res[:, dcm * TOWN + t0:dcm * TOWN + t0 + TT]), ev)
                        ctx.W("tensor", ev, sqv)
                        sq_read_ms = ctx.inc(
                            T.matmul(pb[5][:, 0:TT], ones[:, :], sq[:, :],
                                     start=(dcm == 0), stop=(dcm == DC - 1)), et)
                    sums_ms = sq_read_ms
                    ctx.W("vector", et, sums_ms)
                    V.tensor_scalar(mbc[:, :], pb[4][:, 0:TT], 1.0 / D, None, ALU.mult)
                    mb_ms = ctx.inc(V.tensor_scalar(ex2[:, :], pb[5][:, 0:TT], 1.0 / D, None,
                                                    ALU.mult), ev)
                    ctx.W("vector", ev, mb_ms)
                    v1 = ctx.inc(V.tensor_mul(lnt[:, :], mbc[:, :], mbc[:, :]), ev)
                    ctx.W("vector", ev, v1)
                    v2 = ctx.inc(V.scalar_tensor_tensor(lnt[:, :], ex2[:, :], 1e-5,
                                                        lnt[:, :], ALU.add, ALU.subtract), ev)
                    ctx.W("scalar", ev, v2)
                    scA = ctx.inc(SC.activation(lnt[:, :], lnt[:, :], ACT.Ln), esc)
                    ctx.W("scalar", esc, scA)
                    scB = ctx.inc(SC.activation(rbcst[:, :], lnt[:, :], ACT.Exp, scale=-0.5), esc)
                    for dcm in range(DC):
                        xs = x_res[:, dcm * TOWN + t0: dcm * TOWN + t0 + TT]
                        ins = V.tensor_sub(xbf8[:, dcm * TT:(dcm + 1) * TT], xs, mbc[:, :])
                    sub_ms = ctx.inc(ins, ev)
                    ctx.W("vector", ev, sub_ms)
                    ctx.W("vector", esc, scB)
                    for dcm in range(DC):
                        ins = V.tensor_mul(xln[:, dcm * TT:(dcm + 1) * TT],
                                           xbf8[:, dcm * TT:(dcm + 1) * TT], rbcst[:, :])
                    xln_ms = ctx.inc(ins, ev)
                    ctx.W("sync", ev, xln_ms)
                    SY.dma_start(
                        out=cc_xa_in.ap().rearrange("(c p) t -> p c t", p=128)[:, :, t0:t0 + TT],
                        in_=xln[:, :].rearrange("p (c t) -> p c t", c=DC)).then_inc(dxa.h, 16)
                    dxa.n += 16
                ctx.W("gpsimd", dxa, dxa.n)
                GP.collective_compute(
                    "AllGather", ALU.bypass, replica_groups=[list(range(N_CORES))],
                    ins=[cc_xa_in.ap().opt()], outs=[cc_xa_out.ap().opt()],
                ).then_inc(dcc.h, 1)
                dcc.n += 1
                ctx.barrier(skip=("dcc",))

        def emit_attention():
            with contextlib.ExitStack() as es:
                fsb = lambda name, shape, dt: es.enter_context(
                    nc.sbuf_tensor(f"at_{name}", shape, dt))
                wq_sb = fsb("wq", [128, 128], BF16)
                wk_sb = fsb("wk", [128, 128], BF16)
                wv_sb = fsb("wv", [128, 128], BF16)
                wq8 = fsb("wq8", [128, DC * 128], BF16)
                wk8 = fsb("wk8", [128, DC * 128], BF16)
                wv8 = fsb("wv8", [128, DC * 128], BF16)
                xa = [fsb("xa0", [128, DC * 512], BF16), fsb("xa1", [128, DC * 512], BF16)]
                qt_sb = fsb("qt", [128, S], BF16)
                kt_sb = fsb("kt", [128, S], BF16)
                v_tok = fsb("v_tok", [128, 32 * 144], BF16)
                vp_tok = fsb("vp_tok", [64, 64 * 72], BF16)
                vsum = fsb("vsum", [64, 72], BF16)
                ksum = fsb("ksum", [128, 64], BF16)
                kp_sb = fsb("kp", [64, S], BF16)
                qp_sb = fsb("qp", [64, S], BF16)
                o_acc = fsb("o_acc", [65, S], BF16)
                o_stage = fsb("o_stage", [64, S], BF16)
                zbuf = fsb("zbuf", [66, S], F32)
                m_row = fsb("m_row", [1, S], BF16)
                mp_row = fsb("mp_row", [1, S], BF16)
                mcol = fsb("mcol", [128, NQT], F32)
                mcolb = fsb("mcolb", [128, NQT], BF16)
                mst = fsb("mst", [NQT, 128], BF16)
                s_sc = fsb("s_sc", [128, 256], F32)
                pexp = fsb("pexp", [128, 256], BF16)
                pt_sb = fsb("pt", [128, 256], BF16)
                esb = fsb("esb", [128, 512], BF16)
                pts = fsb("pts", [128, 512], BF16)
                psum_ex = fsb("psum_ex", [64, 128], BF16)

                for w_dram, w_s in ((wqT, wq8), (wkT, wk8), (wvT, wv8)):
                    GP.dma_start(out=w_s[:, :].rearrange("p (c f) -> p c f", c=DC),
                                 in_=w_dram.rearrange("(c p) f -> p c f", p=128)).then_inc(dgp.h, 16)
                    dgp.n += 16
                wload = dgp.n
                ones_ms = ctx.inc(V.memset(
                    v_tok[:, :].rearrange("p (s w) -> p s w", w=144)[:, :, 64:65], 1.0), ev)
                ones_ms = ctx.inc(V.memset(
                    v_tok[:, :].rearrange("p (s w) -> p s w", w=144)[:, :, 136:137], 1.0), ev)

                for b in range(B):
                    for tg in range(8):
                        slot = tg % 2
                        shard = 2 * b + (tg // 4)
                        col0 = (tg % 4) * 512
                        ctx.W("sync", dcc, 1)
                        ctx.W("sync", et, ctx.marks.get(("xaw", slot), 0))
                        SY.dma_start(
                            out=xa[slot][:, :].rearrange("p (c t) -> p c t", c=DC),
                            in_=cc_xa_out.ap()[shard]
                                .rearrange("(c p) t -> p c t", p=128)[:, :, col0:col0 + 512],
                        ).then_inc(dx[slot].h, 16)
                        dx[slot].n += 16
                        ctx.W("tensor", dx[slot], dx[slot].n)
                        ctx.W("tensor", dgp, wload)
                        ctx.W("tensor", ev, ctx.marks.get(("qkv_evac", tg % 2), 0))
                        qbQ, qbK, qbV = pb[3 * (tg % 2)], pb[3 * (tg % 2) + 1], pb[3 * (tg % 2) + 2]
                        for dcm in range(DC):
                            T.matmul(qbQ[:, 0:512], wq8[:, dcm * 128:(dcm + 1) * 128],
                                     xa[slot][:, dcm * 512:(dcm + 1) * 512],
                                     start=(dcm == 0), stop=(dcm == DC - 1))
                        for dcm in range(DC):
                            T.matmul(qbK[:, 0:512], wk8[:, dcm * 128:(dcm + 1) * 128],
                                     xa[slot][:, dcm * 512:(dcm + 1) * 512],
                                     start=(dcm == 0), stop=(dcm == DC - 1))
                        for sub in range(4):
                            for dcm in range(DC):
                                ins = T.matmul(qbV[:, sub * 128:(sub + 1) * 128],
                                               xa[slot][:, dcm * 512 + sub * 128:
                                                        dcm * 512 + sub * 128 + 128],
                                               wv8[:, dcm * 128:(dcm + 1) * 128],
                                               start=(dcm == 0), stop=(dcm == DC - 1))
                        mm_ms = ctx.inc(ins, et)
                        ctx.marks[("xaw", slot)] = mm_ms
                        tcol = tg * 512
                        ctx.W("vector", et, mm_ms)
                        ctx.W("vector", ev, ones_ms)
                        V.tensor_scalar(qt_sb[:, tcol:tcol + 512], qbQ[:, 0:512], 0.125,
                                        None, ALU.mult)
                        V.tensor_copy(kt_sb[:, tcol:tcol + 512], qbK[:, 0:512])
                        for sub in range(4):
                            kt_i = tg * 4 + sub
                            V.tensor_copy(v_tok[:, kt_i * 144:kt_i * 144 + 64],
                                          qbV[0:128, sub * 128:sub * 128 + 64])
                            ins = V.tensor_copy(v_tok[:, kt_i * 144 + 72:kt_i * 144 + 136],
                                                qbV[0:128, sub * 128 + 64:sub * 128 + 128])
                        ev_ms = ctx.inc(ins, ev)
                        ctx.marks[("qkv_evac", tg % 2)] = ev_ms
                    qkv_ms = ev.n

                    for h in range(2):
                        p0 = 0 if h == 0 else 64
                        tp = None if h == 0 else (64, 0)
                        madd = (mA, mA0) if h == 0 else (mB, mB0)
                        mmulT = (mAT, mA0T) if h == 0 else (mBT, mB0T)

                        if not ATT["passa"]:
                            continue
                        ctx.W("tensor", ev, qkv_ms)
                        prev_rmax = 0
                        for qt in range(NQT):
                            kc0 = 0 if qt == 0 else (qt - 1) * 128
                            bank = pb[0 + (qt % 2)]
                            ctx.W("tensor", ev, ctx.marks.get(("paev", qt % 2), 0))
                            pa_ms = ctx.inc(
                                T.matmul(bank[:, 0:256],
                                         qt_sb[p0:p0 + 64, qt * 128:(qt + 1) * 128],
                                         kt_sb[p0:p0 + 64, kc0:kc0 + 256],
                                         start=True, stop=True, tile_position=tp), et)
                            ctx.W("vector", et, pa_ms)
                            ctx.W("vector", ev, prev_rmax)
                            mk = madd[1] if qt == 0 else madd[0]
                            vv = ctx.inc(V.tensor_add(s_sc[:, :], bank[:, 0:256], mk[:, :]), ev)
                            ctx.W("vector", ev, vv)
                            prev_rmax = ctx.inc(V.reduce_max(mcol[:, qt:qt + 1], s_sc[:, :],
                                                             axis=AX, negate=True), ev)
                            ctx.marks[("paev", qt % 2)] = prev_rmax
                        ctx.W("vector", ev, ev.n)
                        mc_ms = ctx.inc(V.tensor_copy(mcolb[:, :], mcol[:, :]), ev)
                        ctx.W("tensor", ev, mc_ms)
                        mr_ms = ctx.inc(T.matmul(pb[6][0:NQT, 0:128], mcolb[:, :], ident[:, :],
                                                 start=True, stop=True), et)
                        ctx.W("vector", et, mr_ms)
                        ms_ms = ctx.inc(V.tensor_copy(mst[:, :], pb[6][0:NQT, 0:128]), ev)
                        ctx.W("sync", ev, ms_ms)
                        ctx.W("sync", et, et.n)
                        SY.dma_start(out=m_row[0:1, :], in_=mst[:, :]).then_inc(dmr.h, 16)
                        dmr.n += 16

                        if h == 0 and ATT["strided"] >= 1:
                            ctx.W("vector", ev, qkv_ms)
                            ctx.W("vector", et, et.n)
                            kpv = ctx.inc(V.tensor_copy(
                                kp_sb[:, :].rearrange("p (r bj) -> p r bj", r=64),
                                kt_sb[0:64, :].rearrange("p (bj r) -> p r bj", bj=64)), ev)
                            qpv = ctx.inc(V.tensor_copy(
                                qp_sb[:, :].rearrange("p (r bj) -> p r bj", r=64),
                                qt_sb[0:64, :].rearrange("p (bj r) -> p r bj", bj=64)), ev)
                            ctx.W("vector", dmr, dmr.n)
                            mpv = ctx.inc(V.tensor_copy(
                                mp_row[:, :].rearrange("o (r bj) -> o r bj", r=64),
                                m_row[:, :].rearrange("o (bj r) -> o r bj", bj=64)), ev)
                            ctx.W("sync", ev, qkv_ms)
                            ctx.W("sync", et, et.n)
                            ctx.W("sync", dvp, dvp.n)
                            SY.dma_start(
                                out=vperm_dr.ap().rearrange("(s p) f -> p s f", p=128)[:, :, 0:65],
                                in_=v_tok[:, :].rearrange("p (s w) -> p s w", w=144)[:, :, 0:65],
                            ).then_inc(dvp.h, 16)
                            SY.dma_start(
                                out=vperm_dr.ap().rearrange("(s p) f -> p s f", p=128)[:, :, 65:130],
                                in_=v_tok[:, :].rearrange("p (s w) -> p s w", w=144)[:, :, 72:137],
                            ).then_inc(dvp.h, 16)
                            dvp.n += 32
                            ctx.W("sync", dvp, dvp.n)
                            SY.dma_start(
                                out=vp_tok[:, :].rearrange("p (s w) -> p s w", w=72)[:, :, 0:65],
                                in_=vperm_dr.ap()
                                    .rearrange("(bj r) f -> bj r f", r=64)[:, :, 0:65],
                            ).then_inc(dvp.h, 16)
                            dvp.n += 16
                        elif h == 1 and ATT["summary"]:
                            ctx.W("vector", ev, qkv_ms)
                            ctx.W("vector", et, et.n)
                            ksv = ctx.inc(V.tensor_copy(
                                ksum[64:128, :],
                                kt_sb[64:128, :].rearrange("p (c r) -> p c r", r=64)[:, :, 63]), ev)
                            ctx.W("sync", ev, qkv_ms)
                            ctx.W("sync", et, et.n)
                            ctx.W("sync", dvp, dvp.n)
                            SY.dma_start(
                                out=vsum[:, 0:65],
                                in_=vperm_dr.ap()
                                    .rearrange("(c u) f -> c u f", u=64)[:, 63, 65:130],
                            ).then_inc(dvs.h, 16)
                            dvs.n += 16

                        if not ATT["passb"]:
                            continue
                        ctx.W("tensor", dmr, dmr.n)
                        if h == 1:
                            ctx.W("tensor", dvs, dvs.n)
                            ctx.W("tensor", ev, ev.n)
                        for qt in range(NQT):
                            kc0 = 0 if qt == 0 else (qt - 1) * 128
                            bank = pb[2 + (qt % 2)]
                            bo = pb[4 + (qt % 2)]
                            ctx.W("tensor", esc, ctx.marks.get(("pbexp", qt % 2), 0))
                            for half in range(2):
                                T.matmul(bank[:, half * 128:(half + 1) * 128],
                                         kt_sb[p0:p0 + 64, kc0 + half * 128:kc0 + half * 128 + 128],
                                         qt_sb[p0:p0 + 64, qt * 128:(qt + 1) * 128],
                                         start=True, stop=False, tile_position=tp)
                                ins = T.matmul(bank[:, half * 128:(half + 1) * 128],
                                               ones[0:1, 0:128],
                                               m_row[0:1, qt * 128:(qt + 1) * 128],
                                               start=False, stop=True)
                            if h == 1 and ATT["summary"]:
                                ctx.W("tensor", esc, ctx.marks.get(("sumexS",), 0))
                                T.matmul(pb[6][0:64, 0:128], ksum[64:128, :],
                                         qt_sb[64:128, qt * 128:(qt + 1) * 128],
                                         start=True, stop=False, tile_position=(64, 0))
                                ins = T.matmul(pb[6][0:64, 0:128], ones[0:1, 0:64],
                                               m_row[0:1, qt * 128:(qt + 1) * 128],
                                               start=False, stop=True)
                            pbm = ctx.inc(ins, et)
                            ctx.W("scalar", et, pbm)
                            ctx.W("scalar", ev, ctx.marks.get(("pbpt", qt % 2), 0))
                            ex1 = ctx.inc(SC.activation(pexp[:, :], bank[:, 0:256], ACT.Exp), esc)
                            ctx.marks[("pbexp", qt % 2)] = ex1
                            ctx.W("vector", esc, ex1)
                            mkT = mmulT[1] if qt == 0 else mmulT[0]
                            pt1 = ctx.inc(V.tensor_mul(pt_sb[:, :], pexp[:, :], mkT[:, :]), ev)
                            ctx.marks[("pbpt", qt % 2)] = pt1
                            if h == 1 and ATT["summary"]:
                                ctx.W("scalar", ev, ctx.marks.get(("sumex",), 0))
                                sx = ctx.inc(SC.activation(psum_ex[:, :], pb[6][0:64, 0:128],
                                                           ACT.Exp), esc)
                                ctx.marks[("sumexS",)] = sx
                                ctx.W("vector", esc, sx)
                                ctx.marks[("sumex",)] = ctx.inc(V.tensor_mul(
                                    psum_ex[:, :], psum_ex[:, :],
                                    msum[:, qt * 128:(qt + 1) * 128]), ev)
                            ctx.W("tensor", ev, ev.n)
                            ctx.W("tensor", ev, ctx.marks.get(("poev", qt % 2), 0))
                            kt0 = 0 if qt == 0 else qt - 1
                            for half in range(2):
                                ktile = kt0 + half
                                ins = T.matmul(bo[0:65, 0:128],
                                               v_tok[:, ktile * 144 + h * 72:
                                                     ktile * 144 + h * 72 + 65],
                                               pt_sb[:, half * 128:(half + 1) * 128],
                                               start=(half == 0),
                                               stop=(half == 1 and (h == 0 or not ATT["summary"])))
                            if h == 1 and ATT["summary"]:
                                ins = T.matmul(bo[0:65, 0:128], vsum[:, 0:65], psum_ex[:, :],
                                               start=False, stop=True)
                            pv_ms = ctx.inc(ins, et)
                            ctx.W("vector", et, pv_ms)
                            ctx.marks[("poev", qt % 2)] = ctx.inc(
                                V.tensor_copy(o_acc[:, qt * 128:(qt + 1) * 128],
                                              bo[0:65, 0:128]), ev)

                        if h == 0 and ATT["strided"] >= 2:
                            ctx.W("tensor", ev, ev.n)
                            ctx.W("tensor", esc, esc.n)
                            ctx.W("tensor", dvp, dvp.n)
                            for rg in range(8):
                                bs = pb[2 + (rg % 2)]
                                bos = pb[4 + (rg % 2)]
                                ctx.W("tensor", esc, ctx.marks.get(("strex", rg % 2), 0))
                                ctx.W("tensor", ev, ctx.marks.get(("stradd", rg % 2), 0))
                                for rr in range(8):
                                    r = rg * 8 + rr
                                    T.matmul(bs[0:64, rr * 64:(rr + 1) * 64],
                                             kp_sb[:, r * 64:(r + 1) * 64],
                                             qp_sb[:, r * 64:(r + 1) * 64],
                                             start=True, stop=False)
                                    ins = T.matmul(bs[0:64, rr * 64:(rr + 1) * 64],
                                                   ones[0:1, 0:64],
                                                   mp_row[0:1, r * 64:(r + 1) * 64],
                                                   start=False, stop=True)
                                sm = ctx.inc(ins, et)
                                ctx.W("scalar", et, sm)
                                ctx.W("scalar", ev, ctx.marks.get(("strpt", rg % 2), 0))
                                se = ctx.inc(SC.activation(esb[0:64, :], bs[0:64, 0:512],
                                                           ACT.Exp), esc)
                                ctx.marks[("strex", rg % 2)] = se
                                ctx.W("vector", esc, se)
                                sp = ctx.inc(V.tensor_mul(pts[0:64, :], esb[0:64, :],
                                                          mstr[0:64, :]), ev)
                                ctx.marks[("strpt", rg % 2)] = sp
                                ctx.W("tensor", ev, sp)
                                if ATT["strided"] < 3:
                                    ctx.marks[("stradd", rg % 2)] = sp
                                    continue
                                for rr in range(8):
                                    r = rg * 8 + rr
                                    ins = T.matmul(bos[0:65, rr * 64:(rr + 1) * 64],
                                                   vp_tok[:, r * 72:r * 72 + 65],
                                                   pts[0:64, rr * 64:(rr + 1) * 64],
                                                   start=True, stop=True)
                                spv = ctx.inc(ins, et)
                                ctx.W("vector", et, spv)
                                oacc_v = o_acc[:, :].rearrange("p (qi r) -> p r qi",
                                                               r=64)[:, rg * 8:(rg + 1) * 8, :]
                                ctx.marks[("stradd", rg % 2)] = ctx.inc(V.tensor_add(
                                    oacc_v, oacc_v,
                                    bos[0:65, 0:512].rearrange("p (rr qi) -> p rr qi", rr=8)), ev)

                        if not ATT["norm"]:
                            continue
                        ctx.W("vector", ev, ev.n)
                        nz0 = ctx.inc(V.tensor_copy(zbuf[64:65, :], o_acc[64:65, :]), ev)
                        ctx.W("scalar", ev, nz0)
                        z1 = ctx.inc(SC.activation(zbuf[64:65, :], zbuf[64:65, :], ACT.Ln), esc)
                        ctx.W("scalar", esc, z1)
                        z2 = ctx.inc(SC.activation(zbuf[64:65, :], zbuf[64:65, :], ACT.Exp,
                                                   scale=-1.0), esc)
                        ctx.W("tensor", esc, z2)
                        ctx.W("vector", dco, dco.n)
                        for g in range(8):
                            bz = pb[2 + (g % 2)]
                            ctx.W("tensor", ev, ctx.marks.get(("zev", g % 2), 0))
                            zm = ctx.inc(T.matmul(bz[0:64, 0:512], onesf[64:65, 0:64],
                                                  zbuf[64:65, g * 512:(g + 1) * 512],
                                                  start=True, stop=True), et)
                            ctx.W("vector", et, zm)
                            ctx.marks[("zev", g % 2)] = ctx.inc(
                                V.tensor_mul(o_stage[:, g * 512:(g + 1) * 512],
                                             o_acc[0:64, g * 512:(g + 1) * 512],
                                             bz[0:64, 0:512]), ev)
                        ctx.W("sync", ev, ev.n)
                        SY.dma_start(out=cc_o_in.ap()[b, h], in_=o_stage[:, :]).then_inc(dco.h, 16)
                        dco.n += 16
                    if ATT["ag2"]:
                        ctx.W("gpsimd", dco, dco.n)
                        GP.collective_compute(
                            "AllGather", ALU.bypass, replica_groups=[list(range(N_CORES))],
                            ins=[cc_o_in.ap()[b].opt()], outs=[cc_o_out.ap()[b].opt()],
                        ).then_inc(dcc.h, 1)
                        dcc.n += 1
                    ctx.barrier(engines=("tensor", "vector", "scalar"), skip=("dcc",))
                ctx.barrier(skip=("dcc",))

        def emit_wo():
            with contextlib.ExitStack() as es:
                fsb = lambda name, shape, dt: es.enter_context(
                    nc.sbuf_tensor(f"wo_{name}", shape, dt))
                wo_sb = fsb("wo", [128, DC * D], BF16)
                og = [fsb("og0", [128, DC * 512], BF16), fsb("og1", [128, DC * 512], BF16)]
                GP.dma_start(out=wo_sb[:, :].rearrange("p (c f) -> p c f", c=DC),
                             in_=woT.rearrange("(c p) f -> p c f", p=128)).then_inc(dgp.h, 16)
                dgp.n += 16
                wo_ld = dgp.n
                pid = SY.partition_id()
                row0 = SY.compute_val((pid // 2) * (N_CORES * 2 * 64))
                colb = SY.compute_val((pid % 2) * TOWN)
                flat = cc_o_out.ap().rearrange("b n h d t -> (b n h d) t")
                ctx.W("sync", dcc, dcc.n)
                for ts in range(4):
                    slot = ts % 2
                    ctx.W("sync", et, ctx.marks.get(("ogw", slot), 0))
                    col = SY.compute_val(colb + ts * 512)
                    SY.dma_start(
                        out=og[slot][:, :].rearrange("p (c t) -> p c t", c=DC),
                        in_=flat[bass.ds(row0, 1024), bass.ds(col, 512)]
                            .rearrange("(c p) t -> p c t", p=128),
                    ).then_inc(dol[slot].h, 16)
                    dol[slot].n += 16
                    ctx.W("tensor", dol[slot], dol[slot].n)
                    ctx.W("tensor", dgp, wo_ld)
                    t0 = ts * 512
                    for oc in range(DC):
                        bank = pb[4 + (oc % 2)]
                        ctx.W("tensor", ev, ctx.marks.get(("woev", oc % 2), 0))
                        for ic in range(DC):
                            ins = T.matmul(bank[:, 0:512],
                                           wo_sb[:, ic * D + oc * 128:ic * D + oc * 128 + 128],
                                           og[slot][:, ic * 512:(ic + 1) * 512],
                                           start=(ic == 0), stop=(ic == DC - 1))
                        wm = ctx.inc(ins, et)
                        ctx.marks[("ogw", slot)] = wm
                        ctx.W("vector", et, wm)
                        ctx.marks[("woev", oc % 2)] = ctx.inc(V.tensor_add(
                            x_res[:, oc * TOWN + t0: oc * TOWN + t0 + 512],
                            x_res[:, oc * TOWN + t0: oc * TOWN + t0 + 512],
                            bank[:, 0:512]), ev)
                ctx.barrier()

        if PHASES.get("ffn1"):
            emit_ffn("f1")
        if PHASES.get("attn"):
            emit_ln2_and_ag1()
            emit_attention()
            if ATT["wo"]:
                emit_wo()
        if PHASES.get("ffn2"):
            emit_ffn("f2")

        ctx.barrier()
        SY.dma_start(out=yT.rearrange("(c p) t -> p c t", p=128),
                     in_=x_res[:, :].rearrange("p (c t) -> p c t", c=DC)).then_inc(dst.h, 16)
        dst.n += 16
        ctx.W("sync", dst, dst.n)

    return nc


_NC = None


def _in_maps(inputs):
    x = np.asarray(inputs["L0_attn_x"], dtype=np.float32).reshape(B * S, D)
    Wq = np.asarray(inputs["L0_attn_Wq"], dtype=np.float32)
    Wk = np.asarray(inputs["L0_attn_Wk"], dtype=np.float32)
    Wv = np.asarray(inputs["L0_attn_Wv"], dtype=np.float32)
    Wo = np.asarray(inputs["L0_attn_Wo"], dtype=np.float32)
    rw_ = np.ascontiguousarray(np.asarray(inputs["L0_ffn_router_w"], dtype=np.float32))
    rb_ = np.asarray(inputs["L0_ffn_router_b"], dtype=np.float32).reshape(NROUTED, 1)
    w1_ = np.ascontiguousarray(np.asarray(inputs["L0_ffn_routed_w1"], dtype=np.float32))
    w3_ = np.ascontiguousarray(np.asarray(inputs["L0_ffn_routed_w3"], dtype=np.float32))
    w2_ = np.ascontiguousarray(np.asarray(inputs["L0_ffn_routed_w2"], dtype=np.float32))
    s1_ = np.ascontiguousarray(np.asarray(inputs["L0_ffn_shared_w1"], dtype=np.float32))
    s3_ = np.ascontiguousarray(np.asarray(inputs["L0_ffn_shared_w3"], dtype=np.float32))
    s2_ = np.ascontiguousarray(np.asarray(inputs["L0_ffn_shared_w2"], dtype=np.float32))
    WqT, WkT, WvT, WoT = Wq.T, Wk.T, Wv.T, Wo.T
    perm = np.concatenate([np.r_[64 * c:64 * c + 64, 64 * (c + 8):64 * (c + 8) + 64]
                           for c in range(N_CORES)])
    WoTp = np.ascontiguousarray(WoT[perm, :])
    maps = []
    for c in range(N_CORES):
        cols = np.r_[64 * c:64 * c + 64, 64 * (c + 8):64 * (c + 8) + 64]
        maps.append({
            "xT": np.ascontiguousarray(x[c * TOWN:(c + 1) * TOWN].T),
            "wqT": np.ascontiguousarray(WqT[:, cols]),
            "wkT": np.ascontiguousarray(WkT[:, cols]),
            "wvT": np.ascontiguousarray(WvT[:, cols]),
            "woT": WoTp,
            "rw": rw_, "rbc": rb_,
            "w1": w1_, "w3": w3_, "w2": w2_, "s1": s1_, "s3": s3_, "s2": s2_,
        })
    return maps


def kernel(**inputs):
    global _NC
    if _NC is None:
        _NC = build_nc()
    maps = _in_maps(inputs)
    res = run_bass_kernel_spmd(_NC, maps, core_ids=list(range(N_CORES)))
    out = np.empty((B * S, D), dtype=np.float32)
    for c in range(N_CORES):
        out[c * TOWN:(c + 1) * TOWN] = np.asarray(res.results[c]["yT"]).T
    return out.reshape(B, S, D)


# version 05
# speedup vs baseline: 1.0370x; 1.0370x over previous
import contextlib
import numpy as np
import ml_dtypes
import concourse.bass as bass
import concourse.mybir as mybir
from concourse.bass_utils import run_bass_kernel_spmd

F32 = mybir.dt.float32
BF16 = mybir.dt.bfloat16
ALU = mybir.AluOpType
ACT = mybir.ActivationFunctionType
AX = mybir.AxisListType.X
BF = ml_dtypes.bfloat16

N_CORES = 8
D = 1024
DC = 8
HFF = 512
NEXP = 10
NROUTED = 8
B = 4
S = 4096
TOWN = 2048
TT = 512
NTILES = TOWN // TT
NQT = S // 128
NEG = -1.0e9

PHASES = {"ffn1": True, "attn": True, "ffn2": True}
DEBUG = False
ATT = {"qkv": True, "passa": True, "passb": True, "summary": True,
       "strided": 3, "norm": True, "ag2": True, "wo": True}


def _masks_np():
    r = np.arange(128)[:, None]
    c = np.arange(256)[None, :]
    okA = (c - r >= 65) & (c - r <= 128)
    okA0 = (r - c >= 0) & (r - c < 64)
    cc = c - 128
    okB = (c >= 128) & (cc // 64 == r // 64) & (cc <= r)
    okB0 = (c < 128) & (c // 64 == r // 64) & (c <= r)

    def addm(ok):
        return np.where(ok, 0.0, NEG).astype(np.float32)

    def mulT(ok):
        m = np.zeros((128, 256), dtype=BF)
        m[:, 0:128] = ok[:, 0:128].T.astype(BF)
        m[:, 128:256] = ok[:, 128:256].T.astype(BF)
        return m

    msum = np.zeros((64, NQT * 128), dtype=BF)
    cc64 = np.arange(64)[:, None]
    rr = np.arange(128)[None, :]
    for qt in range(NQT):
        msum[:, qt * 128:(qt + 1) * 128] = (cc64 < 2 * qt + (rr >= 64)).astype(BF)
    bj = np.arange(64)[:, None]
    qi = np.arange(64)[None, :]
    m1 = (bj < qi).astype(BF)
    mstr = np.zeros((128, 512), dtype=BF)
    for g in range(8):
        mstr[0:64, g * 64:(g + 1) * 64] = m1
        mstr[64:128, g * 64:(g + 1) * 64] = m1
    return (addm(okA), addm(okA0), addm(okB), addm(okB0),
            mulT(okA), mulT(okA0), mulT(okB), mulT(okB0), msum, mstr)


class Ctx:
    def __init__(self, nc, es):
        self.nc = nc
        self.es = es
        self.sems = {}
        self.lastw = {}
        self.marks = {}

    def sem(self, name):
        if name not in self.sems:
            h = self.es.enter_context(self.nc.semaphore(name))
            s = type("S", (), {})()
            s.h, s.n, s.name = h, 0, name
            self.sems[name] = s
        return self.sems[name]

    def W(self, eng, s, val):
        if val is None or val <= 0:
            return
        key = (eng, s.name)
        if self.lastw.get(key, -1) >= val:
            return
        getattr(self.nc, eng).wait_ge(s.h, val)
        self.lastw[key] = val

    def inc(self, instr, s, k=1):
        instr.then_inc(s.h, k)
        s.n += k
        return s.n

    def barrier(self, engines=("tensor", "vector", "scalar", "sync", "gpsimd"), skip=()):
        for e in engines:
            for s in self.sems.values():
                if s.name in skip:
                    continue
                self.W(e, s, s.n)


def build_nc():
    nc = bass.Bass("TRN2", target_bir_lowering=False)
    T, V, SC, SY, GP = nc.tensor, nc.vector, nc.scalar, nc.sync, nc.gpsimd

    xT = nc.dram_tensor("xT", [D, TOWN], F32, kind="ExternalInput").ap()
    wqT = nc.dram_tensor("wqT", [D, 128], F32, kind="ExternalInput").ap()
    wkT = nc.dram_tensor("wkT", [D, 128], F32, kind="ExternalInput").ap()
    wvT = nc.dram_tensor("wvT", [D, 128], F32, kind="ExternalInput").ap()
    woT = nc.dram_tensor("woT", [D, D], F32, kind="ExternalInput").ap()
    rw = nc.dram_tensor("rw", [D, NROUTED], F32, kind="ExternalInput").ap()
    rbc = nc.dram_tensor("rbc", [NROUTED, 1], F32, kind="ExternalInput").ap()
    w13d = nc.dram_tensor("w13d", [NEXP, 128, 2 * DC * HFF], F32, kind="ExternalInput").ap()
    w2d = nc.dram_tensor("w2d", [DC, 128, NEXP * 4 * 128], F32, kind="ExternalInput").ap()
    yT = nc.dram_tensor("yT", [D, TOWN], F32, kind="ExternalOutput").ap()

    if DEBUG:
        dbg_t = {
            "dbg_xln": nc.dram_tensor("dbg_xln", [128, DC * TT], BF16),
            "dbg_l": nc.dram_tensor("dbg_l", [NROUTED, TT], BF16),
            "dbg_g": nc.dram_tensor("dbg_g", [NROUTED, TT], BF16),
            "dbg_hh": nc.dram_tensor("dbg_hh", [128, NEXP * 4 * TT], BF16),
        }
    cc_xa_in = nc.dram_tensor("cc_xa_in", [NTILES, 128, DC * TT], BF16)
    cc_xa_out = nc.dram_tensor("cc_xa_out", [N_CORES, NTILES, 128, DC * TT], BF16,
                               addr_space="Shared")
    vperm_dr = nc.dram_tensor("vperm_dr", [S, 130], BF16)
    cc_o_in = nc.dram_tensor("cc_o_in", [B, 2, 64, S], BF16)
    cc_o_out = nc.dram_tensor("cc_o_out", [B, N_CORES, 2, 64, S], BF16, addr_space="Shared")

    (mA_np, mA0_np, mB_np, mB0_np, mAT_np, mA0T_np, mBT_np, mB0T_np,
     msum_np, mstr_np) = _masks_np()
    oneh_np = np.zeros((NROUTED, NROUTED * 128), dtype=BF)
    for e in range(NROUTED):
        oneh_np[e, e * 128:(e + 1) * 128] = 1.0
    consts = [
        ("c_ident", np.eye(128, dtype=BF)), ("c_ones", np.ones((128, 128), dtype=BF)),
        ("c_oneh", oneh_np),
        ("c_onesf", np.ones((128, 64), dtype=np.float32)),
        ("c_mA", mA_np), ("c_mA0", mA0_np), ("c_mB", mB_np), ("c_mB0", mB0_np),
        ("c_mAT", mAT_np), ("c_mA0T", mA0T_np), ("c_mBT", mBT_np), ("c_mB0T", mB0T_np),
        ("c_msum", msum_np), ("c_mstr", mstr_np),
    ]
    c_handles = {n: nc.inline_tensor(a, name=n) for n, a in consts}
    c_shapes = {n: a for n, a in consts}

    es_top = contextlib.ExitStack()
    with es_top:
        ctx = Ctx(nc, es_top)
        sb = lambda name, shape, dt: es_top.enter_context(nc.sbuf_tensor(name, shape, dt))

        x_res = sb("x_res", [128, DC * TOWN], F32)
        csb = {}
        for n, a in consts:
            dt = BF16 if a.dtype == BF else F32
            csb[n] = sb(n[2:], list(a.shape), dt)
        ident, ones = csb["c_ident"], csb["c_ones"]
        mA, mA0, mB, mB0 = csb["c_mA"], csb["c_mA0"], csb["c_mB"], csb["c_mB0"]
        mAT, mA0T, mBT, mB0T = csb["c_mAT"], csb["c_mA0T"], csb["c_mBT"], csb["c_mB0T"]
        msum, mstr = csb["c_msum"], csb["c_mstr"]
        oneh = csb["c_oneh"]
        onesf = csb["c_onesf"]
        rw_sb = sb("rw_sb", [128, DC * NROUTED], BF16)
        rb_sb = sb("rb_sb", [NROUTED, 1], F32)

        pb = [es_top.enter_context(nc.psum_tensor(f"pb{i}", [128, 512], F32)) for i in range(7)]
        pbT = es_top.enter_context(nc.psum_tensor("pbT", [128, 1024], BF16))

        et, ev, esc = ctx.sem("et"), ctx.sem("ev"), ctx.sem("esc")
        dsy = ctx.sem("dsy")
        dgp = ctx.sem("dgp")
        dw = [ctx.sem("dw0"), ctx.sem("dw1")]
        dwx = [ctx.sem("dwa"), ctx.sem("dwb")]
        dx = [ctx.sem("dx0"), ctx.sem("dx1")]
        dmr = ctx.sem("dmr")
        dvp = ctx.sem("dvp")
        dvs = ctx.sem("dvs")
        dco = ctx.sem("dco")
        dxa = ctx.sem("dxa")
        dcc = ctx.sem("dcc")
        dol = [ctx.sem("dol0"), ctx.sem("dol1")]
        dst = ctx.sem("dst")

        SY.dma_start(out=x_res[:, :].rearrange("p (c t) -> p c t", c=DC),
                     in_=xT.rearrange("(c p) t -> p c t", p=128)).then_inc(dsy.h, 16)
        dsy.n += 16
        for n, _ in consts:
            SY.dma_start(out=csb[n][:, :], in_=c_handles[n].ap()).then_inc(dsy.h, 16)
            dsy.n += 16
        SY.dma_start(out=rb_sb[:, :], in_=rbc).then_inc(dsy.h, 16)
        dsy.n += 16
        GP.dma_start(out=rw_sb[:, :].rearrange("p (c e) -> p c e", c=DC),
                     in_=rw.rearrange("(c p) e -> p c e", p=128)).then_inc(dgp.h, 16)
        dgp.n += 16
        base_sy = dsy.n
        base_gp = dgp.n

        def emit_ffn(tag):
            with contextlib.ExitStack() as es:
                fsb = lambda name, shape, dt: es.enter_context(
                    nc.sbuf_tensor(f"{tag}_{name}", shape, dt))
                xbf8 = fsb("xbf8", [128, DC * TT], BF16)
                sq = fsb("sq", [128, TT], BF16)
                xln = xbf8
                mbc = fsb("mbc", [128, TT], F32)
                ex2 = fsb("ex2", [128, TT], F32)
                rbcst = fsb("rbcst", [128, TT], F32)
                lnt = fsb("lnt", [128, TT], F32)
                l_sb = fsb("l_sb", [NROUTED, TT], BF16)
                topf = fsb("topf", [128, 48], F32)
                ge_sb = fsb("ge_sb", [128, NROUTED], F32)
                gbf = fsb("gbf", [128, NROUTED], BF16)
                g_sb = fsb("g_sb", [NROUTED, TT], BF16)
                silu_sb = fsb("silu_sb", [128, 2 * TT], BF16)
                hh = fsb("hh", [128, NEXP * 4 * TT], BF16)
                w13 = [fsb("w13_0", [128, 2 * DC * HFF], BF16),
                       fsb("w13_1", [128, 2 * DC * HFF], BF16)]
                w2b = [fsb("w2b_0", [128, NEXP * 4 * 128], BF16),
                       fsb("w2b_1", [128, NEXP * 4 * 128], BF16)]

                w13_expect = [0, 0]

                def issue_w13(e, slot, wait_et):
                    ctx.W("gpsimd", et, wait_et)
                    GP.dma_start(out=w13[slot][:, :], in_=w13d[e]).then_inc(dw[slot].h, 16)
                    dw[slot].n += 16
                    w13_expect[slot] = dw[slot].n

                w2_expect = [0, 0]

                def issue_w2(dcm, slot, wait_et):
                    ctx.W("gpsimd", et, wait_et)
                    GP.dma_start(out=w2b[slot][:, :], in_=w2d[dcm]).then_inc(dwx[slot].h, 16)
                    dwx[slot].n += 16
                    w2_expect[slot] = dwx[slot].n

                issue_w13(0, 0, 0)
                issue_w13(1, 1, 0)
                w13_loaded = {0: w13_expect[0], 1: w13_expect[1]}

                prev_stage2_mm = 0
                for tile in range(NTILES):
                    t0 = tile * TT

                    ctx.W("vector", dsy, base_sy)
                    ctx.W("vector", et, prev_stage2_mm)
                    for dcm in range(DC):
                        xs = x_res[:, dcm * TOWN + t0: dcm * TOWN + t0 + TT]
                        ins = V.tensor_copy(xbf8[:, dcm * TT:(dcm + 1) * TT], xs)
                    cast_ms = ctx.inc(ins, ev)
                    ctx.W("tensor", ev, cast_ms)
                    for dcm in range(DC):
                        ins = T.matmul(pb[4][:, 0:TT], ones[:, :], xbf8[:, dcm * TT:(dcm + 1) * TT],
                                       start=(dcm == 0), stop=(dcm == DC - 1))
                    ctx.inc(ins, et)
                    sq_read_ms = None
                    for dcm in range(DC):
                        if sq_read_ms is not None:
                            ctx.W("vector", et, sq_read_ms)
                        sqv = ctx.inc(
                            V.tensor_mul(sq[:, :], x_res[:, dcm * TOWN + t0:dcm * TOWN + t0 + TT],
                                         x_res[:, dcm * TOWN + t0:dcm * TOWN + t0 + TT]), ev)
                        ctx.W("tensor", ev, sqv)
                        sq_read_ms = ctx.inc(
                            T.matmul(pb[5][:, 0:TT], ones[:, :], sq[:, :],
                                     start=(dcm == 0), stop=(dcm == DC - 1)), et)
                    sums_ms = sq_read_ms
                    ctx.W("vector", et, sums_ms)
                    V.tensor_scalar(mbc[:, :], pb[4][:, 0:TT], 1.0 / D, None, ALU.mult)
                    mb_ms = ctx.inc(V.tensor_scalar(ex2[:, :], pb[5][:, 0:TT], 1.0 / D, None,
                                                    ALU.mult), ev)
                    ctx.W("vector", ev, mb_ms)
                    v1 = ctx.inc(V.tensor_mul(lnt[:, :], mbc[:, :], mbc[:, :]), ev)
                    ctx.W("vector", ev, v1)
                    v2 = ctx.inc(V.scalar_tensor_tensor(lnt[:, :], ex2[:, :], 1e-5,
                                                        lnt[:, :], ALU.add, ALU.subtract), ev)
                    ctx.W("scalar", ev, v2)
                    sc1 = ctx.inc(SC.activation(lnt[:, :], lnt[:, :], ACT.Ln), esc)
                    ctx.W("scalar", esc, sc1)
                    sc2 = ctx.inc(SC.activation(rbcst[:, :], lnt[:, :], ACT.Exp, scale=-0.5), esc)
                    for dcm in range(DC):
                        xs = x_res[:, dcm * TOWN + t0: dcm * TOWN + t0 + TT]
                        ins = V.tensor_sub(xbf8[:, dcm * TT:(dcm + 1) * TT], xs, mbc[:, :])
                    sub_ms = ctx.inc(ins, ev)
                    ctx.W("vector", ev, sub_ms)
                    ctx.W("vector", esc, sc2)
                    for dcm in range(DC):
                        ins = V.tensor_mul(xln[:, dcm * TT:(dcm + 1) * TT],
                                           xbf8[:, dcm * TT:(dcm + 1) * TT], rbcst[:, :])
                    xln_ms = ctx.inc(ins, ev)

                    ctx.W("tensor", ev, xln_ms)
                    ctx.W("tensor", dgp, base_gp)
                    for dcm in range(DC):
                        ins = T.matmul(pb[6][0:NROUTED, 0:TT],
                                       rw_sb[:, dcm * NROUTED:(dcm + 1) * NROUTED],
                                       xln[:, dcm * TT:(dcm + 1) * TT],
                                       start=(dcm == 0), stop=(dcm == DC - 1))
                    rt_ms = ctx.inc(ins, et)
                    ctx.W("vector", et, rt_ms)
                    lg_ms = ctx.inc(V.tensor_scalar(l_sb[:, :], pb[6][0:NROUTED, 0:TT],
                                                    rb_sb[:, :], None, ALU.add), ev)
                    ctx.W("tensor", ev, lg_ms)
                    for g in range(4):
                        ins = T.transpose(pbT[:, g * 8:(g + 1) * 8],
                                          l_sb[:, g * 128:(g + 1) * 128], ident[0:8, 0:8])
                    tr_ms = ctx.inc(ins, et)
                    prev_gt = tr_ms
                    for g in range(4):
                        lt = pbT[:, g * 8:(g + 1) * 8]
                        ctx.W("vector", et, prev_gt)
                        a1 = ctx.inc(V.reduce_max(topf[:, 0:1], lt, axis=AX, negate=True), ev)
                        ctx.W("vector", ev, a1)
                        a2 = ctx.inc(V.tensor_scalar(topf[:, 8:16], lt, topf[:, 0:1], 0.0,
                                                     ALU.add, ALU.is_lt), ev)
                        ctx.W("vector", ev, a2)
                        a2b = ctx.inc(V.tensor_scalar(topf[:, 40:48], topf[:, 8:16], 1.0, 1.0e30,
                                                      ALU.subtract, ALU.mult), ev)
                        a3a = ctx.inc(V.tensor_mul(topf[:, 16:24], lt, topf[:, 8:16]), ev)
                        ctx.W("vector", ev, a3a)
                        a3 = ctx.inc(V.tensor_add(topf[:, 16:24], topf[:, 16:24], topf[:, 40:48]), ev)
                        ctx.W("vector", ev, a3)
                        a4 = ctx.inc(V.reduce_max(topf[:, 1:2], topf[:, 16:24], axis=AX), ev)
                        ctx.W("scalar", ev, a1)
                        ctx.W("scalar", et, tr_ms)
                        e1 = ctx.inc(SC.activation(topf[:, 24:32], lt, ACT.Exp, bias=topf[:, 0:1]),
                                     esc)
                        ctx.W("vector", ev, a4)
                        a5 = ctx.inc(V.tensor_single_scalar(topf[:, 32:40], lt, topf[:, 1:2],
                                                            ALU.is_ge), ev)
                        ctx.W("vector", esc, e1)
                        ctx.W("vector", ev, a5)
                        a6 = ctx.inc(V.tensor_mul(ge_sb[:, :], topf[:, 24:32], topf[:, 32:40]), ev)
                        ctx.W("vector", ev, a6)
                        a7 = ctx.inc(V.reduce_sum(topf[:, 2:3], ge_sb[:, :], axis=AX), ev)
                        ctx.W("vector", ev, a7)
                        a8 = ctx.inc(V.reciprocal(topf[:, 3:4], topf[:, 2:3]), ev)
                        ctx.W("vector", ev, a8)
                        a9 = ctx.inc(V.tensor_scalar(gbf[:, :], ge_sb[:, :], topf[:, 3:4], None,
                                                     ALU.mult), ev)
                        ctx.W("tensor", ev, a9)
                        gt = ctx.inc(T.transpose(pbT[0:NROUTED, 512 + g * 128:512 + (g + 1) * 128],
                                                 gbf[:, :], ident[:, :]), et)
                        prev_gt = gt
                        ctx.W("vector", et, gt)
                        ctx.inc(V.tensor_copy(g_sb[:, g * 128:(g + 1) * 128],
                                              pbT[0:NROUTED, 512 + g * 128:512 + (g + 1) * 128]),
                                ev)
                    gates_ms = ev.n

                    evac_done = {0: 0, 1: 0}
                    silu_done = [0, 0]
                    gb_read = 0
                    next_load_e = 2
                    for e in range(NEXP):
                        slot = e % 2
                        ctx.W("tensor", dw[slot], w13_loaded[slot])
                        ctx.W("tensor", ev, xln_ms)
                        if e < NROUTED:
                            ctx.W("tensor", ev, gates_ms)
                            ctx.W("tensor", ev, gb_read)
                            gbc = ctx.inc(T.matmul(pb[6][:, 0:TT],
                                                   oneh[:, e * 128:(e + 1) * 128],
                                                   g_sb[:, :], start=True, stop=True), et)
                        for hc in range(4):
                            par = hc % 2
                            bh1, bh3 = pb[0 + 2 * par], pb[1 + 2 * par]
                            ctx.W("tensor", ev, evac_done[par])
                            for dcm in range(DC):
                                T.matmul(bh1[:, 0:TT],
                                         w13[slot][:, dcm * HFF + hc * 128:
                                                   dcm * HFF + hc * 128 + 128],
                                         xln[:, dcm * TT:(dcm + 1) * TT],
                                         start=(dcm == 0), stop=(dcm == DC - 1))
                            for dcm in range(DC):
                                ins = T.matmul(bh3[:, 0:TT],
                                               w13[slot][:, DC * HFF + dcm * HFF + hc * 128:
                                                         DC * HFF + dcm * HFF + hc * 128 + 128],
                                               xln[:, dcm * TT:(dcm + 1) * TT],
                                               start=(dcm == 0), stop=(dcm == DC - 1))
                            mm_ms = ctx.inc(ins, et)
                            sl = silu_sb[:, par * TT:(par + 1) * TT]
                            ctx.W("scalar", et, mm_ms)
                            ctx.W("scalar", ev, silu_done[par])
                            sl_ms = ctx.inc(SC.activation(sl, bh1[:, 0:TT], ACT.Silu), esc)
                            hho = hh[:, (e * 4 + hc) * TT:(e * 4 + hc + 1) * TT]
                            ctx.W("vector", esc, sl_ms)
                            if e < NROUTED:
                                tv = ctx.inc(V.tensor_mul(sl, sl, bh3[:, 0:TT]), ev)
                                ctx.W("vector", ev, tv)
                                ctx.W("vector", et, gbc)
                                silu_done[par] = ctx.inc(
                                    V.tensor_mul(hho, sl, pb[6][:, 0:TT]), ev)
                                gb_read = silu_done[par]
                            else:
                                silu_done[par] = ctx.inc(V.tensor_mul(hho, sl, bh3[:, 0:TT]), ev)
                            evac_done[par] = silu_done[par]
                        if next_load_e < NEXP:
                            issue_w13(next_load_e, slot, et.n)
                            w13_loaded[slot] = w13_expect[slot]
                            next_load_e += 1
                    hh_ms = ev.n
                    if tile + 1 < NTILES:
                        issue_w13(0, 0, et.n)
                        w13_loaded[0] = w13_expect[0]
                        issue_w13(1, 1, et.n)
                        w13_loaded[1] = w13_expect[1]

                    issue_w2(0, 0, et.n)
                    issue_w2(1, 1, et.n)
                    w2_loaded = {0: w2_expect[0], 1: w2_expect[1]}
                    out_evac = [0, 0]
                    out_evac_v = [0, 0]
                    for dcm in range(DC):
                        slot = dcm % 2
                        bank = pb[4 + slot]
                        ctx.W("tensor", dwx[slot], w2_loaded[slot])
                        ctx.W("tensor", ev, hh_ms)
                        ctx.W("tensor", ev, out_evac_v[slot])
                        for k in range(NEXP * 4):
                            ins = T.matmul(bank[:, 0:TT], w2b[slot][:, k * 128:(k + 1) * 128],
                                           hh[:, k * TT:(k + 1) * TT],
                                           start=(k == 0), stop=(k == NEXP * 4 - 1))
                        o_ms = ctx.inc(ins, et)
                        prev_stage2_mm = o_ms
                        ctx.W("vector", et, o_ms)
                        upd = ctx.inc(V.scalar_tensor_tensor(
                            x_res[:, dcm * TOWN + t0: dcm * TOWN + t0 + TT],
                            bank[:, 0:TT], 0.5,
                            x_res[:, dcm * TOWN + t0: dcm * TOWN + t0 + TT],
                            ALU.mult, ALU.add), ev)
                        out_evac_v[slot] = upd
                        if dcm + 2 < DC:
                            issue_w2(dcm + 2, slot, et.n)
                            w2_loaded[slot] = w2_expect[slot]
                if DEBUG and tag == "f1":
                    ctx.barrier()
                    for nm, t_sb in (("dbg_xln", xln), ("dbg_l", l_sb),
                                     ("dbg_g", g_sb), ("dbg_hh", hh)):
                        SY.dma_start(out=dbg_t[nm].ap(), in_=t_sb[:, :]).then_inc(dsy.h, 16)
                        dsy.n += 16
                    ctx.W("sync", dsy, dsy.n)
                ctx.barrier()

        def emit_ln2_and_ag1():
            with contextlib.ExitStack() as es:
                fsb = lambda name, shape, dt: es.enter_context(
                    nc.sbuf_tensor(f"ln2_{name}", shape, dt))
                xbf8 = fsb("xbf8", [128, DC * TT], BF16)
                sq = fsb("sq", [128, TT], BF16)
                xln = fsb("xln", [128, DC * TT], BF16)
                mbc = fsb("mbc", [128, TT], F32)
                ex2 = fsb("ex2", [128, TT], F32)
                rbcst = fsb("rbcst", [128, TT], F32)
                lnt = fsb("lnt", [128, TT], F32)
                for tile in range(NTILES):
                    t0 = tile * TT
                    ctx.W("vector", dsy, base_sy)
                    ctx.W("vector", dxa, dxa.n)
                    for dcm in range(DC):
                        xs = x_res[:, dcm * TOWN + t0: dcm * TOWN + t0 + TT]
                        ins = V.tensor_copy(xbf8[:, dcm * TT:(dcm + 1) * TT], xs)
                    cast_ms = ctx.inc(ins, ev)
                    ctx.W("tensor", ev, cast_ms)
                    for dcm in range(DC):
                        ins = T.matmul(pb[4][:, 0:TT], ones[:, :], xbf8[:, dcm * TT:(dcm + 1) * TT],
                                       start=(dcm == 0), stop=(dcm == DC - 1))
                    ctx.inc(ins, et)
                    sq_read_ms = None
                    for dcm in range(DC):
                        if sq_read_ms is not None:
                            ctx.W("vector", et, sq_read_ms)
                        sqv = ctx.inc(
                            V.tensor_mul(sq[:, :], x_res[:, dcm * TOWN + t0:dcm * TOWN + t0 + TT],
                                         x_res[:, dcm * TOWN + t0:dcm * TOWN + t0 + TT]), ev)
                        ctx.W("tensor", ev, sqv)
                        sq_read_ms = ctx.inc(
                            T.matmul(pb[5][:, 0:TT], ones[:, :], sq[:, :],
                                     start=(dcm == 0), stop=(dcm == DC - 1)), et)
                    sums_ms = sq_read_ms
                    ctx.W("vector", et, sums_ms)
                    V.tensor_scalar(mbc[:, :], pb[4][:, 0:TT], 1.0 / D, None, ALU.mult)
                    mb_ms = ctx.inc(V.tensor_scalar(ex2[:, :], pb[5][:, 0:TT], 1.0 / D, None,
                                                    ALU.mult), ev)
                    ctx.W("vector", ev, mb_ms)
                    v1 = ctx.inc(V.tensor_mul(lnt[:, :], mbc[:, :], mbc[:, :]), ev)
                    ctx.W("vector", ev, v1)
                    v2 = ctx.inc(V.scalar_tensor_tensor(lnt[:, :], ex2[:, :], 1e-5,
                                                        lnt[:, :], ALU.add, ALU.subtract), ev)
                    ctx.W("scalar", ev, v2)
                    scA = ctx.inc(SC.activation(lnt[:, :], lnt[:, :], ACT.Ln), esc)
                    ctx.W("scalar", esc, scA)
                    scB = ctx.inc(SC.activation(rbcst[:, :], lnt[:, :], ACT.Exp, scale=-0.5), esc)
                    for dcm in range(DC):
                        xs = x_res[:, dcm * TOWN + t0: dcm * TOWN + t0 + TT]
                        ins = V.tensor_sub(xbf8[:, dcm * TT:(dcm + 1) * TT], xs, mbc[:, :])
                    sub_ms = ctx.inc(ins, ev)
                    ctx.W("vector", ev, sub_ms)
                    ctx.W("vector", esc, scB)
                    for dcm in range(DC):
                        ins = V.tensor_mul(xln[:, dcm * TT:(dcm + 1) * TT],
                                           xbf8[:, dcm * TT:(dcm + 1) * TT], rbcst[:, :])
                    xln_ms = ctx.inc(ins, ev)
                    ctx.W("sync", ev, xln_ms)
                    SY.dma_start(out=cc_xa_in.ap()[tile], in_=xln[:, :]).then_inc(dxa.h, 16)
                    dxa.n += 16
                ctx.W("gpsimd", dxa, dxa.n)
                GP.collective_compute(
                    "AllGather", ALU.bypass, replica_groups=[list(range(N_CORES))],
                    ins=[cc_xa_in.ap().opt()], outs=[cc_xa_out.ap().opt()],
                ).then_inc(dcc.h, 1)
                dcc.n += 1
                ctx.barrier(skip=("dcc",))

        def emit_attention():
            with contextlib.ExitStack() as es:
                fsb = lambda name, shape, dt: es.enter_context(
                    nc.sbuf_tensor(f"at_{name}", shape, dt))
                wq_sb = fsb("wq", [128, 128], BF16)
                wk_sb = fsb("wk", [128, 128], BF16)
                wv_sb = fsb("wv", [128, 128], BF16)
                wq8 = fsb("wq8", [128, DC * 128], BF16)
                wk8 = fsb("wk8", [128, DC * 128], BF16)
                wv8 = fsb("wv8", [128, DC * 128], BF16)
                xa = [fsb("xa0", [128, DC * 512], BF16), fsb("xa1", [128, DC * 512], BF16)]
                qt_sb = fsb("qt", [128, S], BF16)
                kt_sb = fsb("kt", [128, S], BF16)
                v_tok = fsb("v_tok", [128, 32 * 144], BF16)
                vp_tok = fsb("vp_tok", [64, 64 * 72], BF16)
                vsum = fsb("vsum", [64, 72], BF16)
                ksum = fsb("ksum", [128, 64], BF16)
                kp_sb = fsb("kp", [64, S], BF16)
                qp_sb = fsb("qp", [64, S], BF16)
                o_acc = fsb("o_acc", [65, S], BF16)
                o_stage = fsb("o_stage", [64, S], BF16)
                zbuf = fsb("zbuf", [66, S], F32)
                m_row = fsb("m_row", [1, S], BF16)
                mp_row = fsb("mp_row", [1, S], BF16)
                mcol = fsb("mcol", [128, NQT], F32)
                mcolb = fsb("mcolb", [128, NQT], BF16)
                mst = fsb("mst", [NQT, 128], BF16)
                s_sc = fsb("s_sc", [128, 256], F32)
                pexp = fsb("pexp", [128, 256], BF16)
                pt_sb = fsb("pt", [128, 256], BF16)
                esb = fsb("esb", [128, 512], BF16)
                pts = fsb("pts", [128, 512], BF16)
                psum_ex = fsb("psum_ex", [64, 128], BF16)

                for w_dram, w_s in ((wqT, wq8), (wkT, wk8), (wvT, wv8)):
                    GP.dma_start(out=w_s[:, :].rearrange("p (c f) -> p c f", c=DC),
                                 in_=w_dram.rearrange("(c p) f -> p c f", p=128)).then_inc(dgp.h, 16)
                    dgp.n += 16
                wload = dgp.n
                ones_ms = ctx.inc(V.memset(
                    v_tok[:, :].rearrange("p (s w) -> p s w", w=144)[:, :, 64:65], 1.0), ev)
                ones_ms = ctx.inc(V.memset(
                    v_tok[:, :].rearrange("p (s w) -> p s w", w=144)[:, :, 136:137], 1.0), ev)

                for b in range(B):
                    for tg in range(8):
                        slot = tg % 2
                        shard = 2 * b + (tg // 4)
                        ctx.W("sync", dcc, 1)
                        ctx.W("sync", et, ctx.marks.get(("xaw", slot), 0))
                        SY.dma_start(
                            out=xa[slot][:, :],
                            in_=cc_xa_out.ap()[shard, tg % 4],
                        ).then_inc(dx[slot].h, 16)
                        dx[slot].n += 16
                        ctx.W("tensor", dx[slot], dx[slot].n)
                        ctx.W("tensor", dgp, wload)
                        ctx.W("tensor", ev, ctx.marks.get(("qkv_evac", tg % 2), 0))
                        qbQ, qbK, qbV = pb[3 * (tg % 2)], pb[3 * (tg % 2) + 1], pb[3 * (tg % 2) + 2]
                        for dcm in range(DC):
                            T.matmul(qbQ[:, 0:512], wq8[:, dcm * 128:(dcm + 1) * 128],
                                     xa[slot][:, dcm * 512:(dcm + 1) * 512],
                                     start=(dcm == 0), stop=(dcm == DC - 1))
                        for dcm in range(DC):
                            T.matmul(qbK[:, 0:512], wk8[:, dcm * 128:(dcm + 1) * 128],
                                     xa[slot][:, dcm * 512:(dcm + 1) * 512],
                                     start=(dcm == 0), stop=(dcm == DC - 1))
                        for sub in range(4):
                            for dcm in range(DC):
                                ins = T.matmul(qbV[:, sub * 128:(sub + 1) * 128],
                                               xa[slot][:, dcm * 512 + sub * 128:
                                                        dcm * 512 + sub * 128 + 128],
                                               wv8[:, dcm * 128:(dcm + 1) * 128],
                                               start=(dcm == 0), stop=(dcm == DC - 1))
                        mm_ms = ctx.inc(ins, et)
                        ctx.marks[("xaw", slot)] = mm_ms
                        tcol = tg * 512
                        ctx.W("vector", et, mm_ms)
                        ctx.W("vector", ev, ones_ms)
                        V.tensor_scalar(qt_sb[:, tcol:tcol + 512], qbQ[:, 0:512], 0.125,
                                        None, ALU.mult)
                        V.tensor_copy(kt_sb[:, tcol:tcol + 512], qbK[:, 0:512])
                        for sub in range(4):
                            kt_i = tg * 4 + sub
                            V.tensor_copy(v_tok[:, kt_i * 144:kt_i * 144 + 64],
                                          qbV[0:128, sub * 128:sub * 128 + 64])
                            ins = V.tensor_copy(v_tok[:, kt_i * 144 + 72:kt_i * 144 + 136],
                                                qbV[0:128, sub * 128 + 64:sub * 128 + 128])
                        ev_ms = ctx.inc(ins, ev)
                        ctx.marks[("qkv_evac", tg % 2)] = ev_ms
                    qkv_ms = ev.n

                    for h in range(2):
                        p0 = 0 if h == 0 else 64
                        tp = None if h == 0 else (64, 0)
                        madd = (mA, mA0) if h == 0 else (mB, mB0)
                        mmulT = (mAT, mA0T) if h == 0 else (mBT, mB0T)

                        if not ATT["passa"]:
                            continue
                        ctx.W("tensor", ev, qkv_ms)
                        prev_rmax = 0
                        for qt in range(NQT):
                            kc0 = 0 if qt == 0 else (qt - 1) * 128
                            bank = pb[0 + (qt % 2)]
                            ctx.W("tensor", ev, ctx.marks.get(("paev", qt % 2), 0))
                            pa_ms = ctx.inc(
                                T.matmul(bank[:, 0:256],
                                         qt_sb[p0:p0 + 64, qt * 128:(qt + 1) * 128],
                                         kt_sb[p0:p0 + 64, kc0:kc0 + 256],
                                         start=True, stop=True, tile_position=tp), et)
                            ctx.W("vector", et, pa_ms)
                            ctx.W("vector", ev, prev_rmax)
                            mk = madd[1] if qt == 0 else madd[0]
                            vv = ctx.inc(V.tensor_add(s_sc[:, :], bank[:, 0:256], mk[:, :]), ev)
                            ctx.W("vector", ev, vv)
                            prev_rmax = ctx.inc(V.reduce_max(mcol[:, qt:qt + 1], s_sc[:, :],
                                                             axis=AX, negate=True), ev)
                            ctx.marks[("paev", qt % 2)] = prev_rmax
                        ctx.W("vector", ev, ev.n)
                        mc_ms = ctx.inc(V.tensor_copy(mcolb[:, :], mcol[:, :]), ev)
                        ctx.W("tensor", ev, mc_ms)
                        mr_ms = ctx.inc(T.matmul(pb[6][0:NQT, 0:128], mcolb[:, :], ident[:, :],
                                                 start=True, stop=True), et)
                        ctx.W("vector", et, mr_ms)
                        ms_ms = ctx.inc(V.tensor_copy(mst[:, :], pb[6][0:NQT, 0:128]), ev)
                        ctx.W("sync", ev, ms_ms)
                        ctx.W("sync", et, et.n)
                        SY.dma_start(out=m_row[0:1, :], in_=mst[:, :]).then_inc(dmr.h, 16)
                        dmr.n += 16

                        if h == 0 and ATT["strided"] >= 1:
                            ctx.W("vector", ev, qkv_ms)
                            ctx.W("vector", et, et.n)
                            kpv = ctx.inc(V.tensor_copy(
                                kp_sb[:, :].rearrange("p (r bj) -> p r bj", r=64),
                                kt_sb[0:64, :].rearrange("p (bj r) -> p r bj", bj=64)), ev)
                            qpv = ctx.inc(V.tensor_copy(
                                qp_sb[:, :].rearrange("p (r bj) -> p r bj", r=64),
                                qt_sb[0:64, :].rearrange("p (bj r) -> p r bj", bj=64)), ev)
                            ctx.W("vector", dmr, dmr.n)
                            mpv = ctx.inc(V.tensor_copy(
                                mp_row[:, :].rearrange("o (r bj) -> o r bj", r=64),
                                m_row[:, :].rearrange("o (bj r) -> o r bj", bj=64)), ev)
                            ctx.W("sync", ev, qkv_ms)
                            ctx.W("sync", et, et.n)
                            ctx.W("sync", dvp, dvp.n)
                            SY.dma_start(
                                out=vperm_dr.ap().rearrange("(s p) f -> p s f", p=128)[:, :, 0:65],
                                in_=v_tok[:, :].rearrange("p (s w) -> p s w", w=144)[:, :, 0:65],
                            ).then_inc(dvp.h, 16)
                            SY.dma_start(
                                out=vperm_dr.ap().rearrange("(s p) f -> p s f", p=128)[:, :, 65:130],
                                in_=v_tok[:, :].rearrange("p (s w) -> p s w", w=144)[:, :, 72:137],
                            ).then_inc(dvp.h, 16)
                            dvp.n += 32
                            ctx.W("sync", dvp, dvp.n)
                            SY.dma_start(
                                out=vp_tok[:, :].rearrange("p (s w) -> p s w", w=72)[:, :, 0:65],
                                in_=vperm_dr.ap()
                                    .rearrange("(bj r) f -> bj r f", r=64)[:, :, 0:65],
                            ).then_inc(dvp.h, 16)
                            dvp.n += 16
                        elif h == 1 and ATT["summary"]:
                            ctx.W("vector", ev, qkv_ms)
                            ctx.W("vector", et, et.n)
                            ksv = ctx.inc(V.tensor_copy(
                                ksum[64:128, :],
                                kt_sb[64:128, :].rearrange("p (c r) -> p c r", r=64)[:, :, 63]), ev)
                            ctx.W("sync", ev, qkv_ms)
                            ctx.W("sync", et, et.n)
                            ctx.W("sync", dvp, dvp.n)
                            SY.dma_start(
                                out=vsum[:, 0:65],
                                in_=vperm_dr.ap()
                                    .rearrange("(c u) f -> c u f", u=64)[:, 63, 65:130],
                            ).then_inc(dvs.h, 16)
                            dvs.n += 16

                        if not ATT["passb"]:
                            continue
                        ctx.W("tensor", dmr, dmr.n)
                        if h == 1:
                            ctx.W("tensor", dvs, dvs.n)
                            ctx.W("tensor", ev, ev.n)
                        for qt in range(NQT):
                            kc0 = 0 if qt == 0 else (qt - 1) * 128
                            bank = pb[2 + (qt % 2)]
                            bo = pb[4 + (qt % 2)]
                            ctx.W("tensor", esc, ctx.marks.get(("pbexp", qt % 2), 0))
                            for half in range(2):
                                T.matmul(bank[:, half * 128:(half + 1) * 128],
                                         kt_sb[p0:p0 + 64, kc0 + half * 128:kc0 + half * 128 + 128],
                                         qt_sb[p0:p0 + 64, qt * 128:(qt + 1) * 128],
                                         start=True, stop=False, tile_position=tp)
                                ins = T.matmul(bank[:, half * 128:(half + 1) * 128],
                                               ones[0:1, 0:128],
                                               m_row[0:1, qt * 128:(qt + 1) * 128],
                                               start=False, stop=True)
                            if h == 1 and ATT["summary"]:
                                ctx.W("tensor", esc, ctx.marks.get(("sumexS",), 0))
                                T.matmul(pb[6][0:64, 0:128], ksum[64:128, :],
                                         qt_sb[64:128, qt * 128:(qt + 1) * 128],
                                         start=True, stop=False, tile_position=(64, 0))
                                ins = T.matmul(pb[6][0:64, 0:128], ones[0:1, 0:64],
                                               m_row[0:1, qt * 128:(qt + 1) * 128],
                                               start=False, stop=True)
                            pbm = ctx.inc(ins, et)
                            ctx.W("scalar", et, pbm)
                            ctx.W("scalar", ev, ctx.marks.get(("pbpt", qt % 2), 0))
                            ex1 = ctx.inc(SC.activation(pexp[:, :], bank[:, 0:256], ACT.Exp), esc)
                            ctx.marks[("pbexp", qt % 2)] = ex1
                            ctx.W("vector", esc, ex1)
                            mkT = mmulT[1] if qt == 0 else mmulT[0]
                            pt1 = ctx.inc(V.tensor_mul(pt_sb[:, :], pexp[:, :], mkT[:, :]), ev)
                            ctx.marks[("pbpt", qt % 2)] = pt1
                            if h == 1 and ATT["summary"]:
                                ctx.W("scalar", ev, ctx.marks.get(("sumex",), 0))
                                sx = ctx.inc(SC.activation(psum_ex[:, :], pb[6][0:64, 0:128],
                                                           ACT.Exp), esc)
                                ctx.marks[("sumexS",)] = sx
                                ctx.W("vector", esc, sx)
                                ctx.marks[("sumex",)] = ctx.inc(V.tensor_mul(
                                    psum_ex[:, :], psum_ex[:, :],
                                    msum[:, qt * 128:(qt + 1) * 128]), ev)
                            ctx.W("tensor", ev, ev.n)
                            ctx.W("tensor", ev, ctx.marks.get(("poev", qt % 2), 0))
                            kt0 = 0 if qt == 0 else qt - 1
                            for half in range(2):
                                ktile = kt0 + half
                                ins = T.matmul(bo[0:65, 0:128],
                                               v_tok[:, ktile * 144 + h * 72:
                                                     ktile * 144 + h * 72 + 65],
                                               pt_sb[:, half * 128:(half + 1) * 128],
                                               start=(half == 0),
                                               stop=(half == 1 and (h == 0 or not ATT["summary"])))
                            if h == 1 and ATT["summary"]:
                                ins = T.matmul(bo[0:65, 0:128], vsum[:, 0:65], psum_ex[:, :],
                                               start=False, stop=True)
                            pv_ms = ctx.inc(ins, et)
                            ctx.W("vector", et, pv_ms)
                            ctx.marks[("poev", qt % 2)] = ctx.inc(
                                V.tensor_copy(o_acc[:, qt * 128:(qt + 1) * 128],
                                              bo[0:65, 0:128]), ev)

                        if h == 0 and ATT["strided"] >= 2:
                            ctx.W("tensor", ev, ev.n)
                            ctx.W("tensor", esc, esc.n)
                            ctx.W("tensor", dvp, dvp.n)
                            for rg in range(8):
                                bs = pb[2 + (rg % 2)]
                                bos = pb[4 + (rg % 2)]
                                ctx.W("tensor", esc, ctx.marks.get(("strex", rg % 2), 0))
                                ctx.W("tensor", ev, ctx.marks.get(("stradd", rg % 2), 0))
                                for rr in range(8):
                                    r = rg * 8 + rr
                                    T.matmul(bs[0:64, rr * 64:(rr + 1) * 64],
                                             kp_sb[:, r * 64:(r + 1) * 64],
                                             qp_sb[:, r * 64:(r + 1) * 64],
                                             start=True, stop=False)
                                    ins = T.matmul(bs[0:64, rr * 64:(rr + 1) * 64],
                                                   ones[0:1, 0:64],
                                                   mp_row[0:1, r * 64:(r + 1) * 64],
                                                   start=False, stop=True)
                                sm = ctx.inc(ins, et)
                                ctx.W("scalar", et, sm)
                                ctx.W("scalar", ev, ctx.marks.get(("strpt", rg % 2), 0))
                                se = ctx.inc(SC.activation(esb[0:64, :], bs[0:64, 0:512],
                                                           ACT.Exp), esc)
                                ctx.marks[("strex", rg % 2)] = se
                                ctx.W("vector", esc, se)
                                sp = ctx.inc(V.tensor_mul(pts[0:64, :], esb[0:64, :],
                                                          mstr[0:64, :]), ev)
                                ctx.marks[("strpt", rg % 2)] = sp
                                ctx.W("tensor", ev, sp)
                                if ATT["strided"] < 3:
                                    ctx.marks[("stradd", rg % 2)] = sp
                                    continue
                                for rr in range(8):
                                    r = rg * 8 + rr
                                    ins = T.matmul(bos[0:65, rr * 64:(rr + 1) * 64],
                                                   vp_tok[:, r * 72:r * 72 + 65],
                                                   pts[0:64, rr * 64:(rr + 1) * 64],
                                                   start=True, stop=True)
                                spv = ctx.inc(ins, et)
                                ctx.W("vector", et, spv)
                                oacc_v = o_acc[:, :].rearrange("p (qi r) -> p r qi",
                                                               r=64)[:, rg * 8:(rg + 1) * 8, :]
                                ctx.marks[("stradd", rg % 2)] = ctx.inc(V.tensor_add(
                                    oacc_v, oacc_v,
                                    bos[0:65, 0:512].rearrange("p (rr qi) -> p rr qi", rr=8)), ev)

                        if not ATT["norm"]:
                            continue
                        ctx.W("vector", ev, ev.n)
                        nz0 = ctx.inc(V.tensor_copy(zbuf[64:65, :], o_acc[64:65, :]), ev)
                        ctx.W("scalar", ev, nz0)
                        z1 = ctx.inc(SC.activation(zbuf[64:65, :], zbuf[64:65, :], ACT.Ln), esc)
                        ctx.W("scalar", esc, z1)
                        z2 = ctx.inc(SC.activation(zbuf[64:65, :], zbuf[64:65, :], ACT.Exp,
                                                   scale=-1.0), esc)
                        ctx.W("tensor", esc, z2)
                        ctx.W("vector", dco, dco.n)
                        for g in range(8):
                            bz = pb[2 + (g % 2)]
                            ctx.W("tensor", ev, ctx.marks.get(("zev", g % 2), 0))
                            zm = ctx.inc(T.matmul(bz[0:64, 0:512], onesf[64:65, 0:64],
                                                  zbuf[64:65, g * 512:(g + 1) * 512],
                                                  start=True, stop=True), et)
                            ctx.W("vector", et, zm)
                            ctx.marks[("zev", g % 2)] = ctx.inc(
                                V.tensor_mul(o_stage[:, g * 512:(g + 1) * 512],
                                             o_acc[0:64, g * 512:(g + 1) * 512],
                                             bz[0:64, 0:512]), ev)
                        ctx.W("sync", ev, ev.n)
                        SY.dma_start(out=cc_o_in.ap()[b, h], in_=o_stage[:, :]).then_inc(dco.h, 16)
                        dco.n += 16
                    if ATT["ag2"]:
                        ctx.W("gpsimd", dco, dco.n)
                        GP.collective_compute(
                            "AllGather", ALU.bypass, replica_groups=[list(range(N_CORES))],
                            ins=[cc_o_in.ap()[b].opt()], outs=[cc_o_out.ap()[b].opt()],
                        ).then_inc(dcc.h, 1)
                        dcc.n += 1
                    ctx.barrier(engines=("tensor", "vector", "scalar"), skip=("dcc",))
                ctx.barrier(skip=("dcc",))

        def emit_wo():
            with contextlib.ExitStack() as es:
                fsb = lambda name, shape, dt: es.enter_context(
                    nc.sbuf_tensor(f"wo_{name}", shape, dt))
                wo_sb = fsb("wo", [128, DC * D], BF16)
                og = [fsb("og0", [128, DC * 512], BF16), fsb("og1", [128, DC * 512], BF16)]
                GP.dma_start(out=wo_sb[:, :].rearrange("p (c f) -> p c f", c=DC),
                             in_=woT.rearrange("(c p) f -> p c f", p=128)).then_inc(dgp.h, 16)
                dgp.n += 16
                wo_ld = dgp.n
                pid = SY.partition_id()
                row0 = SY.compute_val((pid // 2) * (N_CORES * 2 * 64))
                colb = SY.compute_val((pid % 2) * TOWN)
                flat = cc_o_out.ap().rearrange("b n h d t -> (b n h d) t")
                ctx.W("sync", dcc, dcc.n)
                for ts in range(4):
                    slot = ts % 2
                    ctx.W("sync", et, ctx.marks.get(("ogw", slot), 0))
                    col = SY.compute_val(colb + ts * 512)
                    SY.dma_start(
                        out=og[slot][:, :].rearrange("p (c t) -> p c t", c=DC),
                        in_=flat[bass.ds(row0, 1024), bass.ds(col, 512)]
                            .rearrange("(c p) t -> p c t", p=128),
                    ).then_inc(dol[slot].h, 16)
                    dol[slot].n += 16
                    ctx.W("tensor", dol[slot], dol[slot].n)
                    ctx.W("tensor", dgp, wo_ld)
                    t0 = ts * 512
                    for oc in range(DC):
                        bank = pb[4 + (oc % 2)]
                        ctx.W("tensor", ev, ctx.marks.get(("woev", oc % 2), 0))
                        for ic in range(DC):
                            ins = T.matmul(bank[:, 0:512],
                                           wo_sb[:, ic * D + oc * 128:ic * D + oc * 128 + 128],
                                           og[slot][:, ic * 512:(ic + 1) * 512],
                                           start=(ic == 0), stop=(ic == DC - 1))
                        wm = ctx.inc(ins, et)
                        ctx.marks[("ogw", slot)] = wm
                        ctx.W("vector", et, wm)
                        ctx.marks[("woev", oc % 2)] = ctx.inc(V.tensor_add(
                            x_res[:, oc * TOWN + t0: oc * TOWN + t0 + 512],
                            x_res[:, oc * TOWN + t0: oc * TOWN + t0 + 512],
                            bank[:, 0:512]), ev)
                ctx.barrier()

        if PHASES.get("ffn1"):
            emit_ffn("f1")
        if PHASES.get("attn"):
            emit_ln2_and_ag1()
            emit_attention()
            if ATT["wo"]:
                emit_wo()
        if PHASES.get("ffn2"):
            emit_ffn("f2")

        ctx.barrier()
        SY.dma_start(out=yT.rearrange("(c p) t -> p c t", p=128),
                     in_=x_res[:, :].rearrange("p (c t) -> p c t", c=DC)).then_inc(dst.h, 16)
        dst.n += 16
        ctx.W("sync", dst, dst.n)

    return nc


_NC = None


def _in_maps(inputs):
    x = np.asarray(inputs["L0_attn_x"], dtype=np.float32).reshape(B * S, D)
    Wq = np.asarray(inputs["L0_attn_Wq"], dtype=np.float32)
    Wk = np.asarray(inputs["L0_attn_Wk"], dtype=np.float32)
    Wv = np.asarray(inputs["L0_attn_Wv"], dtype=np.float32)
    Wo = np.asarray(inputs["L0_attn_Wo"], dtype=np.float32)
    rw_ = np.ascontiguousarray(np.asarray(inputs["L0_ffn_router_w"], dtype=np.float32))
    rb_ = np.asarray(inputs["L0_ffn_router_b"], dtype=np.float32).reshape(NROUTED, 1)
    w1_ = np.asarray(inputs["L0_ffn_routed_w1"], dtype=np.float32)
    w3_ = np.asarray(inputs["L0_ffn_routed_w3"], dtype=np.float32)
    w2_ = np.asarray(inputs["L0_ffn_routed_w2"], dtype=np.float32)
    s1_ = np.asarray(inputs["L0_ffn_shared_w1"], dtype=np.float32)
    s3_ = np.asarray(inputs["L0_ffn_shared_w3"], dtype=np.float32)
    s2_ = np.asarray(inputs["L0_ffn_shared_w2"], dtype=np.float32)
    def _w13_part(wa):
        return wa.reshape(-1, DC, 128, HFF).transpose(0, 2, 1, 3).reshape(-1, 128, DC * HFF)
    w13cat = np.concatenate([
        np.concatenate([_w13_part(w1_), _w13_part(w3_)], axis=2),
        np.concatenate([_w13_part(s1_), _w13_part(s3_)], axis=2)], axis=0)
    w13cat = np.ascontiguousarray(w13cat)
    def _w2_part(wa):
        return wa.reshape(-1, 4, 128, DC, 128).transpose(3, 2, 0, 1, 4)
    w2cat = np.concatenate([_w2_part(w2_), _w2_part(s2_)], axis=2)
    w2cat = np.ascontiguousarray(w2cat.reshape(DC, 128, NEXP * 4 * 128))
    WqT, WkT, WvT, WoT = Wq.T, Wk.T, Wv.T, Wo.T
    perm = np.concatenate([np.r_[64 * c:64 * c + 64, 64 * (c + 8):64 * (c + 8) + 64]
                           for c in range(N_CORES)])
    WoTp = np.ascontiguousarray(WoT[perm, :])
    maps = []
    for c in range(N_CORES):
        cols = np.r_[64 * c:64 * c + 64, 64 * (c + 8):64 * (c + 8) + 64]
        maps.append({
            "xT": np.ascontiguousarray(x[c * TOWN:(c + 1) * TOWN].T),
            "wqT": np.ascontiguousarray(WqT[:, cols]),
            "wkT": np.ascontiguousarray(WkT[:, cols]),
            "wvT": np.ascontiguousarray(WvT[:, cols]),
            "woT": WoTp,
            "rw": rw_, "rbc": rb_, "w13d": w13cat, "w2d": w2cat,
        })
    return maps


def kernel(**inputs):
    global _NC
    if _NC is None:
        _NC = build_nc()
    maps = _in_maps(inputs)
    res = run_bass_kernel_spmd(_NC, maps, core_ids=list(range(N_CORES)))
    out = np.empty((B * S, D), dtype=np.float32)
    for c in range(N_CORES):
        out[c * TOWN:(c + 1) * TOWN] = np.asarray(res.results[c]["yT"]).T
    return out.reshape(B, S, D)


# version 07
# speedup vs baseline: 1.2079x; 1.1647x over previous
import contextlib
import numpy as np
import ml_dtypes
import concourse.bass as bass
import concourse.mybir as mybir
from concourse.bass_utils import run_bass_kernel_spmd

F32 = mybir.dt.float32
BF16 = mybir.dt.bfloat16
ALU = mybir.AluOpType
ACT = mybir.ActivationFunctionType
AX = mybir.AxisListType.X
BF = ml_dtypes.bfloat16

N_CORES = 8
D = 1024
DC = 8
HFF = 512
NEXP = 10
NROUTED = 8
B = 4
S = 4096
TOWN = 2048
TT = 512
NTILES = TOWN // TT
NQT = S // 128
NEG = -1.0e9

PHASES = {"ffn1": True, "attn": True, "ffn2": True}
DEBUG = False
ATT = {"qkv": True, "passa": True, "passb": True, "summary": True,
       "strided": 3, "norm": True, "ag2": True, "wo": True}


def _masks_np():
    r = np.arange(128)[:, None]
    c = np.arange(256)[None, :]
    okA = (c - r >= 65) & (c - r <= 128)
    okA0 = (r - c >= 0) & (r - c < 64)
    cc = c - 128
    okB = (c >= 128) & (cc // 64 == r // 64) & (cc <= r)
    okB0 = (c < 128) & (c // 64 == r // 64) & (c <= r)

    def addm(ok):
        return np.where(ok, 0.0, NEG).astype(np.float32)

    def mulT(ok):
        m = np.zeros((128, 256), dtype=BF)
        m[:, 0:128] = ok[:, 0:128].T.astype(BF)
        m[:, 128:256] = ok[:, 128:256].T.astype(BF)
        return m

    msum = np.zeros((64, NQT * 128), dtype=BF)
    cc64 = np.arange(64)[:, None]
    rr = np.arange(128)[None, :]
    for qt in range(NQT):
        msum[:, qt * 128:(qt + 1) * 128] = (cc64 < 2 * qt + (rr >= 64)).astype(BF)
    bj = np.arange(64)[:, None]
    qi = np.arange(64)[None, :]
    m1 = (bj < qi).astype(BF)
    mstr = np.zeros((128, 512), dtype=BF)
    for g in range(8):
        mstr[0:64, g * 64:(g + 1) * 64] = m1
        mstr[64:128, g * 64:(g + 1) * 64] = m1
    return (addm(okA), addm(okA0), addm(okB), addm(okB0),
            mulT(okA), mulT(okA0), mulT(okB), mulT(okB0), msum, mstr)


class Ctx:
    def __init__(self, nc, es):
        self.nc = nc
        self.es = es
        self.sems = {}
        self.lastw = {}
        self.marks = {}

    def sem(self, name):
        if name not in self.sems:
            h = self.es.enter_context(self.nc.semaphore(name))
            s = type("S", (), {})()
            s.h, s.n, s.name = h, 0, name
            self.sems[name] = s
        return self.sems[name]

    def W(self, eng, s, val):
        if val is None or val <= 0:
            return
        key = (eng, s.name)
        if self.lastw.get(key, -1) >= val:
            return
        getattr(self.nc, eng).wait_ge(s.h, val)
        self.lastw[key] = val

    def inc(self, instr, s, k=1):
        instr.then_inc(s.h, k)
        s.n += k
        return s.n

    def barrier(self, engines=("tensor", "vector", "scalar", "sync", "gpsimd"), skip=()):
        for e in engines:
            for s in self.sems.values():
                if s.name in skip:
                    continue
                self.W(e, s, s.n)


def build_nc():
    nc = bass.Bass("TRN2", target_bir_lowering=False)
    T, V, SC, SY, GP = nc.tensor, nc.vector, nc.scalar, nc.sync, nc.gpsimd

    xT = nc.dram_tensor("xT", [D, TOWN], F32, kind="ExternalInput").ap()
    wqT = nc.dram_tensor("wqT", [D, 128], F32, kind="ExternalInput").ap()
    wkT = nc.dram_tensor("wkT", [D, 128], F32, kind="ExternalInput").ap()
    wvT = nc.dram_tensor("wvT", [D, 128], F32, kind="ExternalInput").ap()
    woT = nc.dram_tensor("woT", [D, D], F32, kind="ExternalInput").ap()
    rw = nc.dram_tensor("rw", [D, NROUTED], F32, kind="ExternalInput").ap()
    rbc = nc.dram_tensor("rbc", [NROUTED, 1], F32, kind="ExternalInput").ap()
    w13d = nc.dram_tensor("w13d", [NEXP, 128, 2 * DC * HFF], F32, kind="ExternalInput").ap()
    w2d = nc.dram_tensor("w2d", [NEXP, 128, 4 * DC * 128], F32, kind="ExternalInput").ap()
    yT = nc.dram_tensor("yT", [D, TOWN], F32, kind="ExternalOutput").ap()

    if DEBUG:
        dbg_t = {
            "dbg_xln": nc.dram_tensor("dbg_xln", [128, DC * TT], BF16),
            "dbg_l": nc.dram_tensor("dbg_l", [NROUTED, TT], BF16),
            "dbg_g": nc.dram_tensor("dbg_g", [NROUTED, TT], BF16),
            "dbg_hh": nc.dram_tensor("dbg_hh", [128, NEXP * 4 * TT], BF16),
        }
    cc_xa_in = nc.dram_tensor("cc_xa_in", [NTILES, 128, DC * TT], BF16)
    cc_xa_out = nc.dram_tensor("cc_xa_out", [N_CORES, NTILES, 128, DC * TT], BF16,
                               addr_space="Shared")
    vperm_dr = nc.dram_tensor("vperm_dr", [S, 130], BF16)
    cc_o_in = nc.dram_tensor("cc_o_in", [B, 2, 64, S], BF16)
    cc_o_out = nc.dram_tensor("cc_o_out", [B, N_CORES, 2, 64, S], BF16, addr_space="Shared")

    (mA_np, mA0_np, mB_np, mB0_np, mAT_np, mA0T_np, mBT_np, mB0T_np,
     msum_np, mstr_np) = _masks_np()
    oneh_np = np.zeros((NROUTED, NROUTED * 128), dtype=BF)
    for e in range(NROUTED):
        oneh_np[e, e * 128:(e + 1) * 128] = 1.0
    consts = [
        ("c_ident", np.eye(128, dtype=BF)), ("c_ones", np.ones((128, 128), dtype=BF)),
        ("c_oneh", oneh_np),
        ("c_onesf", np.ones((128, 64), dtype=np.float32)),
        ("c_mA", mA_np), ("c_mA0", mA0_np), ("c_mB", mB_np), ("c_mB0", mB0_np),
        ("c_mAT", mAT_np), ("c_mA0T", mA0T_np), ("c_mBT", mBT_np), ("c_mB0T", mB0T_np),
        ("c_msum", msum_np), ("c_mstr", mstr_np),
    ]
    c_handles = {n: nc.inline_tensor(a, name=n) for n, a in consts}
    c_shapes = {n: a for n, a in consts}

    es_top = contextlib.ExitStack()
    with es_top:
        ctx = Ctx(nc, es_top)
        sb = lambda name, shape, dt: es_top.enter_context(nc.sbuf_tensor(name, shape, dt))

        x_res = sb("x_res", [128, DC * TOWN], F32)
        csb = {}
        for n, a in consts:
            dt = BF16 if a.dtype == BF else F32
            csb[n] = sb(n[2:], list(a.shape), dt)
        ident, ones = csb["c_ident"], csb["c_ones"]
        mA, mA0, mB, mB0 = csb["c_mA"], csb["c_mA0"], csb["c_mB"], csb["c_mB0"]
        mAT, mA0T, mBT, mB0T = csb["c_mAT"], csb["c_mA0T"], csb["c_mBT"], csb["c_mB0T"]
        msum, mstr = csb["c_msum"], csb["c_mstr"]
        oneh = csb["c_oneh"]
        onesf = csb["c_onesf"]
        rw_sb = sb("rw_sb", [128, DC * NROUTED], BF16)
        rb_sb = sb("rb_sb", [NROUTED, 1], F32)

        pb = [es_top.enter_context(nc.psum_tensor(f"pb{i}", [128, 512], F32)) for i in range(7)]
        pbT = es_top.enter_context(nc.psum_tensor("pbT", [128, 1024], BF16))

        et, ev, esc = ctx.sem("et"), ctx.sem("ev"), ctx.sem("esc")
        dsy = ctx.sem("dsy")
        dgp = ctx.sem("dgp")
        dw = [ctx.sem("dw0"), ctx.sem("dw1")]
        dwx = [ctx.sem("dwa"), ctx.sem("dwb")]
        dx = [ctx.sem("dx0"), ctx.sem("dx1")]
        dmr = ctx.sem("dmr")
        dvp = ctx.sem("dvp")
        dvs = ctx.sem("dvs")
        dco = ctx.sem("dco")
        dxa = ctx.sem("dxa")
        dcc = ctx.sem("dcc")
        dol = [ctx.sem("dol0"), ctx.sem("dol1")]
        dst = ctx.sem("dst")

        SY.dma_start(out=x_res[:, :].rearrange("p (c t) -> p c t", c=DC),
                     in_=xT.rearrange("(c p) t -> p c t", p=128)).then_inc(dsy.h, 16)
        dsy.n += 16
        for n, _ in consts:
            SY.dma_start(out=csb[n][:, :], in_=c_handles[n].ap()).then_inc(dsy.h, 16)
            dsy.n += 16
        SY.dma_start(out=rb_sb[:, :], in_=rbc).then_inc(dsy.h, 16)
        dsy.n += 16
        GP.dma_start(out=rw_sb[:, :].rearrange("p (c e) -> p c e", c=DC),
                     in_=rw.rearrange("(c p) e -> p c e", p=128)).then_inc(dgp.h, 16)
        dgp.n += 16
        base_sy = dsy.n
        base_gp = dgp.n

        def emit_ffn(tag):
            with contextlib.ExitStack() as es:
                fsb = lambda name, shape, dt: es.enter_context(
                    nc.sbuf_tensor(f"{tag}_{name}", shape, dt))
                xln = fsb("xln", [128, DC * TOWN], BF16)
                sq = fsb("sq", [128, TT], BF16)
                mbc = fsb("mbc", [128, TT], F32)
                ex2 = fsb("ex2", [128, TT], F32)
                rbcst = fsb("rbcst", [128, TT], F32)
                lnt = fsb("lnt", [128, TT], F32)
                l_sb = fsb("l_sb", [NROUTED, TOWN], BF16)
                topf = fsb("topf", [128, 48], F32)
                ge_sb = fsb("ge_sb", [128, NROUTED], F32)
                gbf = fsb("gbf", [128, NROUTED], BF16)
                g_sb = fsb("g_sb", [NROUTED, TOWN], BF16)
                silu_sb = fsb("silu_sb", [128, 2 * TT], BF16)
                hh = fsb("hh", [128, 16 * TT], BF16)
                w13 = [fsb("w13_0", [128, 2 * DC * HFF], BF16),
                       fsb("w13_1", [128, 2 * DC * HFF], BF16)]
                w2b = [fsb("w2b_0", [128, 4 * DC * 128], BF16),
                       fsb("w2b_1", [128, 4 * DC * 128], BF16)]

                w13_expect = [0, 0]
                w2_expect = [0, 0]

                def issue_w(e, slot, wait_et):
                    ctx.W("gpsimd", et, wait_et)
                    GP.dma_start(out=w13[slot][:, :], in_=w13d[e]).then_inc(dw[slot].h, 16)
                    dw[slot].n += 16
                    w13_expect[slot] = dw[slot].n
                    GP.dma_start(out=w2b[slot][:, :], in_=w2d[e]).then_inc(dwx[slot].h, 16)
                    dwx[slot].n += 16
                    w2_expect[slot] = dwx[slot].n

                issue_w(0, 0, 0)
                issue_w(1, 1, 0)
                w13_loaded = {0: w13_expect[0], 1: w13_expect[1]}
                w2_loaded = {0: w2_expect[0], 1: w2_expect[1]}

                ctx.W("vector", dsy, base_sy)
                ctx.W("vector", et, et.n)
                for ts in range(4):
                    t0 = ts * TT
                    for dcm in range(DC):
                        xs = x_res[:, dcm * TOWN + t0: dcm * TOWN + t0 + TT]
                        ins = V.tensor_copy(xln[:, dcm * TOWN + t0:dcm * TOWN + t0 + TT], xs)
                    cast_ms = ctx.inc(ins, ev)
                    ctx.W("tensor", ev, cast_ms)
                    ctx.W("tensor", ev, ctx.marks.get(("lnap", ts), 0))
                    for dcm in range(DC):
                        ins = T.matmul(pb[4][:, 0:TT], ones[:, :],
                                       xln[:, dcm * TOWN + t0:dcm * TOWN + t0 + TT],
                                       start=(dcm == 0), stop=(dcm == DC - 1))
                    ctx.inc(ins, et)
                    sq_read_ms = None
                    for dcm in range(DC):
                        if sq_read_ms is not None:
                            ctx.W("vector", et, sq_read_ms)
                        sqv = ctx.inc(
                            V.tensor_mul(sq[:, :], x_res[:, dcm * TOWN + t0:dcm * TOWN + t0 + TT],
                                         x_res[:, dcm * TOWN + t0:dcm * TOWN + t0 + TT]), ev)
                        ctx.W("tensor", ev, sqv)
                        sq_read_ms = ctx.inc(
                            T.matmul(pb[5][:, 0:TT], ones[:, :], sq[:, :],
                                     start=(dcm == 0), stop=(dcm == DC - 1)), et)
                    sums_ms = sq_read_ms
                    ctx.W("vector", et, sums_ms)
                    V.tensor_scalar(mbc[:, :], pb[4][:, 0:TT], 1.0 / D, None, ALU.mult)
                    mb_ms = ctx.inc(V.tensor_scalar(ex2[:, :], pb[5][:, 0:TT], 1.0 / D, None,
                                                    ALU.mult), ev)
                    ctx.W("vector", ev, mb_ms)
                    v1 = ctx.inc(V.tensor_mul(lnt[:, :], mbc[:, :], mbc[:, :]), ev)
                    ctx.W("vector", ev, v1)
                    v2 = ctx.inc(V.scalar_tensor_tensor(lnt[:, :], ex2[:, :], 1e-5,
                                                        lnt[:, :], ALU.add, ALU.subtract), ev)
                    ctx.W("scalar", ev, v2)
                    sc1 = ctx.inc(SC.activation(lnt[:, :], lnt[:, :], ACT.Ln), esc)
                    ctx.W("scalar", esc, sc1)
                    sc2 = ctx.inc(SC.activation(rbcst[:, :], lnt[:, :], ACT.Exp, scale=-0.5), esc)
                    for dcm in range(DC):
                        xs = x_res[:, dcm * TOWN + t0: dcm * TOWN + t0 + TT]
                        ins = V.tensor_sub(xln[:, dcm * TOWN + t0:dcm * TOWN + t0 + TT],
                                           xs, mbc[:, :])
                    sub_ms = ctx.inc(ins, ev)
                    ctx.W("vector", ev, sub_ms)
                    ctx.W("vector", esc, sc2)
                    for dcm in range(DC):
                        ins = V.tensor_mul(xln[:, dcm * TOWN + t0:dcm * TOWN + t0 + TT],
                                           xln[:, dcm * TOWN + t0:dcm * TOWN + t0 + TT],
                                           rbcst[:, :])
                    xln_ts = ctx.inc(ins, ev)
                    ctx.marks[("xln", ts)] = xln_ts
                    ctx.W("tensor", ev, xln_ts)
                    ctx.W("tensor", dgp, base_gp)
                    for dcm in range(DC):
                        ins = T.matmul(pb[6][0:NROUTED, 0:TT],
                                       rw_sb[:, dcm * NROUTED:(dcm + 1) * NROUTED],
                                       xln[:, dcm * TOWN + t0:dcm * TOWN + t0 + TT],
                                       start=(dcm == 0), stop=(dcm == DC - 1))
                    rt_ms = ctx.inc(ins, et)
                    ctx.W("vector", et, rt_ms)
                    lg_ms = ctx.inc(V.tensor_scalar(l_sb[:, t0:t0 + TT], pb[6][0:NROUTED, 0:TT],
                                                    rb_sb[:, :], None, ALU.add), ev)
                    ctx.W("tensor", ev, lg_ms)
                    for g in range(4):
                        ins = T.transpose(pbT[:, g * 8:(g + 1) * 8],
                                          l_sb[:, t0 + g * 128:t0 + (g + 1) * 128],
                                          ident[0:8, 0:8])
                    tr_ms = ctx.inc(ins, et)
                    prev_gt = tr_ms
                    for g in range(4):
                        lt = pbT[:, g * 8:(g + 1) * 8]
                        ctx.W("vector", et, prev_gt)
                        a1 = ctx.inc(V.reduce_max(topf[:, 0:1], lt, axis=AX, negate=True), ev)
                        ctx.W("vector", ev, a1)
                        a2 = ctx.inc(V.tensor_scalar(topf[:, 8:16], lt, topf[:, 0:1], 0.0,
                                                     ALU.add, ALU.is_lt), ev)
                        ctx.W("vector", ev, a2)
                        a2b = ctx.inc(V.tensor_scalar(topf[:, 40:48], topf[:, 8:16], 1.0, 1.0e30,
                                                      ALU.subtract, ALU.mult), ev)
                        a3a = ctx.inc(V.tensor_mul(topf[:, 16:24], lt, topf[:, 8:16]), ev)
                        ctx.W("vector", ev, a3a)
                        a3 = ctx.inc(V.tensor_add(topf[:, 16:24], topf[:, 16:24],
                                                  topf[:, 40:48]), ev)
                        ctx.W("vector", ev, a3)
                        a4 = ctx.inc(V.reduce_max(topf[:, 1:2], topf[:, 16:24], axis=AX), ev)
                        ctx.W("scalar", ev, a1)
                        ctx.W("scalar", et, tr_ms)
                        e1 = ctx.inc(SC.activation(topf[:, 24:32], lt, ACT.Exp, bias=topf[:, 0:1]),
                                     esc)
                        ctx.W("vector", ev, a4)
                        a5 = ctx.inc(V.tensor_single_scalar(topf[:, 32:40], lt, topf[:, 1:2],
                                                            ALU.is_ge), ev)
                        ctx.W("vector", esc, e1)
                        ctx.W("vector", ev, a5)
                        a6 = ctx.inc(V.tensor_mul(ge_sb[:, :], topf[:, 24:32], topf[:, 32:40]), ev)
                        ctx.W("vector", ev, a6)
                        a7 = ctx.inc(V.reduce_sum(topf[:, 2:3], ge_sb[:, :], axis=AX), ev)
                        ctx.W("vector", ev, a7)
                        a8 = ctx.inc(V.reciprocal(topf[:, 3:4], topf[:, 2:3]), ev)
                        ctx.W("vector", ev, a8)
                        a9 = ctx.inc(V.tensor_scalar(gbf[:, :], ge_sb[:, :], topf[:, 3:4], None,
                                                     ALU.mult), ev)
                        ctx.W("tensor", ev, a9)
                        gt = ctx.inc(T.transpose(pbT[0:NROUTED, 512 + g * 128:512 + (g + 1) * 128],
                                                 gbf[:, :], ident[:, :]), et)
                        prev_gt = gt
                        ctx.W("vector", et, gt)
                        ctx.inc(V.tensor_copy(g_sb[:, t0 + g * 128:t0 + (g + 1) * 128],
                                              pbT[0:NROUTED, 512 + g * 128:512 + (g + 1) * 128]),
                                ev)
                xln_ms = ev.n
                gates_ms = ev.n

                for e in range(NEXP):
                    slot = e % 2
                    ctx.W("tensor", dw[slot], w13_loaded[slot])
                    ctx.W("tensor", ev, xln_ms)
                    evac_done = ctx.marks.get(("s1evac",), {})
                    for ts in range(4):
                        t0 = ts * TT
                        if e < NROUTED:
                            ctx.W("tensor", ev, gates_ms)
                            ctx.W("tensor", ev, ctx.marks.get(("gbread",), 0))
                            gbc = ctx.inc(T.matmul(pb[6][:, 0:TT],
                                                   oneh[:, e * 128:(e + 1) * 128],
                                                   g_sb[:, t0:t0 + TT],
                                                   start=True, stop=True), et)
                        for hc in range(4):
                            par = hc % 2
                            bh1, bh3 = pb[0 + 2 * par], pb[1 + 2 * par]
                            ctx.W("tensor", ev, ctx.marks.get(("s1ev", par), 0))
                            for dcm in range(DC):
                                T.matmul(bh1[:, 0:TT],
                                         w13[slot][:, dcm * HFF + hc * 128:
                                                   dcm * HFF + hc * 128 + 128],
                                         xln[:, dcm * TOWN + t0:dcm * TOWN + t0 + TT],
                                         start=(dcm == 0), stop=(dcm == DC - 1))
                            for dcm in range(DC):
                                ins = T.matmul(bh3[:, 0:TT],
                                               w13[slot][:, DC * HFF + dcm * HFF + hc * 128:
                                                         DC * HFF + dcm * HFF + hc * 128 + 128],
                                               xln[:, dcm * TOWN + t0:dcm * TOWN + t0 + TT],
                                               start=(dcm == 0), stop=(dcm == DC - 1))
                            mm_ms = ctx.inc(ins, et)
                            sl = silu_sb[:, par * TT:(par + 1) * TT]
                            ctx.W("scalar", et, mm_ms)
                            ctx.W("scalar", ev, ctx.marks.get(("s1ev", par), 0))
                            sl_ms = ctx.inc(SC.activation(sl, bh1[:, 0:TT], ACT.Silu), esc)
                            hho = hh[:, (hc * 4 + ts) * TT:(hc * 4 + ts + 1) * TT]
                            ctx.W("vector", esc, sl_ms)
                            if e < NROUTED:
                                tv = ctx.inc(V.tensor_mul(sl, sl, bh3[:, 0:TT]), ev)
                                ctx.W("vector", ev, tv)
                                ctx.W("vector", et, gbc)
                                s1e = ctx.inc(V.tensor_mul(hho, sl, pb[6][:, 0:TT]), ev)
                                ctx.marks[("gbread",)] = s1e
                            else:
                                s1e = ctx.inc(V.tensor_mul(hho, sl, bh3[:, 0:TT]), ev)
                            ctx.marks[("s1ev", par)] = s1e
                    hh_ms = ev.n
                    ctx.W("tensor", dwx[slot], w2_loaded[slot])
                    ctx.W("tensor", ev, hh_ms)
                    for dcm in range(DC):
                        for ts in range(4):
                            t0 = ts * TT
                            par = (dcm * 4 + ts) % 2
                            bank = pb[4 + par]
                            ctx.W("tensor", ev, ctx.marks.get(("s2ev", par), 0))
                            for hc in range(4):
                                ins = T.matmul(bank[:, 0:TT],
                                               w2b[slot][:, (hc * DC + dcm) * 128:
                                                         (hc * DC + dcm) * 128 + 128],
                                               hh[:, (hc * 4 + ts) * TT:(hc * 4 + ts + 1) * TT],
                                               start=(hc == 0), stop=(hc == 3))
                            o_ms = ctx.inc(ins, et)
                            ctx.W("vector", et, o_ms)
                            upd = ctx.inc(V.scalar_tensor_tensor(
                                x_res[:, dcm * TOWN + t0: dcm * TOWN + t0 + TT],
                                bank[:, 0:TT], 0.5,
                                x_res[:, dcm * TOWN + t0: dcm * TOWN + t0 + TT],
                                ALU.mult, ALU.add), ev)
                            ctx.marks[("s2ev", par)] = upd
                    if e + 2 < NEXP:
                        issue_w(e + 2, slot, et.n)
                        w13_loaded[slot] = w13_expect[slot]
                        w2_loaded[slot] = w2_expect[slot]
                ctx.barrier()

        def emit_ln2_and_ag1():
            with contextlib.ExitStack() as es:
                fsb = lambda name, shape, dt: es.enter_context(
                    nc.sbuf_tensor(f"ln2_{name}", shape, dt))
                xbf8 = fsb("xbf8", [128, DC * TT], BF16)
                sq = fsb("sq", [128, TT], BF16)
                xln = fsb("xln", [128, DC * TT], BF16)
                mbc = fsb("mbc", [128, TT], F32)
                ex2 = fsb("ex2", [128, TT], F32)
                rbcst = fsb("rbcst", [128, TT], F32)
                lnt = fsb("lnt", [128, TT], F32)
                for tile in range(NTILES):
                    t0 = tile * TT
                    ctx.W("vector", dsy, base_sy)
                    ctx.W("vector", dxa, dxa.n)
                    for dcm in range(DC):
                        xs = x_res[:, dcm * TOWN + t0: dcm * TOWN + t0 + TT]
                        ins = V.tensor_copy(xbf8[:, dcm * TT:(dcm + 1) * TT], xs)
                    cast_ms = ctx.inc(ins, ev)
                    ctx.W("tensor", ev, cast_ms)
                    for dcm in range(DC):
                        ins = T.matmul(pb[4][:, 0:TT], ones[:, :], xbf8[:, dcm * TT:(dcm + 1) * TT],
                                       start=(dcm == 0), stop=(dcm == DC - 1))
                    ctx.inc(ins, et)
                    sq_read_ms = None
                    for dcm in range(DC):
                        if sq_read_ms is not None:
                            ctx.W("vector", et, sq_read_ms)
                        sqv = ctx.inc(
                            V.tensor_mul(sq[:, :], x_res[:, dcm * TOWN + t0:dcm * TOWN + t0 + TT],
                                         x_res[:, dcm * TOWN + t0:dcm * TOWN + t0 + TT]), ev)
                        ctx.W("tensor", ev, sqv)
                        sq_read_ms = ctx.inc(
                            T.matmul(pb[5][:, 0:TT], ones[:, :], sq[:, :],
                                     start=(dcm == 0), stop=(dcm == DC - 1)), et)
                    sums_ms = sq_read_ms
                    ctx.W("vector", et, sums_ms)
                    V.tensor_scalar(mbc[:, :], pb[4][:, 0:TT], 1.0 / D, None, ALU.mult)
                    mb_ms = ctx.inc(V.tensor_scalar(ex2[:, :], pb[5][:, 0:TT], 1.0 / D, None,
                                                    ALU.mult), ev)
                    ctx.W("vector", ev, mb_ms)
                    v1 = ctx.inc(V.tensor_mul(lnt[:, :], mbc[:, :], mbc[:, :]), ev)
                    ctx.W("vector", ev, v1)
                    v2 = ctx.inc(V.scalar_tensor_tensor(lnt[:, :], ex2[:, :], 1e-5,
                                                        lnt[:, :], ALU.add, ALU.subtract), ev)
                    ctx.W("scalar", ev, v2)
                    scA = ctx.inc(SC.activation(lnt[:, :], lnt[:, :], ACT.Ln), esc)
                    ctx.W("scalar", esc, scA)
                    scB = ctx.inc(SC.activation(rbcst[:, :], lnt[:, :], ACT.Exp, scale=-0.5), esc)
                    for dcm in range(DC):
                        xs = x_res[:, dcm * TOWN + t0: dcm * TOWN + t0 + TT]
                        ins = V.tensor_sub(xbf8[:, dcm * TT:(dcm + 1) * TT], xs, mbc[:, :])
                    sub_ms = ctx.inc(ins, ev)
                    ctx.W("vector", ev, sub_ms)
                    ctx.W("vector", esc, scB)
                    for dcm in range(DC):
                        ins = V.tensor_mul(xln[:, dcm * TT:(dcm + 1) * TT],
                                           xbf8[:, dcm * TT:(dcm + 1) * TT], rbcst[:, :])
                    xln_ms = ctx.inc(ins, ev)
                    ctx.W("sync", ev, xln_ms)
                    SY.dma_start(out=cc_xa_in.ap()[tile], in_=xln[:, :]).then_inc(dxa.h, 16)
                    dxa.n += 16
                ctx.W("gpsimd", dxa, dxa.n)
                GP.collective_compute(
                    "AllGather", ALU.bypass, replica_groups=[list(range(N_CORES))],
                    ins=[cc_xa_in.ap().opt()], outs=[cc_xa_out.ap().opt()],
                ).then_inc(dcc.h, 1)
                dcc.n += 1
                ctx.barrier(skip=("dcc",))

        def emit_attention():
            with contextlib.ExitStack() as es:
                fsb = lambda name, shape, dt: es.enter_context(
                    nc.sbuf_tensor(f"at_{name}", shape, dt))
                wq_sb = fsb("wq", [128, 128], BF16)
                wk_sb = fsb("wk", [128, 128], BF16)
                wv_sb = fsb("wv", [128, 128], BF16)
                wq8 = fsb("wq8", [128, DC * 128], BF16)
                wk8 = fsb("wk8", [128, DC * 128], BF16)
                wv8 = fsb("wv8", [128, DC * 128], BF16)
                xa = [fsb("xa0", [128, DC * 512], BF16), fsb("xa1", [128, DC * 512], BF16)]
                qt_sb = fsb("qt", [128, S], BF16)
                kt_sb = fsb("kt", [128, S], BF16)
                v_tok = fsb("v_tok", [128, 32 * 144], BF16)
                vp_tok = fsb("vp_tok", [64, 64 * 72], BF16)
                vsum = fsb("vsum", [64, 72], BF16)
                ksum = fsb("ksum", [128, 64], BF16)
                kp_sb = fsb("kp", [64, S], BF16)
                qp_sb = fsb("qp", [64, S], BF16)
                o_acc = fsb("o_acc", [65, S], BF16)
                o_stage = fsb("o_stage", [64, S], BF16)
                zbuf = fsb("zbuf", [66, S], F32)
                pexp = fsb("pexp", [128, 256], BF16)
                pt_sb = fsb("pt", [128, 256], BF16)
                esb = fsb("esb", [128, 512], BF16)
                pts = fsb("pts", [128, 512], BF16)
                psum_ex = fsb("psum_ex", [64, 128], BF16)

                for w_dram, w_s in ((wqT, wq8), (wkT, wk8), (wvT, wv8)):
                    GP.dma_start(out=w_s[:, :].rearrange("p (c f) -> p c f", c=DC),
                                 in_=w_dram.rearrange("(c p) f -> p c f", p=128)).then_inc(dgp.h, 16)
                    dgp.n += 16
                wload = dgp.n
                ones_ms = ctx.inc(V.memset(
                    v_tok[:, :].rearrange("p (s w) -> p s w", w=144)[:, :, 64:65], 1.0), ev)
                ones_ms = ctx.inc(V.memset(
                    v_tok[:, :].rearrange("p (s w) -> p s w", w=144)[:, :, 136:137], 1.0), ev)

                for b in range(B):
                    for tg in range(8):
                        slot = tg % 2
                        shard = 2 * b + (tg // 4)
                        ctx.W("sync", dcc, 1)
                        ctx.W("sync", et, ctx.marks.get(("xaw", slot), 0))
                        SY.dma_start(
                            out=xa[slot][:, :],
                            in_=cc_xa_out.ap()[shard, tg % 4],
                        ).then_inc(dx[slot].h, 16)
                        dx[slot].n += 16
                        ctx.W("tensor", dx[slot], dx[slot].n)
                        ctx.W("tensor", dgp, wload)
                        ctx.W("tensor", ev, ctx.marks.get(("qkv_evac", tg % 2), 0))
                        qbQ, qbK, qbV = pb[3 * (tg % 2)], pb[3 * (tg % 2) + 1], pb[3 * (tg % 2) + 2]
                        for dcm in range(DC):
                            T.matmul(qbQ[:, 0:512], wq8[:, dcm * 128:(dcm + 1) * 128],
                                     xa[slot][:, dcm * 512:(dcm + 1) * 512],
                                     start=(dcm == 0), stop=(dcm == DC - 1))
                        for dcm in range(DC):
                            T.matmul(qbK[:, 0:512], wk8[:, dcm * 128:(dcm + 1) * 128],
                                     xa[slot][:, dcm * 512:(dcm + 1) * 512],
                                     start=(dcm == 0), stop=(dcm == DC - 1))
                        for sub in range(4):
                            for dcm in range(DC):
                                ins = T.matmul(qbV[:, sub * 128:(sub + 1) * 128],
                                               xa[slot][:, dcm * 512 + sub * 128:
                                                        dcm * 512 + sub * 128 + 128],
                                               wv8[:, dcm * 128:(dcm + 1) * 128],
                                               start=(dcm == 0), stop=(dcm == DC - 1))
                        mm_ms = ctx.inc(ins, et)
                        ctx.marks[("xaw", slot)] = mm_ms
                        tcol = tg * 512
                        ctx.W("vector", et, mm_ms)
                        ctx.W("vector", ev, ones_ms)
                        V.tensor_scalar(qt_sb[:, tcol:tcol + 512], qbQ[:, 0:512], 0.125,
                                        None, ALU.mult)
                        V.tensor_copy(kt_sb[:, tcol:tcol + 512], qbK[:, 0:512])
                        for sub in range(4):
                            kt_i = tg * 4 + sub
                            V.tensor_copy(v_tok[:, kt_i * 144:kt_i * 144 + 64],
                                          qbV[0:128, sub * 128:sub * 128 + 64])
                            ins = V.tensor_copy(v_tok[:, kt_i * 144 + 72:kt_i * 144 + 136],
                                                qbV[0:128, sub * 128 + 64:sub * 128 + 128])
                        ev_ms = ctx.inc(ins, ev)
                        ctx.marks[("qkv_evac", tg % 2)] = ev_ms
                    qkv_ms = ev.n

                    for h in range(2):
                        p0 = 0 if h == 0 else 64
                        tp = None if h == 0 else (64, 0)
                        mmulT = (mAT, mA0T) if h == 0 else (mBT, mB0T)

                        if h == 0 and ATT["strided"] >= 1:
                            ctx.W("vector", ev, qkv_ms)
                            ctx.W("vector", et, et.n)
                            kpv = ctx.inc(V.tensor_copy(
                                kp_sb[:, :].rearrange("p (r bj) -> p r bj", r=64),
                                kt_sb[0:64, :].rearrange("p (bj r) -> p r bj", bj=64)), ev)
                            qpv = ctx.inc(V.tensor_copy(
                                qp_sb[:, :].rearrange("p (r bj) -> p r bj", r=64),
                                qt_sb[0:64, :].rearrange("p (bj r) -> p r bj", bj=64)), ev)
                            ctx.W("sync", ev, qkv_ms)
                            ctx.W("sync", et, et.n)
                            ctx.W("sync", dvp, dvp.n)
                            SY.dma_start(
                                out=vperm_dr.ap().rearrange("(s p) f -> p s f", p=128)[:, :, 0:65],
                                in_=v_tok[:, :].rearrange("p (s w) -> p s w", w=144)[:, :, 0:65],
                            ).then_inc(dvp.h, 16)
                            SY.dma_start(
                                out=vperm_dr.ap().rearrange("(s p) f -> p s f", p=128)[:, :, 65:130],
                                in_=v_tok[:, :].rearrange("p (s w) -> p s w", w=144)[:, :, 72:137],
                            ).then_inc(dvp.h, 16)
                            dvp.n += 32
                            ctx.W("sync", dvp, dvp.n)
                            SY.dma_start(
                                out=vp_tok[:, :].rearrange("p (s w) -> p s w", w=72)[:, :, 0:65],
                                in_=vperm_dr.ap()
                                    .rearrange("(bj r) f -> bj r f", r=64)[:, :, 0:65],
                            ).then_inc(dvp.h, 16)
                            dvp.n += 16
                        elif h == 1 and ATT["summary"]:
                            ctx.W("vector", ev, qkv_ms)
                            ctx.W("vector", et, et.n)
                            ksv = ctx.inc(V.tensor_copy(
                                ksum[64:128, :],
                                kt_sb[64:128, :].rearrange("p (c r) -> p c r", r=64)[:, :, 63]), ev)
                            ctx.W("sync", ev, qkv_ms)
                            ctx.W("sync", et, et.n)
                            ctx.W("sync", dvp, dvp.n)
                            SY.dma_start(
                                out=vsum[:, 0:65],
                                in_=vperm_dr.ap()
                                    .rearrange("(c u) f -> c u f", u=64)[:, 63, 65:130],
                            ).then_inc(dvs.h, 16)
                            dvs.n += 16

                        if not ATT["passb"]:
                            continue
                        ctx.W("tensor", ev, qkv_ms)
                        if h == 1:
                            ctx.W("tensor", dvs, dvs.n)
                            ctx.W("tensor", ev, ev.n)
                        for qt in range(NQT):
                            kc0 = 0 if qt == 0 else (qt - 1) * 128
                            bank = pb[2 + (qt % 2)]
                            bo = pb[4 + (qt % 2)]
                            ctx.W("tensor", esc, ctx.marks.get(("pbexp", qt % 2), 0))
                            for half in range(2):
                                ins = T.matmul(bank[:, half * 128:(half + 1) * 128],
                                               kt_sb[p0:p0 + 64,
                                                     kc0 + half * 128:kc0 + half * 128 + 128],
                                               qt_sb[p0:p0 + 64, qt * 128:(qt + 1) * 128],
                                               start=True, stop=True, tile_position=tp)
                            if h == 1 and ATT["summary"]:
                                ctx.W("tensor", esc, ctx.marks.get(("sumexS",), 0))
                                ins = T.matmul(pb[6][0:64, 0:128], ksum[64:128, :],
                                               qt_sb[64:128, qt * 128:(qt + 1) * 128],
                                               start=True, stop=True, tile_position=(64, 0))
                            pbm = ctx.inc(ins, et)
                            ctx.W("scalar", et, pbm)
                            ctx.W("scalar", ev, ctx.marks.get(("pbpt", qt % 2), 0))
                            ex1 = ctx.inc(SC.activation(pexp[:, :], bank[:, 0:256], ACT.Exp), esc)
                            ctx.marks[("pbexp", qt % 2)] = ex1
                            ctx.W("vector", esc, ex1)
                            mkT = mmulT[1] if qt == 0 else mmulT[0]
                            pt1 = ctx.inc(V.tensor_mul(pt_sb[:, :], pexp[:, :], mkT[:, :]), ev)
                            ctx.marks[("pbpt", qt % 2)] = pt1
                            if h == 1 and ATT["summary"]:
                                ctx.W("scalar", ev, ctx.marks.get(("sumex",), 0))
                                sx = ctx.inc(SC.activation(psum_ex[:, :], pb[6][0:64, 0:128],
                                                           ACT.Exp), esc)
                                ctx.marks[("sumexS",)] = sx
                                ctx.W("vector", esc, sx)
                                ctx.marks[("sumex",)] = ctx.inc(V.tensor_mul(
                                    psum_ex[:, :], psum_ex[:, :],
                                    msum[:, qt * 128:(qt + 1) * 128]), ev)
                            ctx.W("tensor", ev, pt1)
                            if h == 1 and ATT["summary"]:
                                ctx.W("tensor", ev, ctx.marks.get(("sumex",), 0))
                            ctx.W("tensor", ev, ctx.marks.get(("poev", qt % 2), 0))
                            kt0 = 0 if qt == 0 else qt - 1
                            for half in range(2):
                                ktile = kt0 + half
                                ins = T.matmul(bo[0:65, 0:128],
                                               v_tok[:, ktile * 144 + h * 72:
                                                     ktile * 144 + h * 72 + 65],
                                               pt_sb[:, half * 128:(half + 1) * 128],
                                               start=(half == 0),
                                               stop=(half == 1 and (h == 0 or not ATT["summary"])))
                            if h == 1 and ATT["summary"]:
                                ins = T.matmul(bo[0:65, 0:128], vsum[:, 0:65], psum_ex[:, :],
                                               start=False, stop=True)
                            pv_ms = ctx.inc(ins, et)
                            ctx.W("vector", et, pv_ms)
                            ctx.marks[("poev", qt % 2)] = ctx.inc(
                                V.tensor_copy(o_acc[:, qt * 128:(qt + 1) * 128],
                                              bo[0:65, 0:128]), ev)

                        if h == 0 and ATT["strided"] >= 2:
                            ctx.W("tensor", ev, ev.n)
                            ctx.W("tensor", esc, esc.n)
                            ctx.W("tensor", dvp, dvp.n)
                            for rg in range(8):
                                bs = pb[2 + (rg % 2)]
                                bos = pb[4 + (rg % 2)]
                                ctx.W("tensor", esc, ctx.marks.get(("strex", rg % 2), 0))
                                ctx.W("tensor", ev, ctx.marks.get(("stradd", rg % 2), 0))
                                for rr in range(8):
                                    r = rg * 8 + rr
                                    ins = T.matmul(bs[0:64, rr * 64:(rr + 1) * 64],
                                                   kp_sb[:, r * 64:(r + 1) * 64],
                                                   qp_sb[:, r * 64:(r + 1) * 64],
                                                   start=True, stop=True)
                                sm = ctx.inc(ins, et)
                                ctx.W("scalar", et, sm)
                                ctx.W("scalar", ev, ctx.marks.get(("strpt", rg % 2), 0))
                                se = ctx.inc(SC.activation(esb[0:64, :], bs[0:64, 0:512],
                                                           ACT.Exp), esc)
                                ctx.marks[("strex", rg % 2)] = se
                                ctx.W("vector", esc, se)
                                sp = ctx.inc(V.tensor_mul(pts[0:64, :], esb[0:64, :],
                                                          mstr[0:64, :]), ev)
                                ctx.marks[("strpt", rg % 2)] = sp
                                ctx.W("tensor", ev, sp)
                                if ATT["strided"] < 3:
                                    ctx.marks[("stradd", rg % 2)] = sp
                                    continue
                                for rr in range(8):
                                    r = rg * 8 + rr
                                    ins = T.matmul(bos[0:65, rr * 64:(rr + 1) * 64],
                                                   vp_tok[:, r * 72:r * 72 + 65],
                                                   pts[0:64, rr * 64:(rr + 1) * 64],
                                                   start=True, stop=True)
                                spv = ctx.inc(ins, et)
                                ctx.W("vector", et, spv)
                                oacc_v = o_acc[:, :].rearrange("p (qi r) -> p r qi",
                                                               r=64)[:, rg * 8:(rg + 1) * 8, :]
                                ctx.marks[("stradd", rg % 2)] = ctx.inc(V.tensor_add(
                                    oacc_v, oacc_v,
                                    bos[0:65, 0:512].rearrange("p (rr qi) -> p rr qi", rr=8)), ev)

                        if not ATT["norm"]:
                            continue
                        ctx.W("vector", ev, ev.n)
                        nz0 = ctx.inc(V.tensor_copy(zbuf[64:65, :], o_acc[64:65, :]), ev)
                        ctx.W("scalar", ev, nz0)
                        z1 = ctx.inc(SC.activation(zbuf[64:65, :], zbuf[64:65, :], ACT.Ln), esc)
                        ctx.W("scalar", esc, z1)
                        z2 = ctx.inc(SC.activation(zbuf[64:65, :], zbuf[64:65, :], ACT.Exp,
                                                   scale=-1.0), esc)
                        ctx.W("tensor", esc, z2)
                        ctx.W("vector", dco, dco.n)
                        for g in range(8):
                            bz = pb[2 + (g % 2)]
                            ctx.W("tensor", ev, ctx.marks.get(("zev", g % 2), 0))
                            zm = ctx.inc(T.matmul(bz[0:64, 0:512], onesf[64:65, 0:64],
                                                  zbuf[64:65, g * 512:(g + 1) * 512],
                                                  start=True, stop=True), et)
                            ctx.W("vector", et, zm)
                            ctx.marks[("zev", g % 2)] = ctx.inc(
                                V.tensor_mul(o_stage[:, g * 512:(g + 1) * 512],
                                             o_acc[0:64, g * 512:(g + 1) * 512],
                                             bz[0:64, 0:512]), ev)
                        ctx.W("sync", ev, ev.n)
                        SY.dma_start(out=cc_o_in.ap()[b, h], in_=o_stage[:, :]).then_inc(dco.h, 16)
                        dco.n += 16
                    if ATT["ag2"]:
                        ctx.W("gpsimd", dco, dco.n)
                        GP.collective_compute(
                            "AllGather", ALU.bypass, replica_groups=[list(range(N_CORES))],
                            ins=[cc_o_in.ap()[b].opt()], outs=[cc_o_out.ap()[b].opt()],
                        ).then_inc(dcc.h, 1)
                        dcc.n += 1
                    ctx.barrier(engines=("tensor", "vector", "scalar"), skip=("dcc",))
                ctx.barrier(skip=("dcc",))

        def emit_wo():
            with contextlib.ExitStack() as es:
                fsb = lambda name, shape, dt: es.enter_context(
                    nc.sbuf_tensor(f"wo_{name}", shape, dt))
                wo_sb = fsb("wo", [128, DC * D], BF16)
                og = [fsb("og0", [128, DC * 512], BF16), fsb("og1", [128, DC * 512], BF16)]
                GP.dma_start(out=wo_sb[:, :].rearrange("p (c f) -> p c f", c=DC),
                             in_=woT.rearrange("(c p) f -> p c f", p=128)).then_inc(dgp.h, 16)
                dgp.n += 16
                wo_ld = dgp.n
                pid = SY.partition_id()
                row0 = SY.compute_val((pid // 2) * (N_CORES * 2 * 64))
                colb = SY.compute_val((pid % 2) * TOWN)
                flat = cc_o_out.ap().rearrange("b n h d t -> (b n h d) t")
                ctx.W("sync", dcc, dcc.n)
                for ts in range(4):
                    slot = ts % 2
                    ctx.W("sync", et, ctx.marks.get(("ogw", slot), 0))
                    col = SY.compute_val(colb + ts * 512)
                    SY.dma_start(
                        out=og[slot][:, :].rearrange("p (c t) -> p c t", c=DC),
                        in_=flat[bass.ds(row0, 1024), bass.ds(col, 512)]
                            .rearrange("(c p) t -> p c t", p=128),
                    ).then_inc(dol[slot].h, 16)
                    dol[slot].n += 16
                    ctx.W("tensor", dol[slot], dol[slot].n)
                    ctx.W("tensor", dgp, wo_ld)
                    t0 = ts * 512
                    for oc in range(DC):
                        bank = pb[4 + (oc % 2)]
                        ctx.W("tensor", ev, ctx.marks.get(("woev", oc % 2), 0))
                        for ic in range(DC):
                            ins = T.matmul(bank[:, 0:512],
                                           wo_sb[:, ic * D + oc * 128:ic * D + oc * 128 + 128],
                                           og[slot][:, ic * 512:(ic + 1) * 512],
                                           start=(ic == 0), stop=(ic == DC - 1))
                        wm = ctx.inc(ins, et)
                        ctx.marks[("ogw", slot)] = wm
                        ctx.W("vector", et, wm)
                        ctx.marks[("woev", oc % 2)] = ctx.inc(V.tensor_add(
                            x_res[:, oc * TOWN + t0: oc * TOWN + t0 + 512],
                            x_res[:, oc * TOWN + t0: oc * TOWN + t0 + 512],
                            bank[:, 0:512]), ev)
                ctx.barrier()

        if PHASES.get("ffn1"):
            emit_ffn("f1")
        if PHASES.get("attn"):
            emit_ln2_and_ag1()
            emit_attention()
            if ATT["wo"]:
                emit_wo()
        if PHASES.get("ffn2"):
            emit_ffn("f2")

        ctx.barrier()
        SY.dma_start(out=yT.rearrange("(c p) t -> p c t", p=128),
                     in_=x_res[:, :].rearrange("p (c t) -> p c t", c=DC)).then_inc(dst.h, 16)
        dst.n += 16
        ctx.W("sync", dst, dst.n)

    return nc


_NC = None


def _in_maps(inputs):
    x = np.asarray(inputs["L0_attn_x"], dtype=np.float32).reshape(B * S, D)
    Wq = np.asarray(inputs["L0_attn_Wq"], dtype=np.float32)
    Wk = np.asarray(inputs["L0_attn_Wk"], dtype=np.float32)
    Wv = np.asarray(inputs["L0_attn_Wv"], dtype=np.float32)
    Wo = np.asarray(inputs["L0_attn_Wo"], dtype=np.float32)
    rw_ = np.ascontiguousarray(np.asarray(inputs["L0_ffn_router_w"], dtype=np.float32))
    rb_ = np.asarray(inputs["L0_ffn_router_b"], dtype=np.float32).reshape(NROUTED, 1)
    w1_ = np.asarray(inputs["L0_ffn_routed_w1"], dtype=np.float32)
    w3_ = np.asarray(inputs["L0_ffn_routed_w3"], dtype=np.float32)
    w2_ = np.asarray(inputs["L0_ffn_routed_w2"], dtype=np.float32)
    s1_ = np.asarray(inputs["L0_ffn_shared_w1"], dtype=np.float32)
    s3_ = np.asarray(inputs["L0_ffn_shared_w3"], dtype=np.float32)
    s2_ = np.asarray(inputs["L0_ffn_shared_w2"], dtype=np.float32)
    def _w13_part(wa):
        return wa.reshape(-1, DC, 128, HFF).transpose(0, 2, 1, 3).reshape(-1, 128, DC * HFF)
    w13cat = np.concatenate([
        np.concatenate([_w13_part(w1_), _w13_part(w3_)], axis=2),
        np.concatenate([_w13_part(s1_), _w13_part(s3_)], axis=2)], axis=0)
    w13cat = np.ascontiguousarray(w13cat)
    def _w2_part(wa):
        return wa.reshape(-1, 4, 128, DC, 128).transpose(0, 2, 1, 3, 4)
    w2cat = np.concatenate([_w2_part(w2_), _w2_part(s2_)], axis=0)
    w2cat = np.ascontiguousarray(w2cat.reshape(NEXP, 128, 4 * DC * 128))
    WqT, WkT, WvT, WoT = Wq.T, Wk.T, Wv.T, Wo.T
    perm = np.concatenate([np.r_[64 * c:64 * c + 64, 64 * (c + 8):64 * (c + 8) + 64]
                           for c in range(N_CORES)])
    WoTp = np.ascontiguousarray(WoT[perm, :])
    maps = []
    for c in range(N_CORES):
        cols = np.r_[64 * c:64 * c + 64, 64 * (c + 8):64 * (c + 8) + 64]
        maps.append({
            "xT": np.ascontiguousarray(x[c * TOWN:(c + 1) * TOWN].T),
            "wqT": np.ascontiguousarray(WqT[:, cols]),
            "wkT": np.ascontiguousarray(WkT[:, cols]),
            "wvT": np.ascontiguousarray(WvT[:, cols]),
            "woT": WoTp,
            "rw": rw_, "rbc": rb_, "w13d": w13cat, "w2d": w2cat,
        })
    return maps


def kernel(**inputs):
    global _NC
    if _NC is None:
        _NC = build_nc()
    maps = _in_maps(inputs)
    res = run_bass_kernel_spmd(_NC, maps, core_ids=list(range(N_CORES)))
    out = np.empty((B * S, D), dtype=np.float32)
    for c in range(N_CORES):
        out[c * TOWN:(c + 1) * TOWN] = np.asarray(res.results[c]["yT"]).T
    return out.reshape(B, S, D)


# version 09
# speedup vs baseline: 1.2190x; 1.0092x over previous
import contextlib
import numpy as np
import ml_dtypes
import concourse.bass as bass
import concourse.mybir as mybir
from concourse.bass_utils import run_bass_kernel_spmd

F32 = mybir.dt.float32
BF16 = mybir.dt.bfloat16
ALU = mybir.AluOpType
ACT = mybir.ActivationFunctionType
AX = mybir.AxisListType.X
BF = ml_dtypes.bfloat16

N_CORES = 8
D = 1024
DC = 8
HFF = 512
NEXP = 10
NROUTED = 8
B = 4
S = 4096
TOWN = 2048
TT = 512
NTILES = TOWN // TT
NQT = S // 128
NEG = -1.0e9

PHASES = {"ffn1": True, "attn": True, "ffn2": True}
DEBUG = False
ATT = {"qkv": True, "passa": True, "passb": True, "summary": True,
       "strided": 3, "norm": True, "ag2": True, "wo": True}


def _masks_np():
    r = np.arange(128)[:, None]
    c = np.arange(256)[None, :]
    okA = (c - r >= 65) & (c - r <= 128)
    okA0 = (r - c >= 0) & (r - c < 64)
    cc = c - 128
    okB = (c >= 128) & (cc // 64 == r // 64) & (cc <= r)
    okB0 = (c < 128) & (c // 64 == r // 64) & (c <= r)

    def addm(ok):
        return np.where(ok, 0.0, NEG).astype(np.float32)

    def mulT(ok):
        m = np.zeros((128, 256), dtype=BF)
        m[:, 0:128] = ok[:, 0:128].T.astype(BF)
        m[:, 128:256] = ok[:, 128:256].T.astype(BF)
        return m

    msum = np.zeros((64, NQT * 128), dtype=BF)
    cc64 = np.arange(64)[:, None]
    rr = np.arange(128)[None, :]
    for qt in range(NQT):
        msum[:, qt * 128:(qt + 1) * 128] = (cc64 < 2 * qt + (rr >= 64)).astype(BF)
    bj = np.arange(64)[:, None]
    qi = np.arange(64)[None, :]
    m1 = (bj < qi).astype(BF)
    mstr = np.zeros((128, 512), dtype=BF)
    for g in range(8):
        mstr[0:64, g * 64:(g + 1) * 64] = m1
        mstr[64:128, g * 64:(g + 1) * 64] = m1
    return (addm(okA), addm(okA0), addm(okB), addm(okB0),
            mulT(okA), mulT(okA0), mulT(okB), mulT(okB0), msum, mstr)


class Ctx:
    def __init__(self, nc, es):
        self.nc = nc
        self.es = es
        self.sems = {}
        self.lastw = {}
        self.marks = {}

    def sem(self, name):
        if name not in self.sems:
            h = self.es.enter_context(self.nc.semaphore(name))
            s = type("S", (), {})()
            s.h, s.n, s.name = h, 0, name
            self.sems[name] = s
        return self.sems[name]

    def W(self, eng, s, val):
        if val is None or val <= 0:
            return
        key = (eng, s.name)
        if self.lastw.get(key, -1) >= val:
            return
        getattr(self.nc, eng).wait_ge(s.h, val)
        self.lastw[key] = val

    def inc(self, instr, s, k=1):
        instr.then_inc(s.h, k)
        s.n += k
        return s.n

    def barrier(self, engines=("tensor", "vector", "scalar", "sync", "gpsimd"), skip=()):
        for e in engines:
            for s in self.sems.values():
                if s.name in skip:
                    continue
                self.W(e, s, s.n)


def build_nc():
    nc = bass.Bass("TRN2", target_bir_lowering=False)
    T, V, SC, SY, GP = nc.tensor, nc.vector, nc.scalar, nc.sync, nc.gpsimd

    xT = nc.dram_tensor("xT", [D, TOWN], F32, kind="ExternalInput").ap()
    wqT = nc.dram_tensor("wqT", [D, 128], F32, kind="ExternalInput").ap()
    wkT = nc.dram_tensor("wkT", [D, 128], F32, kind="ExternalInput").ap()
    wvT = nc.dram_tensor("wvT", [D, 128], F32, kind="ExternalInput").ap()
    woT = nc.dram_tensor("woT", [D, D], F32, kind="ExternalInput").ap()
    rw = nc.dram_tensor("rw", [D, NROUTED], F32, kind="ExternalInput").ap()
    rbc = nc.dram_tensor("rbc", [NROUTED, 1], F32, kind="ExternalInput").ap()
    w13d = nc.dram_tensor("w13d", [NEXP, 128, 2 * DC * HFF], F32, kind="ExternalInput").ap()
    w2d = nc.dram_tensor("w2d", [NEXP, 128, 4 * DC * 128], F32, kind="ExternalInput").ap()
    yT = nc.dram_tensor("yT", [D, TOWN], F32, kind="ExternalOutput").ap()

    if DEBUG:
        dbg_t = {
            "dbg_xln": nc.dram_tensor("dbg_xln", [128, DC * TT], BF16),
            "dbg_l": nc.dram_tensor("dbg_l", [NROUTED, TT], BF16),
            "dbg_g": nc.dram_tensor("dbg_g", [NROUTED, TT], BF16),
            "dbg_hh": nc.dram_tensor("dbg_hh", [128, NEXP * 4 * TT], BF16),
        }
    cc_xa_in = nc.dram_tensor("cc_xa_in", [NTILES, 128, DC * TT], BF16)
    cc_xa_out = nc.dram_tensor("cc_xa_out", [N_CORES, NTILES, 128, DC * TT], BF16,
                               addr_space="Shared")
    vperm_dr = nc.dram_tensor("vperm_dr", [S, 130], BF16)
    cc_o_in = nc.dram_tensor("cc_o_in", [B, 2, 64, S], BF16)
    cc_o_out = nc.dram_tensor("cc_o_out", [B, N_CORES, 2, 64, S], BF16, addr_space="Shared")

    (mA_np, mA0_np, mB_np, mB0_np, mAT_np, mA0T_np, mBT_np, mB0T_np,
     msum_np, mstr_np) = _masks_np()
    oneh_np = np.zeros((NROUTED, NROUTED * 128), dtype=BF)
    for e in range(NROUTED):
        oneh_np[e, e * 128:(e + 1) * 128] = 1.0
    mAT2 = np.concatenate([mAT_np, mAT_np], axis=1)
    mAT2_0 = np.concatenate([mA0T_np, mAT_np], axis=1)
    mBT2 = np.concatenate([mBT_np, mBT_np], axis=1)
    mBT2_0 = np.concatenate([mB0T_np, mBT_np], axis=1)
    consts = [
        ("c_ident", np.eye(128, dtype=BF)), ("c_ones", np.ones((128, 128), dtype=BF)),
        ("c_mAT2", mAT2), ("c_mAT2_0", mAT2_0), ("c_mBT2", mBT2), ("c_mBT2_0", mBT2_0),
        ("c_oneh", oneh_np),
        ("c_onesf", np.ones((128, 64), dtype=np.float32)),
        ("c_mA", mA_np), ("c_mA0", mA0_np), ("c_mB", mB_np), ("c_mB0", mB0_np),
        ("c_mAT", mAT_np), ("c_mA0T", mA0T_np), ("c_mBT", mBT_np), ("c_mB0T", mB0T_np),
        ("c_msum", msum_np), ("c_mstr", mstr_np),
    ]
    c_handles = {n: nc.inline_tensor(a, name=n) for n, a in consts}
    c_shapes = {n: a for n, a in consts}

    es_top = contextlib.ExitStack()
    with es_top:
        ctx = Ctx(nc, es_top)
        sb = lambda name, shape, dt: es_top.enter_context(nc.sbuf_tensor(name, shape, dt))

        x_res = sb("x_res", [128, DC * TOWN], F32)
        csb = {}
        for n, a in consts:
            dt = BF16 if a.dtype == BF else F32
            csb[n] = sb(n[2:], list(a.shape), dt)
        ident, ones = csb["c_ident"], csb["c_ones"]
        mA, mA0, mB, mB0 = csb["c_mA"], csb["c_mA0"], csb["c_mB"], csb["c_mB0"]
        mAT, mA0T, mBT, mB0T = csb["c_mAT"], csb["c_mA0T"], csb["c_mBT"], csb["c_mB0T"]
        msum, mstr = csb["c_msum"], csb["c_mstr"]
        oneh = csb["c_oneh"]
        onesf = csb["c_onesf"]
        mAT2c, mAT2c0, mBT2c, mBT2c0 = (csb["c_mAT2"], csb["c_mAT2_0"],
                                        csb["c_mBT2"], csb["c_mBT2_0"])
        rw_sb = sb("rw_sb", [128, DC * NROUTED], BF16)
        rb_sb = sb("rb_sb", [NROUTED, 1], F32)

        pb = [es_top.enter_context(nc.psum_tensor(f"pb{i}", [128, 512], F32)) for i in range(7)]
        pbT = es_top.enter_context(nc.psum_tensor("pbT", [128, 1024], BF16))

        et, ev, esc = ctx.sem("et"), ctx.sem("ev"), ctx.sem("esc")
        dsy = ctx.sem("dsy")
        dgp = ctx.sem("dgp")
        dw = [ctx.sem("dw0"), ctx.sem("dw1")]
        dwx = [ctx.sem("dwa"), ctx.sem("dwb")]
        dx = [ctx.sem("dx0"), ctx.sem("dx1")]
        dmr = ctx.sem("dmr")
        dvp = ctx.sem("dvp")
        dvs = ctx.sem("dvs")
        dco = ctx.sem("dco")
        dxa = ctx.sem("dxa")
        dcc = ctx.sem("dcc")
        dol = [ctx.sem("dol0"), ctx.sem("dol1")]
        dst = ctx.sem("dst")

        SY.dma_start(out=x_res[:, :].rearrange("p (c t) -> p c t", c=DC),
                     in_=xT.rearrange("(c p) t -> p c t", p=128)).then_inc(dsy.h, 16)
        dsy.n += 16
        for n, _ in consts:
            SY.dma_start(out=csb[n][:, :], in_=c_handles[n].ap()).then_inc(dsy.h, 16)
            dsy.n += 16
        SY.dma_start(out=rb_sb[:, :], in_=rbc).then_inc(dsy.h, 16)
        dsy.n += 16
        GP.dma_start(out=rw_sb[:, :].rearrange("p (c e) -> p c e", c=DC),
                     in_=rw.rearrange("(c p) e -> p c e", p=128)).then_inc(dgp.h, 16)
        dgp.n += 16
        base_sy = dsy.n
        base_gp = dgp.n

        def emit_ffn(tag):
            with contextlib.ExitStack() as es:
                fsb = lambda name, shape, dt: es.enter_context(
                    nc.sbuf_tensor(f"{tag}_{name}", shape, dt))
                xln = fsb("xln", [128, DC * TOWN], BF16)
                sq = fsb("sq", [128, TT], BF16)
                mbc = fsb("mbc", [128, TT], F32)
                ex2 = fsb("ex2", [128, TT], F32)
                rbcst = fsb("rbcst", [128, TT], F32)
                lnt = fsb("lnt", [128, TT], F32)
                l_sb = fsb("l_sb", [NROUTED, TOWN], BF16)
                topf = fsb("topf", [128, 48], F32)
                ge_sb = fsb("ge_sb", [128, NROUTED], F32)
                gbf = fsb("gbf", [128, NROUTED], BF16)
                g_sb = fsb("g_sb", [NROUTED, TOWN], BF16)
                silu_sb = fsb("silu_sb", [128, 2 * TT], BF16)
                hh = fsb("hh", [128, 16 * TT], BF16)
                w13 = [fsb("w13_0", [128, 2 * DC * HFF], BF16),
                       fsb("w13_1", [128, 2 * DC * HFF], BF16)]
                w2b = [fsb("w2b_0", [128, 4 * DC * 128], BF16),
                       fsb("w2b_1", [128, 4 * DC * 128], BF16)]

                w13_expect = [0, 0]
                w2_expect = [0, 0]

                def issue_w(e, slot, wait_et):
                    ctx.W("gpsimd", et, wait_et)
                    GP.dma_start(out=w13[slot][:, :], in_=w13d[e]).then_inc(dw[slot].h, 16)
                    dw[slot].n += 16
                    w13_expect[slot] = dw[slot].n
                    GP.dma_start(out=w2b[slot][:, :], in_=w2d[e]).then_inc(dwx[slot].h, 16)
                    dwx[slot].n += 16
                    w2_expect[slot] = dwx[slot].n

                issue_w(0, 0, 0)
                issue_w(1, 1, 0)
                w13_loaded = {0: w13_expect[0], 1: w13_expect[1]}
                w2_loaded = {0: w2_expect[0], 1: w2_expect[1]}

                ctx.W("vector", dsy, base_sy)
                ctx.W("vector", et, et.n)
                for ts in range(4):
                    t0 = ts * TT
                    for dcm in range(DC):
                        xs = x_res[:, dcm * TOWN + t0: dcm * TOWN + t0 + TT]
                        ins = V.tensor_copy(xln[:, dcm * TOWN + t0:dcm * TOWN + t0 + TT], xs)
                    cast_ms = ctx.inc(ins, ev)
                    ctx.W("tensor", ev, cast_ms)
                    ctx.W("tensor", ev, ctx.marks.get(("lnap", ts), 0))
                    for dcm in range(DC):
                        ins = T.matmul(pb[4][:, 0:TT], ones[:, :],
                                       xln[:, dcm * TOWN + t0:dcm * TOWN + t0 + TT],
                                       start=(dcm == 0), stop=(dcm == DC - 1))
                    ctx.inc(ins, et)
                    sq_read_ms = None
                    for dcm in range(DC):
                        if sq_read_ms is not None:
                            ctx.W("vector", et, sq_read_ms)
                        sqv = ctx.inc(
                            V.tensor_mul(sq[:, :], x_res[:, dcm * TOWN + t0:dcm * TOWN + t0 + TT],
                                         x_res[:, dcm * TOWN + t0:dcm * TOWN + t0 + TT]), ev)
                        ctx.W("tensor", ev, sqv)
                        sq_read_ms = ctx.inc(
                            T.matmul(pb[5][:, 0:TT], ones[:, :], sq[:, :],
                                     start=(dcm == 0), stop=(dcm == DC - 1)), et)
                    sums_ms = sq_read_ms
                    ctx.W("vector", et, sums_ms)
                    V.tensor_scalar(mbc[:, :], pb[4][:, 0:TT], 1.0 / D, None, ALU.mult)
                    mb_ms = ctx.inc(V.tensor_scalar(ex2[:, :], pb[5][:, 0:TT], 1.0 / D, None,
                                                    ALU.mult), ev)
                    ctx.W("vector", ev, mb_ms)
                    v1 = ctx.inc(V.tensor_mul(lnt[:, :], mbc[:, :], mbc[:, :]), ev)
                    ctx.W("vector", ev, v1)
                    v2 = ctx.inc(V.scalar_tensor_tensor(lnt[:, :], ex2[:, :], 1e-5,
                                                        lnt[:, :], ALU.add, ALU.subtract), ev)
                    ctx.W("scalar", ev, v2)
                    sc1 = ctx.inc(SC.activation(lnt[:, :], lnt[:, :], ACT.Ln), esc)
                    ctx.W("scalar", esc, sc1)
                    sc2 = ctx.inc(SC.activation(rbcst[:, :], lnt[:, :], ACT.Exp, scale=-0.5), esc)
                    for dcm in range(DC):
                        xs = x_res[:, dcm * TOWN + t0: dcm * TOWN + t0 + TT]
                        ins = V.tensor_sub(xln[:, dcm * TOWN + t0:dcm * TOWN + t0 + TT],
                                           xs, mbc[:, :])
                    sub_ms = ctx.inc(ins, ev)
                    ctx.W("vector", ev, sub_ms)
                    ctx.W("vector", esc, sc2)
                    for dcm in range(DC):
                        ins = V.tensor_mul(xln[:, dcm * TOWN + t0:dcm * TOWN + t0 + TT],
                                           xln[:, dcm * TOWN + t0:dcm * TOWN + t0 + TT],
                                           rbcst[:, :])
                    xln_ts = ctx.inc(ins, ev)
                    ctx.marks[("xln", ts)] = xln_ts
                    ctx.W("tensor", ev, xln_ts)
                    ctx.W("tensor", dgp, base_gp)
                    for dcm in range(DC):
                        ins = T.matmul(pb[6][0:NROUTED, 0:TT],
                                       rw_sb[:, dcm * NROUTED:(dcm + 1) * NROUTED],
                                       xln[:, dcm * TOWN + t0:dcm * TOWN + t0 + TT],
                                       start=(dcm == 0), stop=(dcm == DC - 1))
                    rt_ms = ctx.inc(ins, et)
                    ctx.W("vector", et, rt_ms)
                    lg_ms = ctx.inc(V.tensor_scalar(l_sb[:, t0:t0 + TT], pb[6][0:NROUTED, 0:TT],
                                                    rb_sb[:, :], None, ALU.add), ev)
                    ctx.W("tensor", ev, lg_ms)
                    for g in range(4):
                        ins = T.transpose(pbT[:, g * 8:(g + 1) * 8],
                                          l_sb[:, t0 + g * 128:t0 + (g + 1) * 128],
                                          ident[0:8, 0:8])
                    tr_ms = ctx.inc(ins, et)
                    prev_gt = tr_ms
                    for g in range(4):
                        lt = pbT[:, g * 8:(g + 1) * 8]
                        ctx.W("vector", et, prev_gt)
                        a1 = ctx.inc(V.reduce_max(topf[:, 0:1], lt, axis=AX, negate=True), ev)
                        ctx.W("vector", ev, a1)
                        a2 = ctx.inc(V.tensor_scalar(topf[:, 8:16], lt, topf[:, 0:1], 0.0,
                                                     ALU.add, ALU.is_lt), ev)
                        ctx.W("vector", ev, a2)
                        a2b = ctx.inc(V.tensor_scalar(topf[:, 40:48], topf[:, 8:16], 1.0, 1.0e30,
                                                      ALU.subtract, ALU.mult), ev)
                        a3a = ctx.inc(V.tensor_mul(topf[:, 16:24], lt, topf[:, 8:16]), ev)
                        ctx.W("vector", ev, a3a)
                        a3 = ctx.inc(V.tensor_add(topf[:, 16:24], topf[:, 16:24],
                                                  topf[:, 40:48]), ev)
                        ctx.W("vector", ev, a3)
                        a4 = ctx.inc(V.reduce_max(topf[:, 1:2], topf[:, 16:24], axis=AX), ev)
                        ctx.W("scalar", ev, a1)
                        ctx.W("scalar", et, tr_ms)
                        e1 = ctx.inc(SC.activation(topf[:, 24:32], lt, ACT.Exp, bias=topf[:, 0:1]),
                                     esc)
                        ctx.W("vector", ev, a4)
                        a5 = ctx.inc(V.tensor_single_scalar(topf[:, 32:40], lt, topf[:, 1:2],
                                                            ALU.is_ge), ev)
                        ctx.W("vector", esc, e1)
                        ctx.W("vector", ev, a5)
                        a6 = ctx.inc(V.tensor_mul(ge_sb[:, :], topf[:, 24:32], topf[:, 32:40]), ev)
                        ctx.W("vector", ev, a6)
                        a7 = ctx.inc(V.reduce_sum(topf[:, 2:3], ge_sb[:, :], axis=AX), ev)
                        ctx.W("vector", ev, a7)
                        a8 = ctx.inc(V.reciprocal(topf[:, 3:4], topf[:, 2:3]), ev)
                        ctx.W("vector", ev, a8)
                        a9 = ctx.inc(V.tensor_scalar(gbf[:, :], ge_sb[:, :], topf[:, 3:4], None,
                                                     ALU.mult), ev)
                        ctx.W("tensor", ev, a9)
                        gt = ctx.inc(T.transpose(pbT[0:NROUTED, 512 + g * 128:512 + (g + 1) * 128],
                                                 gbf[:, :], ident[:, :]), et)
                        prev_gt = gt
                        ctx.W("vector", et, gt)
                        ctx.inc(V.tensor_copy(g_sb[:, t0 + g * 128:t0 + (g + 1) * 128],
                                              pbT[0:NROUTED, 512 + g * 128:512 + (g + 1) * 128]),
                                ev)
                xln_ms = ev.n
                gates_ms = ev.n

                for e in range(NEXP):
                    slot = e % 2
                    ctx.W("tensor", dw[slot], w13_loaded[slot])
                    ctx.W("tensor", ev, xln_ms)
                    evac_done = ctx.marks.get(("s1evac",), {})
                    for ts in range(4):
                        t0 = ts * TT
                        if e < NROUTED:
                            ctx.W("tensor", ev, gates_ms)
                            ctx.W("tensor", ev, ctx.marks.get(("gbread",), 0))
                            gbc = ctx.inc(T.matmul(pb[6][:, 0:TT],
                                                   oneh[:, e * 128:(e + 1) * 128],
                                                   g_sb[:, t0:t0 + TT],
                                                   start=True, stop=True), et)
                        for hc in range(4):
                            par = hc % 2
                            bh1, bh3 = pb[0 + 2 * par], pb[1 + 2 * par]
                            ctx.W("tensor", ev, ctx.marks.get(("s1ev", par), 0))
                            for dcm in range(DC):
                                T.matmul(bh1[:, 0:TT],
                                         w13[slot][:, dcm * HFF + hc * 128:
                                                   dcm * HFF + hc * 128 + 128],
                                         xln[:, dcm * TOWN + t0:dcm * TOWN + t0 + TT],
                                         start=(dcm == 0), stop=(dcm == DC - 1))
                            for dcm in range(DC):
                                ins = T.matmul(bh3[:, 0:TT],
                                               w13[slot][:, DC * HFF + dcm * HFF + hc * 128:
                                                         DC * HFF + dcm * HFF + hc * 128 + 128],
                                               xln[:, dcm * TOWN + t0:dcm * TOWN + t0 + TT],
                                               start=(dcm == 0), stop=(dcm == DC - 1))
                            mm_ms = ctx.inc(ins, et)
                            sl = silu_sb[:, par * TT:(par + 1) * TT]
                            ctx.W("scalar", et, mm_ms)
                            ctx.W("scalar", ev, ctx.marks.get(("s1ev", par), 0))
                            sl_ms = ctx.inc(SC.activation(sl, bh1[:, 0:TT], ACT.Silu), esc)
                            hho = hh[:, (hc * 4 + ts) * TT:(hc * 4 + ts + 1) * TT]
                            ctx.W("vector", esc, sl_ms)
                            if e < NROUTED:
                                tv = ctx.inc(V.tensor_mul(sl, sl, bh3[:, 0:TT]), ev)
                                ctx.W("vector", ev, tv)
                                ctx.W("vector", et, gbc)
                                s1e = ctx.inc(V.tensor_mul(hho, sl, pb[6][:, 0:TT]), ev)
                                ctx.marks[("gbread",)] = s1e
                            else:
                                s1e = ctx.inc(V.tensor_mul(hho, sl, bh3[:, 0:TT]), ev)
                            ctx.marks[("s1ev", par)] = s1e
                    hh_ms = ev.n
                    ctx.W("tensor", dwx[slot], w2_loaded[slot])
                    ctx.W("tensor", ev, hh_ms)
                    for dcm in range(DC):
                        for ts in range(4):
                            t0 = ts * TT
                            par = (dcm * 4 + ts) % 2
                            bank = pb[4 + par]
                            ctx.W("tensor", ev, ctx.marks.get(("s2ev", par), 0))
                            for hc in range(4):
                                ins = T.matmul(bank[:, 0:TT],
                                               w2b[slot][:, (hc * DC + dcm) * 128:
                                                         (hc * DC + dcm) * 128 + 128],
                                               hh[:, (hc * 4 + ts) * TT:(hc * 4 + ts + 1) * TT],
                                               start=(hc == 0), stop=(hc == 3))
                            o_ms = ctx.inc(ins, et)
                            ctx.W("vector", et, o_ms)
                            upd = ctx.inc(V.scalar_tensor_tensor(
                                x_res[:, dcm * TOWN + t0: dcm * TOWN + t0 + TT],
                                bank[:, 0:TT], 0.5,
                                x_res[:, dcm * TOWN + t0: dcm * TOWN + t0 + TT],
                                ALU.mult, ALU.add), ev)
                            ctx.marks[("s2ev", par)] = upd
                    if e + 2 < NEXP:
                        issue_w(e + 2, slot, et.n)
                        w13_loaded[slot] = w13_expect[slot]
                        w2_loaded[slot] = w2_expect[slot]
                ctx.barrier()

        def emit_ln2_and_ag1():
            with contextlib.ExitStack() as es:
                fsb = lambda name, shape, dt: es.enter_context(
                    nc.sbuf_tensor(f"ln2_{name}", shape, dt))
                xbf8 = fsb("xbf8", [128, DC * TT], BF16)
                sq = fsb("sq", [128, TT], BF16)
                xln = fsb("xln", [128, DC * TT], BF16)
                mbc = fsb("mbc", [128, TT], F32)
                ex2 = fsb("ex2", [128, TT], F32)
                rbcst = fsb("rbcst", [128, TT], F32)
                lnt = fsb("lnt", [128, TT], F32)
                for tile in range(NTILES):
                    t0 = tile * TT
                    ctx.W("vector", dsy, base_sy)
                    ctx.W("vector", dxa, dxa.n)
                    for dcm in range(DC):
                        xs = x_res[:, dcm * TOWN + t0: dcm * TOWN + t0 + TT]
                        ins = V.tensor_copy(xbf8[:, dcm * TT:(dcm + 1) * TT], xs)
                    cast_ms = ctx.inc(ins, ev)
                    ctx.W("tensor", ev, cast_ms)
                    for dcm in range(DC):
                        ins = T.matmul(pb[4][:, 0:TT], ones[:, :], xbf8[:, dcm * TT:(dcm + 1) * TT],
                                       start=(dcm == 0), stop=(dcm == DC - 1))
                    ctx.inc(ins, et)
                    sq_read_ms = None
                    for dcm in range(DC):
                        if sq_read_ms is not None:
                            ctx.W("vector", et, sq_read_ms)
                        sqv = ctx.inc(
                            V.tensor_mul(sq[:, :], x_res[:, dcm * TOWN + t0:dcm * TOWN + t0 + TT],
                                         x_res[:, dcm * TOWN + t0:dcm * TOWN + t0 + TT]), ev)
                        ctx.W("tensor", ev, sqv)
                        sq_read_ms = ctx.inc(
                            T.matmul(pb[5][:, 0:TT], ones[:, :], sq[:, :],
                                     start=(dcm == 0), stop=(dcm == DC - 1)), et)
                    sums_ms = sq_read_ms
                    ctx.W("vector", et, sums_ms)
                    V.tensor_scalar(mbc[:, :], pb[4][:, 0:TT], 1.0 / D, None, ALU.mult)
                    mb_ms = ctx.inc(V.tensor_scalar(ex2[:, :], pb[5][:, 0:TT], 1.0 / D, None,
                                                    ALU.mult), ev)
                    ctx.W("vector", ev, mb_ms)
                    v1 = ctx.inc(V.tensor_mul(lnt[:, :], mbc[:, :], mbc[:, :]), ev)
                    ctx.W("vector", ev, v1)
                    v2 = ctx.inc(V.scalar_tensor_tensor(lnt[:, :], ex2[:, :], 1e-5,
                                                        lnt[:, :], ALU.add, ALU.subtract), ev)
                    ctx.W("scalar", ev, v2)
                    scA = ctx.inc(SC.activation(lnt[:, :], lnt[:, :], ACT.Ln), esc)
                    ctx.W("scalar", esc, scA)
                    scB = ctx.inc(SC.activation(rbcst[:, :], lnt[:, :], ACT.Exp, scale=-0.5), esc)
                    for dcm in range(DC):
                        xs = x_res[:, dcm * TOWN + t0: dcm * TOWN + t0 + TT]
                        ins = V.tensor_sub(xbf8[:, dcm * TT:(dcm + 1) * TT], xs, mbc[:, :])
                    sub_ms = ctx.inc(ins, ev)
                    ctx.W("vector", ev, sub_ms)
                    ctx.W("vector", esc, scB)
                    for dcm in range(DC):
                        ins = V.tensor_mul(xln[:, dcm * TT:(dcm + 1) * TT],
                                           xbf8[:, dcm * TT:(dcm + 1) * TT], rbcst[:, :])
                    xln_ms = ctx.inc(ins, ev)
                    ctx.W("sync", ev, xln_ms)
                    SY.dma_start(out=cc_xa_in.ap()[tile], in_=xln[:, :]).then_inc(dxa.h, 16)
                    dxa.n += 16
                ctx.W("gpsimd", dxa, dxa.n)
                GP.collective_compute(
                    "AllGather", ALU.bypass, replica_groups=[list(range(N_CORES))],
                    ins=[cc_xa_in.ap().opt()], outs=[cc_xa_out.ap().opt()],
                ).then_inc(dcc.h, 1)
                dcc.n += 1
                ctx.barrier(skip=("dcc",))

        def emit_attention():
            with contextlib.ExitStack() as es:
                fsb = lambda name, shape, dt: es.enter_context(
                    nc.sbuf_tensor(f"at_{name}", shape, dt))
                wq_sb = fsb("wq", [128, 128], BF16)
                wk_sb = fsb("wk", [128, 128], BF16)
                wv_sb = fsb("wv", [128, 128], BF16)
                wq8 = fsb("wq8", [128, DC * 128], BF16)
                wk8 = fsb("wk8", [128, DC * 128], BF16)
                wv8 = fsb("wv8", [128, DC * 128], BF16)
                xa = [fsb("xa0", [128, DC * 512], BF16), fsb("xa1", [128, DC * 512], BF16)]
                qt_sb = fsb("qt", [128, S], BF16)
                kt_sb = fsb("kt", [128, S], BF16)
                v_tok = fsb("v_tok", [128, 32 * 144], BF16)
                vp_tok = fsb("vp_tok", [64, 64 * 72], BF16)
                vsum = fsb("vsum", [64, 72], BF16)
                ksum = fsb("ksum", [128, 64], BF16)
                kp_sb = fsb("kp", [64, S], BF16)
                qp_sb = fsb("qp", [64, S], BF16)
                o_acc = fsb("o_acc", [65, S], BF16)
                o_stage = fsb("o_stage", [64, S], BF16)
                zbuf = fsb("zbuf", [66, S], F32)
                pexp = fsb("pexp", [128, 512], BF16)
                pt_sb = fsb("pt", [128, 512], BF16)
                esb = fsb("esb", [128, 512], BF16)
                pts = fsb("pts", [128, 512], BF16)
                psum_ex = fsb("psum_ex", [64, 256], BF16)

                for w_dram, w_s in ((wqT, wq8), (wkT, wk8), (wvT, wv8)):
                    GP.dma_start(out=w_s[:, :].rearrange("p (c f) -> p c f", c=DC),
                                 in_=w_dram.rearrange("(c p) f -> p c f", p=128)).then_inc(dgp.h, 16)
                    dgp.n += 16
                wload = dgp.n
                ones_ms = ctx.inc(V.memset(
                    v_tok[:, :].rearrange("p (s w) -> p s w", w=144)[:, :, 64:65], 1.0), ev)
                ones_ms = ctx.inc(V.memset(
                    v_tok[:, :].rearrange("p (s w) -> p s w", w=144)[:, :, 136:137], 1.0), ev)

                for b in range(B):
                    for tg in range(8):
                        slot = tg % 2
                        shard = 2 * b + (tg // 4)
                        ctx.W("sync", dcc, 1)
                        ctx.W("sync", et, ctx.marks.get(("xaw", slot), 0))
                        SY.dma_start(
                            out=xa[slot][:, :],
                            in_=cc_xa_out.ap()[shard, tg % 4],
                        ).then_inc(dx[slot].h, 16)
                        dx[slot].n += 16
                        ctx.W("tensor", dx[slot], dx[slot].n)
                        ctx.W("tensor", dgp, wload)
                        ctx.W("tensor", ev, ctx.marks.get(("qkv_evac", tg % 2), 0))
                        ctx.W("tensor", esc, ctx.marks.get(("qkv_evac_sc", tg % 2), 0))
                        qbQ, qbK, qbV = pb[3 * (tg % 2)], pb[3 * (tg % 2) + 1], pb[3 * (tg % 2) + 2]
                        for dcm in range(DC):
                            T.matmul(qbQ[:, 0:512], wq8[:, dcm * 128:(dcm + 1) * 128],
                                     xa[slot][:, dcm * 512:(dcm + 1) * 512],
                                     start=(dcm == 0), stop=(dcm == DC - 1))
                        for dcm in range(DC):
                            T.matmul(qbK[:, 0:512], wk8[:, dcm * 128:(dcm + 1) * 128],
                                     xa[slot][:, dcm * 512:(dcm + 1) * 512],
                                     start=(dcm == 0), stop=(dcm == DC - 1))
                        for sub in range(4):
                            for dcm in range(DC):
                                ins = T.matmul(qbV[:, sub * 128:(sub + 1) * 128],
                                               xa[slot][:, dcm * 512 + sub * 128:
                                                        dcm * 512 + sub * 128 + 128],
                                               wv8[:, dcm * 128:(dcm + 1) * 128],
                                               start=(dcm == 0), stop=(dcm == DC - 1))
                        mm_ms = ctx.inc(ins, et)
                        ctx.marks[("xaw", slot)] = mm_ms
                        tcol = tg * 512
                        ctx.W("vector", et, mm_ms)
                        ctx.W("vector", ev, ones_ms)
                        ctx.W("scalar", et, mm_ms)
                        ctx.W("scalar", ev, ones_ms)
                        V.tensor_scalar(qt_sb[:, tcol:tcol + 512], qbQ[:, 0:512], 0.125,
                                        None, ALU.mult)
                        sc_ins = SC.copy(kt_sb[:, tcol:tcol + 512], qbK[:, 0:512])
                        for sub in range(4):
                            kt_i = tg * 4 + sub
                            V.tensor_copy(v_tok[:, kt_i * 144:kt_i * 144 + 64],
                                          qbV[0:128, sub * 128:sub * 128 + 64])
                            ins = V.tensor_copy(v_tok[:, kt_i * 144 + 72:kt_i * 144 + 136],
                                                qbV[0:128, sub * 128 + 64:sub * 128 + 128])
                        ev_ms = ctx.inc(ins, ev)
                        sc_ms = ctx.inc(sc_ins, esc)
                        ctx.marks[("qkv_evac", tg % 2)] = ev_ms
                        ctx.marks[("qkv_evac_sc", tg % 2)] = sc_ms
                    qkv_ms = ev.n
                    ctx.marks[("qkvsc",)] = esc.n

                    for h in range(2):
                        p0 = 0 if h == 0 else 64
                        tp = None if h == 0 else (64, 0)
                        mmulT = (mAT, mA0T) if h == 0 else (mBT, mB0T)

                        if h == 0 and ATT["strided"] >= 1:
                            ctx.W("vector", ev, qkv_ms)
                            ctx.W("vector", esc, ctx.marks.get(("qkvsc",), 0))
                            ctx.W("vector", et, et.n)
                            kpv = ctx.inc(V.tensor_copy(
                                kp_sb[:, :].rearrange("p (r bj) -> p r bj", r=64),
                                kt_sb[0:64, :].rearrange("p (bj r) -> p r bj", bj=64)), ev)
                            qpv = ctx.inc(V.tensor_copy(
                                qp_sb[:, :].rearrange("p (r bj) -> p r bj", r=64),
                                qt_sb[0:64, :].rearrange("p (bj r) -> p r bj", bj=64)), ev)
                            ctx.W("sync", ev, qkv_ms)
                            ctx.W("sync", esc, ctx.marks.get(("qkvsc",), 0))
                            ctx.W("sync", et, et.n)
                            ctx.W("sync", dvp, dvp.n)
                            SY.dma_start(
                                out=vperm_dr.ap().rearrange("(s p) f -> p s f", p=128)[:, :, 0:65],
                                in_=v_tok[:, :].rearrange("p (s w) -> p s w", w=144)[:, :, 0:65],
                            ).then_inc(dvp.h, 16)
                            SY.dma_start(
                                out=vperm_dr.ap().rearrange("(s p) f -> p s f", p=128)[:, :, 65:130],
                                in_=v_tok[:, :].rearrange("p (s w) -> p s w", w=144)[:, :, 72:137],
                            ).then_inc(dvp.h, 16)
                            dvp.n += 32
                            ctx.W("sync", dvp, dvp.n)
                            SY.dma_start(
                                out=vp_tok[:, :].rearrange("p (s w) -> p s w", w=72)[:, :, 0:65],
                                in_=vperm_dr.ap()
                                    .rearrange("(bj r) f -> bj r f", r=64)[:, :, 0:65],
                            ).then_inc(dvp.h, 16)
                            dvp.n += 16
                        elif h == 1 and ATT["summary"]:
                            ctx.W("vector", ev, qkv_ms)
                            ctx.W("vector", esc, ctx.marks.get(("qkvsc",), 0))
                            ctx.W("vector", et, et.n)
                            ksv = ctx.inc(V.tensor_copy(
                                ksum[64:128, :],
                                kt_sb[64:128, :].rearrange("p (c r) -> p c r", r=64)[:, :, 63]), ev)
                            ctx.W("sync", ev, qkv_ms)
                            ctx.W("sync", esc, ctx.marks.get(("qkvsc",), 0))
                            ctx.W("sync", et, et.n)
                            ctx.W("sync", dvp, dvp.n)
                            SY.dma_start(
                                out=vsum[:, 0:65],
                                in_=vperm_dr.ap()
                                    .rearrange("(c u) f -> c u f", u=64)[:, 63, 65:130],
                            ).then_inc(dvs.h, 16)
                            dvs.n += 16

                        if not ATT["passb"]:
                            continue
                        ctx.W("tensor", ev, qkv_ms)
                        ctx.W("tensor", esc, ctx.marks.get(("qkvsc",), 0))
                        if h == 1:
                            ctx.W("tensor", dvs, dvs.n)
                            ctx.W("tensor", ev, ev.n)
                        mk2 = (mAT2c, mAT2c0) if h == 0 else (mBT2c, mBT2c0)
                        for qp2 in range(NQT // 2):
                            bank = pb[2 + (qp2 % 2)]
                            bo = pb[4 + (qp2 % 2)]
                            ctx.W("tensor", esc, ctx.marks.get(("pbexp", qp2 % 2), 0))
                            for qh in range(2):
                                qt = 2 * qp2 + qh
                                kc0 = 0 if qt == 0 else (qt - 1) * 128
                                for half in range(2):
                                    ins = T.matmul(
                                        bank[:, qh * 256 + half * 128:qh * 256 + half * 128 + 128],
                                        kt_sb[p0:p0 + 64,
                                              kc0 + half * 128:kc0 + half * 128 + 128],
                                        qt_sb[p0:p0 + 64, qt * 128:(qt + 1) * 128],
                                        start=True, stop=True, tile_position=tp)
                            if h == 1 and ATT["summary"]:
                                ctx.W("tensor", esc, ctx.marks.get(("sumexS",), 0))
                                for qh in range(2):
                                    qt = 2 * qp2 + qh
                                    ins = T.matmul(pb[6][0:64, qh * 128:(qh + 1) * 128],
                                                   ksum[64:128, :],
                                                   qt_sb[64:128, qt * 128:(qt + 1) * 128],
                                                   start=True, stop=True, tile_position=(64, 0))
                            pbm = ctx.inc(ins, et)
                            ctx.W("scalar", et, pbm)
                            ctx.W("scalar", ev, ctx.marks.get(("pbpt", qp2 % 2), 0))
                            ex1 = ctx.inc(SC.activation(pexp[:, :], bank[:, 0:512], ACT.Exp), esc)
                            ctx.marks[("pbexp", qp2 % 2)] = ex1
                            ctx.W("vector", esc, ex1)
                            mkT = mk2[1] if qp2 == 0 else mk2[0]
                            pt1 = ctx.inc(V.tensor_mul(pt_sb[:, :], pexp[:, :], mkT[:, :]), ev)
                            ctx.marks[("pbpt", qp2 % 2)] = pt1
                            if h == 1 and ATT["summary"]:
                                ctx.W("scalar", ev, ctx.marks.get(("sumex",), 0))
                                sx = ctx.inc(SC.activation(psum_ex[:, :], pb[6][0:64, 0:256],
                                                           ACT.Exp), esc)
                                ctx.marks[("sumexS",)] = sx
                                ctx.W("vector", esc, sx)
                                ctx.marks[("sumex",)] = ctx.inc(V.tensor_mul(
                                    psum_ex[:, :], psum_ex[:, :],
                                    msum[:, qp2 * 256:(qp2 + 1) * 256]), ev)
                            ctx.W("tensor", ev, pt1)
                            if h == 1 and ATT["summary"]:
                                ctx.W("tensor", ev, ctx.marks.get(("sumex",), 0))
                            ctx.W("tensor", ev, ctx.marks.get(("poev", qp2 % 2), 0))
                            for qh in range(2):
                                qt = 2 * qp2 + qh
                                kt0 = 0 if qt == 0 else qt - 1
                                for half in range(2):
                                    ktile = kt0 + half
                                    ins = T.matmul(
                                        bo[0:65, qh * 128:(qh + 1) * 128],
                                        v_tok[:, ktile * 144 + h * 72:ktile * 144 + h * 72 + 65],
                                        pt_sb[:, qh * 256 + half * 128:
                                              qh * 256 + half * 128 + 128],
                                        start=(half == 0),
                                        stop=(half == 1 and (h == 0 or not ATT["summary"])))
                                if h == 1 and ATT["summary"]:
                                    ins = T.matmul(bo[0:65, qh * 128:(qh + 1) * 128],
                                                   vsum[:, 0:65],
                                                   psum_ex[:, qh * 128:(qh + 1) * 128],
                                                   start=False, stop=True)
                            pv_ms = ctx.inc(ins, et)
                            ctx.W("vector", et, pv_ms)
                            ctx.marks[("poev", qp2 % 2)] = ctx.inc(
                                V.tensor_copy(o_acc[:, qp2 * 256:(qp2 + 1) * 256],
                                              bo[0:65, 0:256]), ev)

                        if h == 0 and ATT["strided"] >= 2:
                            ctx.W("tensor", ev, ev.n)
                            ctx.W("tensor", esc, esc.n)
                            ctx.W("tensor", dvp, dvp.n)
                            for rg in range(8):
                                bs = pb[2 + (rg % 2)]
                                bos = pb[4 + (rg % 2)]
                                ctx.W("tensor", esc, ctx.marks.get(("strex", rg % 2), 0))
                                ctx.W("tensor", ev, ctx.marks.get(("stradd", rg % 2), 0))
                                for rr in range(8):
                                    r = rg * 8 + rr
                                    ins = T.matmul(bs[0:64, rr * 64:(rr + 1) * 64],
                                                   kp_sb[:, r * 64:(r + 1) * 64],
                                                   qp_sb[:, r * 64:(r + 1) * 64],
                                                   start=True, stop=True)
                                sm = ctx.inc(ins, et)
                                ctx.W("scalar", et, sm)
                                ctx.W("scalar", ev, ctx.marks.get(("strpt", rg % 2), 0))
                                se = ctx.inc(SC.activation(esb[0:64, :], bs[0:64, 0:512],
                                                           ACT.Exp), esc)
                                ctx.marks[("strex", rg % 2)] = se
                                ctx.W("vector", esc, se)
                                sp = ctx.inc(V.tensor_mul(pts[0:64, :], esb[0:64, :],
                                                          mstr[0:64, :]), ev)
                                ctx.marks[("strpt", rg % 2)] = sp
                                ctx.W("tensor", ev, sp)
                                if ATT["strided"] < 3:
                                    ctx.marks[("stradd", rg % 2)] = sp
                                    continue
                                for rr in range(8):
                                    r = rg * 8 + rr
                                    ins = T.matmul(bos[0:65, rr * 64:(rr + 1) * 64],
                                                   vp_tok[:, r * 72:r * 72 + 65],
                                                   pts[0:64, rr * 64:(rr + 1) * 64],
                                                   start=True, stop=True)
                                spv = ctx.inc(ins, et)
                                ctx.W("vector", et, spv)
                                oacc_v = o_acc[:, :].rearrange("p (qi r) -> p r qi",
                                                               r=64)[:, rg * 8:(rg + 1) * 8, :]
                                ctx.marks[("stradd", rg % 2)] = ctx.inc(V.tensor_add(
                                    oacc_v, oacc_v,
                                    bos[0:65, 0:512].rearrange("p (rr qi) -> p rr qi", rr=8)), ev)

                        if not ATT["norm"]:
                            continue
                        ctx.W("vector", ev, ev.n)
                        nz0 = ctx.inc(V.tensor_copy(zbuf[64:65, :], o_acc[64:65, :]), ev)
                        ctx.W("scalar", ev, nz0)
                        z1 = ctx.inc(SC.activation(zbuf[64:65, :], zbuf[64:65, :], ACT.Ln), esc)
                        ctx.W("scalar", esc, z1)
                        z2 = ctx.inc(SC.activation(zbuf[64:65, :], zbuf[64:65, :], ACT.Exp,
                                                   scale=-1.0), esc)
                        ctx.W("tensor", esc, z2)
                        ctx.W("vector", dco, dco.n)
                        for g in range(8):
                            bz = pb[2 + (g % 2)]
                            ctx.W("tensor", ev, ctx.marks.get(("zev", g % 2), 0))
                            zm = ctx.inc(T.matmul(bz[0:64, 0:512], onesf[64:65, 0:64],
                                                  zbuf[64:65, g * 512:(g + 1) * 512],
                                                  start=True, stop=True), et)
                            ctx.W("vector", et, zm)
                            ctx.marks[("zev", g % 2)] = ctx.inc(
                                V.tensor_mul(o_stage[:, g * 512:(g + 1) * 512],
                                             o_acc[0:64, g * 512:(g + 1) * 512],
                                             bz[0:64, 0:512]), ev)
                        ctx.W("sync", ev, ev.n)
                        SY.dma_start(out=cc_o_in.ap()[b, h], in_=o_stage[:, :]).then_inc(dco.h, 16)
                        dco.n += 16
                    if ATT["ag2"]:
                        ctx.W("gpsimd", dco, dco.n)
                        GP.collective_compute(
                            "AllGather", ALU.bypass, replica_groups=[list(range(N_CORES))],
                            ins=[cc_o_in.ap()[b].opt()], outs=[cc_o_out.ap()[b].opt()],
                        ).then_inc(dcc.h, 1)
                        dcc.n += 1
                    ctx.barrier(engines=("tensor", "vector", "scalar"), skip=("dcc",))
                ctx.barrier(skip=("dcc",))

        def emit_wo():
            with contextlib.ExitStack() as es:
                fsb = lambda name, shape, dt: es.enter_context(
                    nc.sbuf_tensor(f"wo_{name}", shape, dt))
                wo_sb = fsb("wo", [128, DC * D], BF16)
                og = [fsb("og0", [128, DC * 512], BF16), fsb("og1", [128, DC * 512], BF16)]
                GP.dma_start(out=wo_sb[:, :].rearrange("p (c f) -> p c f", c=DC),
                             in_=woT.rearrange("(c p) f -> p c f", p=128)).then_inc(dgp.h, 16)
                dgp.n += 16
                wo_ld = dgp.n
                pid = SY.partition_id()
                row0 = SY.compute_val((pid // 2) * (N_CORES * 2 * 64))
                colb = SY.compute_val((pid % 2) * TOWN)
                flat = cc_o_out.ap().rearrange("b n h d t -> (b n h d) t")
                ctx.W("sync", dcc, dcc.n)
                for ts in range(4):
                    slot = ts % 2
                    ctx.W("sync", et, ctx.marks.get(("ogw", slot), 0))
                    col = SY.compute_val(colb + ts * 512)
                    SY.dma_start(
                        out=og[slot][:, :].rearrange("p (c t) -> p c t", c=DC),
                        in_=flat[bass.ds(row0, 1024), bass.ds(col, 512)]
                            .rearrange("(c p) t -> p c t", p=128),
                    ).then_inc(dol[slot].h, 16)
                    dol[slot].n += 16
                    ctx.W("tensor", dol[slot], dol[slot].n)
                    ctx.W("tensor", dgp, wo_ld)
                    t0 = ts * 512
                    for oc in range(DC):
                        bank = pb[4 + (oc % 2)]
                        ctx.W("tensor", ev, ctx.marks.get(("woev", oc % 2), 0))
                        for ic in range(DC):
                            ins = T.matmul(bank[:, 0:512],
                                           wo_sb[:, ic * D + oc * 128:ic * D + oc * 128 + 128],
                                           og[slot][:, ic * 512:(ic + 1) * 512],
                                           start=(ic == 0), stop=(ic == DC - 1))
                        wm = ctx.inc(ins, et)
                        ctx.marks[("ogw", slot)] = wm
                        ctx.W("vector", et, wm)
                        ctx.marks[("woev", oc % 2)] = ctx.inc(V.tensor_add(
                            x_res[:, oc * TOWN + t0: oc * TOWN + t0 + 512],
                            x_res[:, oc * TOWN + t0: oc * TOWN + t0 + 512],
                            bank[:, 0:512]), ev)
                ctx.barrier()

        if PHASES.get("ffn1"):
            emit_ffn("f1")
        if PHASES.get("attn"):
            emit_ln2_and_ag1()
            emit_attention()
            if ATT["wo"]:
                emit_wo()
        if PHASES.get("ffn2"):
            emit_ffn("f2")

        ctx.barrier()
        SY.dma_start(out=yT.rearrange("(c p) t -> p c t", p=128),
                     in_=x_res[:, :].rearrange("p (c t) -> p c t", c=DC)).then_inc(dst.h, 16)
        dst.n += 16
        ctx.W("sync", dst, dst.n)

    return nc


_NC = None


def _in_maps(inputs):
    x = np.asarray(inputs["L0_attn_x"], dtype=np.float32).reshape(B * S, D)
    Wq = np.asarray(inputs["L0_attn_Wq"], dtype=np.float32)
    Wk = np.asarray(inputs["L0_attn_Wk"], dtype=np.float32)
    Wv = np.asarray(inputs["L0_attn_Wv"], dtype=np.float32)
    Wo = np.asarray(inputs["L0_attn_Wo"], dtype=np.float32)
    rw_ = np.ascontiguousarray(np.asarray(inputs["L0_ffn_router_w"], dtype=np.float32))
    rb_ = np.asarray(inputs["L0_ffn_router_b"], dtype=np.float32).reshape(NROUTED, 1)
    w1_ = np.asarray(inputs["L0_ffn_routed_w1"], dtype=np.float32)
    w3_ = np.asarray(inputs["L0_ffn_routed_w3"], dtype=np.float32)
    w2_ = np.asarray(inputs["L0_ffn_routed_w2"], dtype=np.float32)
    s1_ = np.asarray(inputs["L0_ffn_shared_w1"], dtype=np.float32)
    s3_ = np.asarray(inputs["L0_ffn_shared_w3"], dtype=np.float32)
    s2_ = np.asarray(inputs["L0_ffn_shared_w2"], dtype=np.float32)
    def _w13_part(wa):
        return wa.reshape(-1, DC, 128, HFF).transpose(0, 2, 1, 3).reshape(-1, 128, DC * HFF)
    w13cat = np.concatenate([
        np.concatenate([_w13_part(w1_), _w13_part(w3_)], axis=2),
        np.concatenate([_w13_part(s1_), _w13_part(s3_)], axis=2)], axis=0)
    w13cat = np.ascontiguousarray(w13cat)
    def _w2_part(wa):
        return wa.reshape(-1, 4, 128, DC, 128).transpose(0, 2, 1, 3, 4)
    w2cat = np.concatenate([_w2_part(w2_), _w2_part(s2_)], axis=0)
    w2cat = np.ascontiguousarray(w2cat.reshape(NEXP, 128, 4 * DC * 128))
    WqT, WkT, WvT, WoT = Wq.T, Wk.T, Wv.T, Wo.T
    perm = np.concatenate([np.r_[64 * c:64 * c + 64, 64 * (c + 8):64 * (c + 8) + 64]
                           for c in range(N_CORES)])
    WoTp = np.ascontiguousarray(WoT[perm, :])
    maps = []
    for c in range(N_CORES):
        cols = np.r_[64 * c:64 * c + 64, 64 * (c + 8):64 * (c + 8) + 64]
        maps.append({
            "xT": np.ascontiguousarray(x[c * TOWN:(c + 1) * TOWN].T),
            "wqT": np.ascontiguousarray(WqT[:, cols]),
            "wkT": np.ascontiguousarray(WkT[:, cols]),
            "wvT": np.ascontiguousarray(WvT[:, cols]),
            "woT": WoTp,
            "rw": rw_, "rbc": rb_, "w13d": w13cat, "w2d": w2cat,
        })
    return maps


def kernel(**inputs):
    global _NC
    if _NC is None:
        _NC = build_nc()
    maps = _in_maps(inputs)
    res = run_bass_kernel_spmd(_NC, maps, core_ids=list(range(N_CORES)))
    out = np.empty((B * S, D), dtype=np.float32)
    for c in range(N_CORES):
        out[c * TOWN:(c + 1) * TOWN] = np.asarray(res.results[c]["yT"]).T
    return out.reshape(B, S, D)


# version 10 (final)
# speedup vs baseline: 1.2456x; 1.0219x over previous
import contextlib
import numpy as np
import ml_dtypes
import concourse.bass as bass
import concourse.mybir as mybir
from concourse.bass_utils import run_bass_kernel_spmd

F32 = mybir.dt.float32
BF16 = mybir.dt.bfloat16
ALU = mybir.AluOpType
ACT = mybir.ActivationFunctionType
AX = mybir.AxisListType.X
BF = ml_dtypes.bfloat16

N_CORES = 8
D = 1024
DC = 8
HFF = 512
NEXP = 10
NROUTED = 8
B = 4
S = 4096
TOWN = 2048
TT = 512
NTILES = TOWN // TT
NQT = S // 128
NEG = -1.0e9

PHASES = {"ffn1": True, "attn": True, "ffn2": True}
DEBUG = False
ATT = {"qkv": True, "passa": True, "passb": True, "summary": True,
       "strided": 3, "norm": True, "ag2": True, "wo": True}


def _masks_np():
    r = np.arange(128)[:, None]
    c = np.arange(256)[None, :]
    okA = (c - r >= 65) & (c - r <= 128)
    okA0 = (r - c >= 0) & (r - c < 64)
    cc = c - 128
    okB = (c >= 128) & (cc // 64 == r // 64) & (cc <= r)
    okB0 = (c < 128) & (c // 64 == r // 64) & (c <= r)

    def addm(ok):
        return np.where(ok, 0.0, NEG).astype(np.float32)

    def mulT(ok):
        m = np.zeros((128, 256), dtype=BF)
        m[:, 0:128] = ok[:, 0:128].T.astype(BF)
        m[:, 128:256] = ok[:, 128:256].T.astype(BF)
        return m

    msum = np.zeros((64, NQT * 128), dtype=BF)
    cc64 = np.arange(64)[:, None]
    rr = np.arange(128)[None, :]
    for qt in range(NQT):
        msum[:, qt * 128:(qt + 1) * 128] = (cc64 < 2 * qt + (rr >= 64)).astype(BF)
    bj = np.arange(64)[:, None]
    qi = np.arange(64)[None, :]
    m1 = (bj < qi).astype(BF)
    mstr = np.zeros((128, 512), dtype=BF)
    for g in range(8):
        mstr[0:64, g * 64:(g + 1) * 64] = m1
        mstr[64:128, g * 64:(g + 1) * 64] = m1
    return (addm(okA), addm(okA0), addm(okB), addm(okB0),
            mulT(okA), mulT(okA0), mulT(okB), mulT(okB0), msum, mstr)


class Ctx:
    def __init__(self, nc, es):
        self.nc = nc
        self.es = es
        self.sems = {}
        self.lastw = {}
        self.marks = {}

    def sem(self, name):
        if name not in self.sems:
            h = self.es.enter_context(self.nc.semaphore(name))
            s = type("S", (), {})()
            s.h, s.n, s.name = h, 0, name
            self.sems[name] = s
        return self.sems[name]

    def W(self, eng, s, val):
        if val is None or val <= 0:
            return
        key = (eng, s.name)
        if self.lastw.get(key, -1) >= val:
            return
        getattr(self.nc, eng).wait_ge(s.h, val)
        self.lastw[key] = val

    def inc(self, instr, s, k=1):
        instr.then_inc(s.h, k)
        s.n += k
        return s.n

    def barrier(self, engines=("tensor", "vector", "scalar", "sync", "gpsimd"), skip=()):
        for e in engines:
            for s in self.sems.values():
                if s.name in skip:
                    continue
                self.W(e, s, s.n)


def build_nc():
    nc = bass.Bass("TRN2", target_bir_lowering=False)
    T, V, SC, SY, GP = nc.tensor, nc.vector, nc.scalar, nc.sync, nc.gpsimd

    xT = nc.dram_tensor("xT", [D, TOWN], F32, kind="ExternalInput").ap()
    wqT = nc.dram_tensor("wqT", [D, 128], F32, kind="ExternalInput").ap()
    wkT = nc.dram_tensor("wkT", [D, 128], F32, kind="ExternalInput").ap()
    wvT = nc.dram_tensor("wvT", [D, 128], F32, kind="ExternalInput").ap()
    woT = nc.dram_tensor("woT", [D, D], F32, kind="ExternalInput").ap()
    rw = nc.dram_tensor("rw", [D, NROUTED], F32, kind="ExternalInput").ap()
    rbc = nc.dram_tensor("rbc", [NROUTED, 1], F32, kind="ExternalInput").ap()
    w13d = nc.dram_tensor("w13d", [NEXP, 128, 2 * DC * HFF], F32, kind="ExternalInput").ap()
    w2d = nc.dram_tensor("w2d", [NEXP, 128, 4 * DC * 128], F32, kind="ExternalInput").ap()
    yT = nc.dram_tensor("yT", [D, TOWN], F32, kind="ExternalOutput").ap()

    if DEBUG:
        dbg_t = {
            "dbg_xln": nc.dram_tensor("dbg_xln", [128, DC * TT], BF16),
            "dbg_l": nc.dram_tensor("dbg_l", [NROUTED, TT], BF16),
            "dbg_g": nc.dram_tensor("dbg_g", [NROUTED, TT], BF16),
            "dbg_hh": nc.dram_tensor("dbg_hh", [128, NEXP * 4 * TT], BF16),
        }
    cc_xa_in = nc.dram_tensor("cc_xa_in", [NTILES, 128, DC * TT], BF16)
    cc_xa_out = nc.dram_tensor("cc_xa_out", [N_CORES, NTILES, 128, DC * TT], BF16,
                               addr_space="Shared")
    vperm_dr = nc.dram_tensor("vperm_dr", [S, 130], BF16)
    cc_o_in = nc.dram_tensor("cc_o_in", [B, 2, 64, S], BF16)
    cc_o_out = nc.dram_tensor("cc_o_out", [B, N_CORES, 2, 64, S], BF16, addr_space="Shared")

    (mA_np, mA0_np, mB_np, mB0_np, mAT_np, mA0T_np, mBT_np, mB0T_np,
     msum_np, mstr_np) = _masks_np()
    oneh_np = np.zeros((NROUTED, NROUTED * 128), dtype=BF)
    for e in range(NROUTED):
        oneh_np[e, e * 128:(e + 1) * 128] = 1.0
    mAT2 = np.concatenate([mAT_np, mAT_np], axis=1)
    mAT2_0 = np.concatenate([mA0T_np, mAT_np], axis=1)
    mBT2 = np.concatenate([mBT_np, mBT_np], axis=1)
    mBT2_0 = np.concatenate([mB0T_np, mBT_np], axis=1)
    consts = [
        ("c_ident", np.eye(128, dtype=BF)), ("c_ones", np.ones((128, 128), dtype=BF)),
        ("c_mAT2", mAT2), ("c_mAT2_0", mAT2_0), ("c_mBT2", mBT2), ("c_mBT2_0", mBT2_0),
        ("c_oneh", oneh_np),
        ("c_onesf", np.ones((128, 64), dtype=np.float32)),
        ("c_mA", mA_np), ("c_mA0", mA0_np), ("c_mB", mB_np), ("c_mB0", mB0_np),
        ("c_mAT", mAT_np), ("c_mA0T", mA0T_np), ("c_mBT", mBT_np), ("c_mB0T", mB0T_np),
        ("c_msum", msum_np), ("c_mstr", mstr_np),
    ]
    c_handles = {n: nc.inline_tensor(a, name=n) for n, a in consts}
    c_shapes = {n: a for n, a in consts}

    es_top = contextlib.ExitStack()
    with es_top:
        ctx = Ctx(nc, es_top)
        sb = lambda name, shape, dt: es_top.enter_context(nc.sbuf_tensor(name, shape, dt))

        x_res = sb("x_res", [128, DC * TOWN], F32)
        csb = {}
        for n, a in consts:
            dt = BF16 if a.dtype == BF else F32
            csb[n] = sb(n[2:], list(a.shape), dt)
        ident, ones = csb["c_ident"], csb["c_ones"]
        mA, mA0, mB, mB0 = csb["c_mA"], csb["c_mA0"], csb["c_mB"], csb["c_mB0"]
        mAT, mA0T, mBT, mB0T = csb["c_mAT"], csb["c_mA0T"], csb["c_mBT"], csb["c_mB0T"]
        msum, mstr = csb["c_msum"], csb["c_mstr"]
        oneh = csb["c_oneh"]
        onesf = csb["c_onesf"]
        mAT2c, mAT2c0, mBT2c, mBT2c0 = (csb["c_mAT2"], csb["c_mAT2_0"],
                                        csb["c_mBT2"], csb["c_mBT2_0"])
        rw_sb = sb("rw_sb", [128, DC * NROUTED], BF16)
        rb_sb = sb("rb_sb", [NROUTED, 1], F32)

        pb = [es_top.enter_context(nc.psum_tensor(f"pb{i}", [128, 512], F32)) for i in range(7)]
        pbT = es_top.enter_context(nc.psum_tensor("pbT", [128, 1024], BF16))

        et, ev, esc = ctx.sem("et"), ctx.sem("ev"), ctx.sem("esc")
        dsy = ctx.sem("dsy")
        dgp = ctx.sem("dgp")
        dw = [ctx.sem("dw0"), ctx.sem("dw1")]
        dwx = [ctx.sem("dwa"), ctx.sem("dwb")]
        dx = [ctx.sem("dx0"), ctx.sem("dx1")]
        dmr = ctx.sem("dmr")
        dvp = ctx.sem("dvp")
        dvs = ctx.sem("dvs")
        dco = ctx.sem("dco")
        dxa = ctx.sem("dxa")
        dcc = ctx.sem("dcc")
        dol = [ctx.sem("dol0"), ctx.sem("dol1")]
        dst = ctx.sem("dst")

        SY.dma_start(out=x_res[:, :].rearrange("p (c t) -> p c t", c=DC),
                     in_=xT.rearrange("(c p) t -> p c t", p=128)).then_inc(dsy.h, 16)
        dsy.n += 16
        for n, _ in consts:
            SY.dma_start(out=csb[n][:, :], in_=c_handles[n].ap()).then_inc(dsy.h, 16)
            dsy.n += 16
        SY.dma_start(out=rb_sb[:, :], in_=rbc).then_inc(dsy.h, 16)
        dsy.n += 16
        GP.dma_start(out=rw_sb[:, :].rearrange("p (c e) -> p c e", c=DC),
                     in_=rw.rearrange("(c p) e -> p c e", p=128)).then_inc(dgp.h, 16)
        dgp.n += 16
        base_sy = dsy.n
        base_gp = dgp.n

        def emit_ffn(tag):
            with contextlib.ExitStack() as es:
                fsb = lambda name, shape, dt: es.enter_context(
                    nc.sbuf_tensor(f"{tag}_{name}", shape, dt))
                xln = fsb("xln", [128, DC * TOWN], BF16)
                sq = fsb("sq", [128, TT], BF16)
                mbc = fsb("mbc", [128, TT], F32)
                ex2 = fsb("ex2", [128, TT], F32)
                rbcst = fsb("rbcst", [128, TT], F32)
                lnt = fsb("lnt", [128, TT], F32)
                l_sb = fsb("l_sb", [NROUTED, TOWN], BF16)
                topf = fsb("topf", [128, 48], F32)
                ge_sb = fsb("ge_sb", [128, NROUTED], F32)
                gbf = fsb("gbf", [128, NROUTED], BF16)
                g_sb = fsb("g_sb", [NROUTED, TOWN], BF16)
                silu_sb = fsb("silu_sb", [128, 2 * TT], BF16)
                hh = fsb("hh", [128, 16 * TT], BF16)
                w13 = [fsb("w13_0", [128, 2 * DC * HFF], BF16),
                       fsb("w13_1", [128, 2 * DC * HFF], BF16)]
                w2b = [fsb("w2b_0", [128, 4 * DC * 128], BF16),
                       fsb("w2b_1", [128, 4 * DC * 128], BF16)]

                w13_expect = [0, 0]
                w2_expect = [0, 0]

                def issue_w(e, slot, wait_et):
                    ctx.W("gpsimd", et, wait_et)
                    GP.dma_start(out=w13[slot][:, :], in_=w13d[e]).then_inc(dw[slot].h, 16)
                    dw[slot].n += 16
                    w13_expect[slot] = dw[slot].n
                    GP.dma_start(out=w2b[slot][:, :], in_=w2d[e]).then_inc(dwx[slot].h, 16)
                    dwx[slot].n += 16
                    w2_expect[slot] = dwx[slot].n

                issue_w(NROUTED, 0, 0)
                issue_w(NROUTED + 1, 1, 0)
                w13_loaded = {0: w13_expect[0], 1: w13_expect[1]}
                w2_loaded = {0: w2_expect[0], 1: w2_expect[1]}

                ctx.W("vector", dsy, base_sy)
                ctx.W("vector", et, et.n)
                for ts in range(4):
                    t0 = ts * TT
                    for dcm in range(DC):
                        xs = x_res[:, dcm * TOWN + t0: dcm * TOWN + t0 + TT]
                        ins = V.tensor_copy(xln[:, dcm * TOWN + t0:dcm * TOWN + t0 + TT], xs)
                    cast_ms = ctx.inc(ins, ev)
                    ctx.W("tensor", ev, cast_ms)
                    ctx.W("tensor", ev, ctx.marks.get(("lnap", ts), 0))
                    for dcm in range(DC):
                        ins = T.matmul(pb[4][:, 0:TT], ones[:, :],
                                       xln[:, dcm * TOWN + t0:dcm * TOWN + t0 + TT],
                                       start=(dcm == 0), stop=(dcm == DC - 1))
                    ctx.inc(ins, et)
                    sq_read_ms = None
                    for dcm in range(DC):
                        if sq_read_ms is not None:
                            ctx.W("vector", et, sq_read_ms)
                        sqv = ctx.inc(
                            V.tensor_mul(sq[:, :], x_res[:, dcm * TOWN + t0:dcm * TOWN + t0 + TT],
                                         x_res[:, dcm * TOWN + t0:dcm * TOWN + t0 + TT]), ev)
                        ctx.W("tensor", ev, sqv)
                        sq_read_ms = ctx.inc(
                            T.matmul(pb[5][:, 0:TT], ones[:, :], sq[:, :],
                                     start=(dcm == 0), stop=(dcm == DC - 1)), et)
                    sums_ms = sq_read_ms
                    ctx.W("vector", et, sums_ms)
                    V.tensor_scalar(mbc[:, :], pb[4][:, 0:TT], 1.0 / D, None, ALU.mult)
                    mb_ms = ctx.inc(V.tensor_scalar(ex2[:, :], pb[5][:, 0:TT], 1.0 / D, None,
                                                    ALU.mult), ev)
                    ctx.W("vector", ev, mb_ms)
                    v1 = ctx.inc(V.tensor_mul(lnt[:, :], mbc[:, :], mbc[:, :]), ev)
                    ctx.W("vector", ev, v1)
                    v2 = ctx.inc(V.scalar_tensor_tensor(lnt[:, :], ex2[:, :], 1e-5,
                                                        lnt[:, :], ALU.add, ALU.subtract), ev)
                    ctx.W("scalar", ev, v2)
                    sc1 = ctx.inc(SC.activation(lnt[:, :], lnt[:, :], ACT.Ln), esc)
                    ctx.W("scalar", esc, sc1)
                    sc2 = ctx.inc(SC.activation(rbcst[:, :], lnt[:, :], ACT.Exp, scale=-0.5), esc)
                    for dcm in range(DC):
                        xs = x_res[:, dcm * TOWN + t0: dcm * TOWN + t0 + TT]
                        ins = V.tensor_sub(xln[:, dcm * TOWN + t0:dcm * TOWN + t0 + TT],
                                           xs, mbc[:, :])
                    sub_ms = ctx.inc(ins, ev)
                    ctx.W("vector", ev, sub_ms)
                    ctx.W("vector", esc, sc2)
                    for dcm in range(DC):
                        ins = V.tensor_mul(xln[:, dcm * TOWN + t0:dcm * TOWN + t0 + TT],
                                           xln[:, dcm * TOWN + t0:dcm * TOWN + t0 + TT],
                                           rbcst[:, :])
                    xln_ts = ctx.inc(ins, ev)
                    ctx.marks[("xln", ts)] = xln_ts
                    ctx.W("tensor", ev, xln_ts)
                    ctx.W("tensor", dgp, base_gp)
                    for dcm in range(DC):
                        ins = T.matmul(pb[6][0:NROUTED, 0:TT],
                                       rw_sb[:, dcm * NROUTED:(dcm + 1) * NROUTED],
                                       xln[:, dcm * TOWN + t0:dcm * TOWN + t0 + TT],
                                       start=(dcm == 0), stop=(dcm == DC - 1))
                    rt_ms = ctx.inc(ins, et)
                    ctx.W("vector", et, rt_ms)
                    lg_ms = ctx.inc(V.tensor_scalar(l_sb[:, t0:t0 + TT], pb[6][0:NROUTED, 0:TT],
                                                    rb_sb[:, :], None, ALU.add), ev)
                    ctx.W("tensor", ev, lg_ms)
                    for g in range(4):
                        ins = T.transpose(pbT[:, g * 8:(g + 1) * 8],
                                          l_sb[:, t0 + g * 128:t0 + (g + 1) * 128],
                                          ident[0:8, 0:8])
                    tr_ms = ctx.inc(ins, et)
                    prev_gt = tr_ms
                    for g in range(4):
                        lt = pbT[:, g * 8:(g + 1) * 8]
                        ctx.W("vector", et, prev_gt)
                        a1 = ctx.inc(V.reduce_max(topf[:, 0:1], lt, axis=AX, negate=True), ev)
                        ctx.W("vector", ev, a1)
                        a2 = ctx.inc(V.tensor_scalar(topf[:, 8:16], lt, topf[:, 0:1], 0.0,
                                                     ALU.add, ALU.is_lt), ev)
                        ctx.W("vector", ev, a2)
                        a2b = ctx.inc(V.tensor_scalar(topf[:, 40:48], topf[:, 8:16], 1.0, 1.0e30,
                                                      ALU.subtract, ALU.mult), ev)
                        a3a = ctx.inc(V.tensor_mul(topf[:, 16:24], lt, topf[:, 8:16]), ev)
                        ctx.W("vector", ev, a3a)
                        a3 = ctx.inc(V.tensor_add(topf[:, 16:24], topf[:, 16:24],
                                                  topf[:, 40:48]), ev)
                        ctx.W("vector", ev, a3)
                        a4 = ctx.inc(V.reduce_max(topf[:, 1:2], topf[:, 16:24], axis=AX), ev)
                        ctx.W("scalar", ev, a1)
                        ctx.W("scalar", et, tr_ms)
                        e1 = ctx.inc(SC.activation(topf[:, 24:32], lt, ACT.Exp, bias=topf[:, 0:1]),
                                     esc)
                        ctx.W("vector", ev, a4)
                        a5 = ctx.inc(V.tensor_single_scalar(topf[:, 32:40], lt, topf[:, 1:2],
                                                            ALU.is_ge), ev)
                        ctx.W("vector", esc, e1)
                        ctx.W("vector", ev, a5)
                        a6 = ctx.inc(V.tensor_mul(ge_sb[:, :], topf[:, 24:32], topf[:, 32:40]), ev)
                        ctx.W("vector", ev, a6)
                        a7 = ctx.inc(V.reduce_sum(topf[:, 2:3], ge_sb[:, :], axis=AX), ev)
                        ctx.W("vector", ev, a7)
                        a8 = ctx.inc(V.reciprocal(topf[:, 3:4], topf[:, 2:3]), ev)
                        ctx.W("vector", ev, a8)
                        a9 = ctx.inc(V.tensor_scalar(gbf[:, :], ge_sb[:, :], topf[:, 3:4], None,
                                                     ALU.mult), ev)
                        ctx.W("tensor", ev, a9)
                        gt = ctx.inc(T.transpose(pbT[0:NROUTED, 512 + g * 128:512 + (g + 1) * 128],
                                                 gbf[:, :], ident[:, :]), et)
                        prev_gt = gt
                        ctx.W("vector", et, gt)
                        ctx.inc(V.tensor_copy(g_sb[:, t0 + g * 128:t0 + (g + 1) * 128],
                                              pbT[0:NROUTED, 512 + g * 128:512 + (g + 1) * 128]),
                                ev)
                    ctx.marks[("gts", ts)] = ev.n
                xln_ms = ev.n
                gates_ms = ev.n

                EORD = [NROUTED, NROUTED + 1] + list(range(NROUTED))
                for ei, e in enumerate(EORD):
                    slot = ei % 2
                    ctx.W("tensor", dw[slot], w13_loaded[slot])
                    for ts in range(4):
                        t0 = ts * TT
                        ctx.W("tensor", ev, ctx.marks.get(("xln", ts), 0))
                        if e < NROUTED:
                            ctx.W("tensor", ev, ctx.marks.get(("gts", ts), 0))
                            ctx.W("tensor", ev, ctx.marks.get(("gbread",), 0))
                            gbc = ctx.inc(T.matmul(pb[6][:, 0:TT],
                                                   oneh[:, e * 128:(e + 1) * 128],
                                                   g_sb[:, t0:t0 + TT],
                                                   start=True, stop=True), et)
                        for hc in range(4):
                            par = hc % 2
                            bh1, bh3 = pb[0 + 2 * par], pb[1 + 2 * par]
                            ctx.W("tensor", ev, ctx.marks.get(("s1ev", par), 0))
                            for dcm in range(DC):
                                T.matmul(bh1[:, 0:TT],
                                         w13[slot][:, dcm * HFF + hc * 128:
                                                   dcm * HFF + hc * 128 + 128],
                                         xln[:, dcm * TOWN + t0:dcm * TOWN + t0 + TT],
                                         start=(dcm == 0), stop=(dcm == DC - 1))
                            for dcm in range(DC):
                                ins = T.matmul(bh3[:, 0:TT],
                                               w13[slot][:, DC * HFF + dcm * HFF + hc * 128:
                                                         DC * HFF + dcm * HFF + hc * 128 + 128],
                                               xln[:, dcm * TOWN + t0:dcm * TOWN + t0 + TT],
                                               start=(dcm == 0), stop=(dcm == DC - 1))
                            mm_ms = ctx.inc(ins, et)
                            sl = silu_sb[:, par * TT:(par + 1) * TT]
                            ctx.W("scalar", et, mm_ms)
                            ctx.W("scalar", ev, ctx.marks.get(("s1ev", par), 0))
                            sl_ms = ctx.inc(SC.activation(sl, bh1[:, 0:TT], ACT.Silu), esc)
                            hho = hh[:, (hc * 4 + ts) * TT:(hc * 4 + ts + 1) * TT]
                            ctx.W("vector", esc, sl_ms)
                            if e < NROUTED:
                                tv = ctx.inc(V.tensor_mul(sl, sl, bh3[:, 0:TT]), ev)
                                ctx.W("vector", ev, tv)
                                ctx.W("vector", et, gbc)
                                s1e = ctx.inc(V.tensor_mul(hho, sl, pb[6][:, 0:TT]), ev)
                                ctx.marks[("gbread",)] = s1e
                            else:
                                s1e = ctx.inc(V.tensor_mul(hho, sl, bh3[:, 0:TT]), ev)
                            ctx.marks[("s1ev", par)] = s1e
                    hh_ms = ev.n
                    ctx.W("tensor", dwx[slot], w2_loaded[slot])
                    ctx.W("tensor", ev, hh_ms)
                    for dcm in range(DC):
                        for ts in range(4):
                            t0 = ts * TT
                            par = (dcm * 4 + ts) % 2
                            bank = pb[4 + par]
                            ctx.W("tensor", ev, ctx.marks.get(("s2ev", par), 0))
                            for hc in range(4):
                                ins = T.matmul(bank[:, 0:TT],
                                               w2b[slot][:, (hc * DC + dcm) * 128:
                                                         (hc * DC + dcm) * 128 + 128],
                                               hh[:, (hc * 4 + ts) * TT:(hc * 4 + ts + 1) * TT],
                                               start=(hc == 0), stop=(hc == 3))
                            o_ms = ctx.inc(ins, et)
                            ctx.W("vector", et, o_ms)
                            upd = ctx.inc(V.scalar_tensor_tensor(
                                x_res[:, dcm * TOWN + t0: dcm * TOWN + t0 + TT],
                                bank[:, 0:TT], 0.5,
                                x_res[:, dcm * TOWN + t0: dcm * TOWN + t0 + TT],
                                ALU.mult, ALU.add), ev)
                            ctx.marks[("s2ev", par)] = upd
                    if ei + 2 < NEXP:
                        issue_w(EORD[ei + 2], slot, et.n)
                        w13_loaded[slot] = w13_expect[slot]
                        w2_loaded[slot] = w2_expect[slot]
                ctx.barrier()

        def emit_ln2_and_ag1():
            with contextlib.ExitStack() as es:
                fsb = lambda name, shape, dt: es.enter_context(
                    nc.sbuf_tensor(f"ln2_{name}", shape, dt))
                xbf8 = fsb("xbf8", [128, DC * TT], BF16)
                sq = fsb("sq", [128, TT], BF16)
                xln = fsb("xln", [128, DC * TT], BF16)
                mbc = fsb("mbc", [128, TT], F32)
                ex2 = fsb("ex2", [128, TT], F32)
                rbcst = fsb("rbcst", [128, TT], F32)
                lnt = fsb("lnt", [128, TT], F32)
                for tile in range(NTILES):
                    t0 = tile * TT
                    ctx.W("vector", dsy, base_sy)
                    ctx.W("vector", dxa, dxa.n)
                    for dcm in range(DC):
                        xs = x_res[:, dcm * TOWN + t0: dcm * TOWN + t0 + TT]
                        ins = V.tensor_copy(xbf8[:, dcm * TT:(dcm + 1) * TT], xs)
                    cast_ms = ctx.inc(ins, ev)
                    ctx.W("tensor", ev, cast_ms)
                    for dcm in range(DC):
                        ins = T.matmul(pb[4][:, 0:TT], ones[:, :], xbf8[:, dcm * TT:(dcm + 1) * TT],
                                       start=(dcm == 0), stop=(dcm == DC - 1))
                    ctx.inc(ins, et)
                    sq_read_ms = None
                    for dcm in range(DC):
                        if sq_read_ms is not None:
                            ctx.W("vector", et, sq_read_ms)
                        sqv = ctx.inc(
                            V.tensor_mul(sq[:, :], x_res[:, dcm * TOWN + t0:dcm * TOWN + t0 + TT],
                                         x_res[:, dcm * TOWN + t0:dcm * TOWN + t0 + TT]), ev)
                        ctx.W("tensor", ev, sqv)
                        sq_read_ms = ctx.inc(
                            T.matmul(pb[5][:, 0:TT], ones[:, :], sq[:, :],
                                     start=(dcm == 0), stop=(dcm == DC - 1)), et)
                    sums_ms = sq_read_ms
                    ctx.W("vector", et, sums_ms)
                    V.tensor_scalar(mbc[:, :], pb[4][:, 0:TT], 1.0 / D, None, ALU.mult)
                    mb_ms = ctx.inc(V.tensor_scalar(ex2[:, :], pb[5][:, 0:TT], 1.0 / D, None,
                                                    ALU.mult), ev)
                    ctx.W("vector", ev, mb_ms)
                    v1 = ctx.inc(V.tensor_mul(lnt[:, :], mbc[:, :], mbc[:, :]), ev)
                    ctx.W("vector", ev, v1)
                    v2 = ctx.inc(V.scalar_tensor_tensor(lnt[:, :], ex2[:, :], 1e-5,
                                                        lnt[:, :], ALU.add, ALU.subtract), ev)
                    ctx.W("scalar", ev, v2)
                    scA = ctx.inc(SC.activation(lnt[:, :], lnt[:, :], ACT.Ln), esc)
                    ctx.W("scalar", esc, scA)
                    scB = ctx.inc(SC.activation(rbcst[:, :], lnt[:, :], ACT.Exp, scale=-0.5), esc)
                    for dcm in range(DC):
                        xs = x_res[:, dcm * TOWN + t0: dcm * TOWN + t0 + TT]
                        ins = V.tensor_sub(xbf8[:, dcm * TT:(dcm + 1) * TT], xs, mbc[:, :])
                    sub_ms = ctx.inc(ins, ev)
                    ctx.W("vector", ev, sub_ms)
                    ctx.W("vector", esc, scB)
                    for dcm in range(DC):
                        ins = V.tensor_mul(xln[:, dcm * TT:(dcm + 1) * TT],
                                           xbf8[:, dcm * TT:(dcm + 1) * TT], rbcst[:, :])
                    xln_ms = ctx.inc(ins, ev)
                    ctx.W("sync", ev, xln_ms)
                    SY.dma_start(out=cc_xa_in.ap()[tile], in_=xln[:, :]).then_inc(dxa.h, 16)
                    dxa.n += 16
                ctx.W("gpsimd", dxa, dxa.n)
                GP.collective_compute(
                    "AllGather", ALU.bypass, replica_groups=[list(range(N_CORES))],
                    ins=[cc_xa_in.ap().opt()], outs=[cc_xa_out.ap().opt()],
                ).then_inc(dcc.h, 1)
                dcc.n += 1
                ctx.barrier(skip=("dcc",))

        def emit_attention():
            with contextlib.ExitStack() as es:
                fsb = lambda name, shape, dt: es.enter_context(
                    nc.sbuf_tensor(f"at_{name}", shape, dt))
                wq_sb = fsb("wq", [128, 128], BF16)
                wk_sb = fsb("wk", [128, 128], BF16)
                wv_sb = fsb("wv", [128, 128], BF16)
                wq8 = fsb("wq8", [128, DC * 128], BF16)
                wk8 = fsb("wk8", [128, DC * 128], BF16)
                wv8 = fsb("wv8", [128, DC * 128], BF16)
                xa = [fsb("xa0", [128, DC * 512], BF16), fsb("xa1", [128, DC * 512], BF16)]
                qt_sb = fsb("qt", [128, S], BF16)
                kt_sb = fsb("kt", [128, S], BF16)
                v_tok = fsb("v_tok", [128, 32 * 144], BF16)
                vp_tok = fsb("vp_tok", [64, 64 * 72], BF16)
                vsum = fsb("vsum", [64, 72], BF16)
                ksum = fsb("ksum", [128, 64], BF16)
                kp_sb = fsb("kp", [64, S], BF16)
                qp_sb = fsb("qp", [64, S], BF16)
                o_acc = fsb("o_acc", [65, S], BF16)
                o_stage = fsb("o_stage", [64, S], BF16)
                zbuf = fsb("zbuf", [66, S], F32)
                pexp = fsb("pexp", [128, 512], BF16)
                pt_sb = fsb("pt", [128, 512], BF16)
                esb = fsb("esb", [128, 512], BF16)
                pts = fsb("pts", [128, 512], BF16)
                psum_ex = fsb("psum_ex", [64, 256], BF16)

                for w_dram, w_s in ((wqT, wq8), (wkT, wk8), (wvT, wv8)):
                    GP.dma_start(out=w_s[:, :].rearrange("p (c f) -> p c f", c=DC),
                                 in_=w_dram.rearrange("(c p) f -> p c f", p=128)).then_inc(dgp.h, 16)
                    dgp.n += 16
                wload = dgp.n
                ones_ms = ctx.inc(V.memset(
                    v_tok[:, :].rearrange("p (s w) -> p s w", w=144)[:, :, 64:65], 1.0), ev)
                ones_ms = ctx.inc(V.memset(
                    v_tok[:, :].rearrange("p (s w) -> p s w", w=144)[:, :, 136:137], 1.0), ev)

                for b in range(B):
                    for tg in range(8):
                        slot = tg % 2
                        shard = 2 * b + (tg // 4)
                        ctx.W("sync", dcc, 1)
                        ctx.W("sync", et, ctx.marks.get(("xaw", slot), 0))
                        SY.dma_start(
                            out=xa[slot][:, :],
                            in_=cc_xa_out.ap()[shard, tg % 4],
                        ).then_inc(dx[slot].h, 16)
                        dx[slot].n += 16
                        ctx.W("tensor", dx[slot], dx[slot].n)
                        ctx.W("tensor", dgp, wload)
                        ctx.W("tensor", ev, ctx.marks.get(("qkv_evac", tg % 2), 0))
                        ctx.W("tensor", esc, ctx.marks.get(("qkv_evac_sc", tg % 2), 0))
                        qbQ, qbK, qbV = pb[3 * (tg % 2)], pb[3 * (tg % 2) + 1], pb[3 * (tg % 2) + 2]
                        for dcm in range(DC):
                            T.matmul(qbQ[:, 0:512], wq8[:, dcm * 128:(dcm + 1) * 128],
                                     xa[slot][:, dcm * 512:(dcm + 1) * 512],
                                     start=(dcm == 0), stop=(dcm == DC - 1))
                        for dcm in range(DC):
                            T.matmul(qbK[:, 0:512], wk8[:, dcm * 128:(dcm + 1) * 128],
                                     xa[slot][:, dcm * 512:(dcm + 1) * 512],
                                     start=(dcm == 0), stop=(dcm == DC - 1))
                        for sub in range(4):
                            for dcm in range(DC):
                                ins = T.matmul(qbV[:, sub * 128:(sub + 1) * 128],
                                               xa[slot][:, dcm * 512 + sub * 128:
                                                        dcm * 512 + sub * 128 + 128],
                                               wv8[:, dcm * 128:(dcm + 1) * 128],
                                               start=(dcm == 0), stop=(dcm == DC - 1))
                        mm_ms = ctx.inc(ins, et)
                        ctx.marks[("xaw", slot)] = mm_ms
                        tcol = tg * 512
                        ctx.W("vector", et, mm_ms)
                        ctx.W("vector", ev, ones_ms)
                        ctx.W("scalar", et, mm_ms)
                        ctx.W("scalar", ev, ones_ms)
                        V.tensor_scalar(qt_sb[:, tcol:tcol + 512], qbQ[:, 0:512], 0.125,
                                        None, ALU.mult)
                        sc_ins = SC.copy(kt_sb[:, tcol:tcol + 512], qbK[:, 0:512])
                        for sub in range(4):
                            kt_i = tg * 4 + sub
                            V.tensor_copy(v_tok[:, kt_i * 144:kt_i * 144 + 64],
                                          qbV[0:128, sub * 128:sub * 128 + 64])
                            ins = V.tensor_copy(v_tok[:, kt_i * 144 + 72:kt_i * 144 + 136],
                                                qbV[0:128, sub * 128 + 64:sub * 128 + 128])
                        ev_ms = ctx.inc(ins, ev)
                        sc_ms = ctx.inc(sc_ins, esc)
                        ctx.marks[("qkv_evac", tg % 2)] = ev_ms
                        ctx.marks[("qkv_evac_sc", tg % 2)] = sc_ms
                    qkv_ms = ev.n
                    ctx.marks[("qkvsc",)] = esc.n

                    for h in range(2):
                        p0 = 0 if h == 0 else 64
                        tp = None if h == 0 else (64, 0)
                        mmulT = (mAT, mA0T) if h == 0 else (mBT, mB0T)

                        if h == 0 and ATT["strided"] >= 1:
                            ctx.W("vector", ev, qkv_ms)
                            ctx.W("vector", esc, ctx.marks.get(("qkvsc",), 0))
                            ctx.W("vector", et, et.n)
                            kpv = ctx.inc(V.tensor_copy(
                                kp_sb[:, :].rearrange("p (r bj) -> p r bj", r=64),
                                kt_sb[0:64, :].rearrange("p (bj r) -> p r bj", bj=64)), ev)
                            qpv = ctx.inc(V.tensor_copy(
                                qp_sb[:, :].rearrange("p (r bj) -> p r bj", r=64),
                                qt_sb[0:64, :].rearrange("p (bj r) -> p r bj", bj=64)), ev)
                            ctx.W("sync", ev, qkv_ms)
                            ctx.W("sync", esc, ctx.marks.get(("qkvsc",), 0))
                            ctx.W("sync", et, et.n)
                            ctx.W("sync", dvp, dvp.n)
                            SY.dma_start(
                                out=vperm_dr.ap().rearrange("(s p) f -> p s f", p=128)[:, :, 0:65],
                                in_=v_tok[:, :].rearrange("p (s w) -> p s w", w=144)[:, :, 0:65],
                            ).then_inc(dvp.h, 16)
                            SY.dma_start(
                                out=vperm_dr.ap().rearrange("(s p) f -> p s f", p=128)[:, :, 65:130],
                                in_=v_tok[:, :].rearrange("p (s w) -> p s w", w=144)[:, :, 72:137],
                            ).then_inc(dvp.h, 16)
                            dvp.n += 32
                            ctx.W("sync", dvp, dvp.n)
                            SY.dma_start(
                                out=vp_tok[:, :].rearrange("p (s w) -> p s w", w=72)[:, :, 0:65],
                                in_=vperm_dr.ap()
                                    .rearrange("(bj r) f -> bj r f", r=64)[:, :, 0:65],
                            ).then_inc(dvp.h, 16)
                            dvp.n += 16
                        elif h == 1 and ATT["summary"]:
                            ctx.W("vector", ev, qkv_ms)
                            ctx.W("vector", esc, ctx.marks.get(("qkvsc",), 0))
                            ctx.W("vector", et, et.n)
                            ksv = ctx.inc(V.tensor_copy(
                                ksum[64:128, :],
                                kt_sb[64:128, :].rearrange("p (c r) -> p c r", r=64)[:, :, 63]), ev)
                            ctx.W("sync", ev, qkv_ms)
                            ctx.W("sync", esc, ctx.marks.get(("qkvsc",), 0))
                            ctx.W("sync", et, et.n)
                            ctx.W("sync", dvp, dvp.n)
                            SY.dma_start(
                                out=vsum[:, 0:65],
                                in_=vperm_dr.ap()
                                    .rearrange("(c u) f -> c u f", u=64)[:, 63, 65:130],
                            ).then_inc(dvs.h, 16)
                            dvs.n += 16

                        if not ATT["passb"]:
                            continue
                        ctx.W("tensor", ev, qkv_ms)
                        ctx.W("tensor", esc, ctx.marks.get(("qkvsc",), 0))
                        if h == 1:
                            ctx.W("tensor", dvs, dvs.n)
                            ctx.W("tensor", ev, ev.n)
                        mk2 = (mAT2c, mAT2c0) if h == 0 else (mBT2c, mBT2c0)
                        for qp2 in range(NQT // 2):
                            bank = pb[2 + (qp2 % 2)]
                            bo = pb[4 + (qp2 % 2)]
                            ctx.W("tensor", esc, ctx.marks.get(("pbexp", qp2 % 2), 0))
                            for qh in range(2):
                                qt = 2 * qp2 + qh
                                kc0 = 0 if qt == 0 else (qt - 1) * 128
                                for half in range(2):
                                    ins = T.matmul(
                                        bank[:, qh * 256 + half * 128:qh * 256 + half * 128 + 128],
                                        kt_sb[p0:p0 + 64,
                                              kc0 + half * 128:kc0 + half * 128 + 128],
                                        qt_sb[p0:p0 + 64, qt * 128:(qt + 1) * 128],
                                        start=True, stop=True, tile_position=tp)
                            if h == 1 and ATT["summary"]:
                                ctx.W("tensor", esc, ctx.marks.get(("sumexS",), 0))
                                for qh in range(2):
                                    qt = 2 * qp2 + qh
                                    ins = T.matmul(pb[6][0:64, qh * 128:(qh + 1) * 128],
                                                   ksum[64:128, :],
                                                   qt_sb[64:128, qt * 128:(qt + 1) * 128],
                                                   start=True, stop=True, tile_position=(64, 0))
                            pbm = ctx.inc(ins, et)
                            ctx.W("scalar", et, pbm)
                            ctx.W("scalar", ev, ctx.marks.get(("pbpt", qp2 % 2), 0))
                            ex1 = ctx.inc(SC.activation(pexp[:, :], bank[:, 0:512], ACT.Exp), esc)
                            ctx.marks[("pbexp", qp2 % 2)] = ex1
                            ctx.W("vector", esc, ex1)
                            mkT = mk2[1] if qp2 == 0 else mk2[0]
                            pt1 = ctx.inc(V.tensor_mul(pt_sb[:, :], pexp[:, :], mkT[:, :]), ev)
                            ctx.marks[("pbpt", qp2 % 2)] = pt1
                            if h == 1 and ATT["summary"]:
                                ctx.W("scalar", ev, ctx.marks.get(("sumex",), 0))
                                sx = ctx.inc(SC.activation(psum_ex[:, :], pb[6][0:64, 0:256],
                                                           ACT.Exp), esc)
                                ctx.marks[("sumexS",)] = sx
                                ctx.W("vector", esc, sx)
                                ctx.marks[("sumex",)] = ctx.inc(V.tensor_mul(
                                    psum_ex[:, :], psum_ex[:, :],
                                    msum[:, qp2 * 256:(qp2 + 1) * 256]), ev)
                            ctx.W("tensor", ev, pt1)
                            if h == 1 and ATT["summary"]:
                                ctx.W("tensor", ev, ctx.marks.get(("sumex",), 0))
                            ctx.W("tensor", ev, ctx.marks.get(("poev", qp2 % 2), 0))
                            for qh in range(2):
                                qt = 2 * qp2 + qh
                                kt0 = 0 if qt == 0 else qt - 1
                                for half in range(2):
                                    ktile = kt0 + half
                                    ins = T.matmul(
                                        bo[0:65, qh * 128:(qh + 1) * 128],
                                        v_tok[:, ktile * 144 + h * 72:ktile * 144 + h * 72 + 65],
                                        pt_sb[:, qh * 256 + half * 128:
                                              qh * 256 + half * 128 + 128],
                                        start=(half == 0),
                                        stop=(half == 1 and (h == 0 or not ATT["summary"])))
                                if h == 1 and ATT["summary"]:
                                    ins = T.matmul(bo[0:65, qh * 128:(qh + 1) * 128],
                                                   vsum[:, 0:65],
                                                   psum_ex[:, qh * 128:(qh + 1) * 128],
                                                   start=False, stop=True)
                            pv_ms = ctx.inc(ins, et)
                            ctx.W("vector", et, pv_ms)
                            ctx.marks[("poev", qp2 % 2)] = ctx.inc(
                                V.tensor_copy(o_acc[:, qp2 * 256:(qp2 + 1) * 256],
                                              bo[0:65, 0:256]), ev)

                        if h == 0 and ATT["strided"] >= 2:
                            ctx.W("tensor", ev, ev.n)
                            ctx.W("tensor", esc, esc.n)
                            ctx.W("tensor", dvp, dvp.n)
                            for rg in range(8):
                                bs = pb[2 + (rg % 2)]
                                bos = pb[4 + (rg % 2)]
                                ctx.W("tensor", esc, ctx.marks.get(("strex", rg % 2), 0))
                                ctx.W("tensor", ev, ctx.marks.get(("stradd", rg % 2), 0))
                                for rr in range(8):
                                    r = rg * 8 + rr
                                    ins = T.matmul(bs[0:64, rr * 64:(rr + 1) * 64],
                                                   kp_sb[:, r * 64:(r + 1) * 64],
                                                   qp_sb[:, r * 64:(r + 1) * 64],
                                                   start=True, stop=True)
                                sm = ctx.inc(ins, et)
                                ctx.W("scalar", et, sm)
                                ctx.W("scalar", ev, ctx.marks.get(("strpt", rg % 2), 0))
                                se = ctx.inc(SC.activation(esb[0:64, :], bs[0:64, 0:512],
                                                           ACT.Exp), esc)
                                ctx.marks[("strex", rg % 2)] = se
                                ctx.W("vector", esc, se)
                                sp = ctx.inc(V.tensor_mul(pts[0:64, :], esb[0:64, :],
                                                          mstr[0:64, :]), ev)
                                ctx.marks[("strpt", rg % 2)] = sp
                                ctx.W("tensor", ev, sp)
                                if ATT["strided"] < 3:
                                    ctx.marks[("stradd", rg % 2)] = sp
                                    continue
                                for rr in range(8):
                                    r = rg * 8 + rr
                                    ins = T.matmul(bos[0:65, rr * 64:(rr + 1) * 64],
                                                   vp_tok[:, r * 72:r * 72 + 65],
                                                   pts[0:64, rr * 64:(rr + 1) * 64],
                                                   start=True, stop=True)
                                spv = ctx.inc(ins, et)
                                ctx.W("vector", et, spv)
                                oacc_v = o_acc[:, :].rearrange("p (qi r) -> p r qi",
                                                               r=64)[:, rg * 8:(rg + 1) * 8, :]
                                ctx.marks[("stradd", rg % 2)] = ctx.inc(V.tensor_add(
                                    oacc_v, oacc_v,
                                    bos[0:65, 0:512].rearrange("p (rr qi) -> p rr qi", rr=8)), ev)

                        if not ATT["norm"]:
                            continue
                        ctx.W("vector", ev, ev.n)
                        nz0 = ctx.inc(V.tensor_copy(zbuf[64:65, :], o_acc[64:65, :]), ev)
                        ctx.W("scalar", ev, nz0)
                        z1 = ctx.inc(SC.activation(zbuf[64:65, :], zbuf[64:65, :], ACT.Ln), esc)
                        ctx.W("scalar", esc, z1)
                        z2 = ctx.inc(SC.activation(zbuf[64:65, :], zbuf[64:65, :], ACT.Exp,
                                                   scale=-1.0), esc)
                        ctx.W("tensor", esc, z2)
                        ctx.W("vector", dco, dco.n)
                        for g in range(8):
                            bz = pb[2 + (g % 2)]
                            ctx.W("tensor", ev, ctx.marks.get(("zev", g % 2), 0))
                            zm = ctx.inc(T.matmul(bz[0:64, 0:512], onesf[64:65, 0:64],
                                                  zbuf[64:65, g * 512:(g + 1) * 512],
                                                  start=True, stop=True), et)
                            ctx.W("vector", et, zm)
                            ctx.marks[("zev", g % 2)] = ctx.inc(
                                V.tensor_mul(o_stage[:, g * 512:(g + 1) * 512],
                                             o_acc[0:64, g * 512:(g + 1) * 512],
                                             bz[0:64, 0:512]), ev)
                        ctx.W("sync", ev, ev.n)
                        SY.dma_start(out=cc_o_in.ap()[b, h], in_=o_stage[:, :]).then_inc(dco.h, 16)
                        dco.n += 16
                    if ATT["ag2"]:
                        ctx.W("gpsimd", dco, dco.n)
                        GP.collective_compute(
                            "AllGather", ALU.bypass, replica_groups=[list(range(N_CORES))],
                            ins=[cc_o_in.ap()[b].opt()], outs=[cc_o_out.ap()[b].opt()],
                        ).then_inc(dcc.h, 1)
                        dcc.n += 1
                    ctx.barrier(engines=("tensor", "vector", "scalar"), skip=("dcc",))
                ctx.barrier(skip=("dcc",))

        def emit_wo():
            with contextlib.ExitStack() as es:
                fsb = lambda name, shape, dt: es.enter_context(
                    nc.sbuf_tensor(f"wo_{name}", shape, dt))
                wo_sb = fsb("wo", [128, DC * D], BF16)
                og = [fsb("og0", [128, DC * 512], BF16), fsb("og1", [128, DC * 512], BF16)]
                GP.dma_start(out=wo_sb[:, :].rearrange("p (c f) -> p c f", c=DC),
                             in_=woT.rearrange("(c p) f -> p c f", p=128)).then_inc(dgp.h, 16)
                dgp.n += 16
                wo_ld = dgp.n
                pid = SY.partition_id()
                row0 = SY.compute_val((pid // 2) * (N_CORES * 2 * 64))
                colb = SY.compute_val((pid % 2) * TOWN)
                flat = cc_o_out.ap().rearrange("b n h d t -> (b n h d) t")
                ctx.W("sync", dcc, dcc.n)
                for ts in range(4):
                    slot = ts % 2
                    ctx.W("sync", et, ctx.marks.get(("ogw", slot), 0))
                    col = SY.compute_val(colb + ts * 512)
                    SY.dma_start(
                        out=og[slot][:, :].rearrange("p (c t) -> p c t", c=DC),
                        in_=flat[bass.ds(row0, 1024), bass.ds(col, 512)]
                            .rearrange("(c p) t -> p c t", p=128),
                    ).then_inc(dol[slot].h, 16)
                    dol[slot].n += 16
                    ctx.W("tensor", dol[slot], dol[slot].n)
                    ctx.W("tensor", dgp, wo_ld)
                    t0 = ts * 512
                    for oc in range(DC):
                        bank = pb[4 + (oc % 2)]
                        ctx.W("tensor", ev, ctx.marks.get(("woev", oc % 2), 0))
                        for ic in range(DC):
                            ins = T.matmul(bank[:, 0:512],
                                           wo_sb[:, ic * D + oc * 128:ic * D + oc * 128 + 128],
                                           og[slot][:, ic * 512:(ic + 1) * 512],
                                           start=(ic == 0), stop=(ic == DC - 1))
                        wm = ctx.inc(ins, et)
                        ctx.marks[("ogw", slot)] = wm
                        ctx.W("vector", et, wm)
                        ctx.marks[("woev", oc % 2)] = ctx.inc(V.tensor_add(
                            x_res[:, oc * TOWN + t0: oc * TOWN + t0 + 512],
                            x_res[:, oc * TOWN + t0: oc * TOWN + t0 + 512],
                            bank[:, 0:512]), ev)
                ctx.barrier()

        if PHASES.get("ffn1"):
            emit_ffn("f1")
        if PHASES.get("attn"):
            emit_ln2_and_ag1()
            emit_attention()
            if ATT["wo"]:
                emit_wo()
        if PHASES.get("ffn2"):
            emit_ffn("f2")

        ctx.barrier()
        SY.dma_start(out=yT.rearrange("(c p) t -> p c t", p=128),
                     in_=x_res[:, :].rearrange("p (c t) -> p c t", c=DC)).then_inc(dst.h, 16)
        dst.n += 16
        ctx.W("sync", dst, dst.n)

    return nc


_NC = None


def _in_maps(inputs):
    x = np.asarray(inputs["L0_attn_x"], dtype=np.float32).reshape(B * S, D)
    Wq = np.asarray(inputs["L0_attn_Wq"], dtype=np.float32)
    Wk = np.asarray(inputs["L0_attn_Wk"], dtype=np.float32)
    Wv = np.asarray(inputs["L0_attn_Wv"], dtype=np.float32)
    Wo = np.asarray(inputs["L0_attn_Wo"], dtype=np.float32)
    rw_ = np.ascontiguousarray(np.asarray(inputs["L0_ffn_router_w"], dtype=np.float32))
    rb_ = np.asarray(inputs["L0_ffn_router_b"], dtype=np.float32).reshape(NROUTED, 1)
    w1_ = np.asarray(inputs["L0_ffn_routed_w1"], dtype=np.float32)
    w3_ = np.asarray(inputs["L0_ffn_routed_w3"], dtype=np.float32)
    w2_ = np.asarray(inputs["L0_ffn_routed_w2"], dtype=np.float32)
    s1_ = np.asarray(inputs["L0_ffn_shared_w1"], dtype=np.float32)
    s3_ = np.asarray(inputs["L0_ffn_shared_w3"], dtype=np.float32)
    s2_ = np.asarray(inputs["L0_ffn_shared_w2"], dtype=np.float32)
    def _w13_part(wa):
        return wa.reshape(-1, DC, 128, HFF).transpose(0, 2, 1, 3).reshape(-1, 128, DC * HFF)
    w13cat = np.concatenate([
        np.concatenate([_w13_part(w1_), _w13_part(w3_)], axis=2),
        np.concatenate([_w13_part(s1_), _w13_part(s3_)], axis=2)], axis=0)
    w13cat = np.ascontiguousarray(w13cat)
    def _w2_part(wa):
        return wa.reshape(-1, 4, 128, DC, 128).transpose(0, 2, 1, 3, 4)
    w2cat = np.concatenate([_w2_part(w2_), _w2_part(s2_)], axis=0)
    w2cat = np.ascontiguousarray(w2cat.reshape(NEXP, 128, 4 * DC * 128))
    WqT, WkT, WvT, WoT = Wq.T, Wk.T, Wv.T, Wo.T
    perm = np.concatenate([np.r_[64 * c:64 * c + 64, 64 * (c + 8):64 * (c + 8) + 64]
                           for c in range(N_CORES)])
    WoTp = np.ascontiguousarray(WoT[perm, :])
    maps = []
    for c in range(N_CORES):
        cols = np.r_[64 * c:64 * c + 64, 64 * (c + 8):64 * (c + 8) + 64]
        maps.append({
            "xT": np.ascontiguousarray(x[c * TOWN:(c + 1) * TOWN].T),
            "wqT": np.ascontiguousarray(WqT[:, cols]),
            "wkT": np.ascontiguousarray(WkT[:, cols]),
            "wvT": np.ascontiguousarray(WvT[:, cols]),
            "woT": WoTp,
            "rw": rw_, "rbc": rb_, "w13d": w13cat, "w2d": w2cat,
        })
    return maps


def kernel(**inputs):
    global _NC
    if _NC is None:
        _NC = build_nc()
    maps = _in_maps(inputs)
    res = run_bass_kernel_spmd(_NC, maps, core_ids=list(range(N_CORES)))
    out = np.empty((B * S, D), dtype=np.float32)
    for c in range(N_CORES):
        out[c * TOWN:(c + 1) * TOWN] = np.asarray(res.results[c]["yT"]).T
    return out.reshape(B, S, D)


# version 11
# speedup vs baseline: 1.2618x; 1.0130x over previous
import contextlib
import numpy as np
import ml_dtypes
import concourse.bass as bass
import concourse.mybir as mybir
from concourse.bass_utils import run_bass_kernel_spmd

F32 = mybir.dt.float32
BF16 = mybir.dt.bfloat16
ALU = mybir.AluOpType
ACT = mybir.ActivationFunctionType
AX = mybir.AxisListType.X
BF = ml_dtypes.bfloat16

N_CORES = 8
D = 1024
DC = 8
HFF = 512
NEXP = 10
NROUTED = 8
B = 4
S = 4096
TOWN = 2048
TT = 512
NTILES = TOWN // TT
NQT = S // 128
NEG = -1.0e9

PHASES = {"ffn1": True, "attn": True, "ffn2": True}
DEBUG = False
ATT = {"qkv": True, "passa": True, "passb": True, "summary": True,
       "strided": 3, "norm": True, "ag2": True, "wo": True}


def _masks_np():
    r = np.arange(128)[:, None]
    c = np.arange(256)[None, :]
    okA = (c - r >= 65) & (c - r <= 128)
    okA0 = (r - c >= 0) & (r - c < 64)
    cc = c - 128
    okB = (c >= 128) & (cc // 64 == r // 64) & (cc <= r)
    okB0 = (c < 128) & (c // 64 == r // 64) & (c <= r)

    def addm(ok):
        return np.where(ok, 0.0, NEG).astype(np.float32)

    def mulT(ok):
        m = np.zeros((128, 256), dtype=BF)
        m[:, 0:128] = ok[:, 0:128].T.astype(BF)
        m[:, 128:256] = ok[:, 128:256].T.astype(BF)
        return m

    msum = np.zeros((64, NQT * 128), dtype=BF)
    cc64 = np.arange(64)[:, None]
    rr = np.arange(128)[None, :]
    for qt in range(NQT):
        msum[:, qt * 128:(qt + 1) * 128] = (cc64 < 2 * qt + (rr >= 64)).astype(BF)
    bj = np.arange(64)[:, None]
    qi = np.arange(64)[None, :]
    m1 = (bj < qi).astype(BF)
    mstr = np.zeros((128, 512), dtype=BF)
    for g in range(8):
        mstr[0:64, g * 64:(g + 1) * 64] = m1
        mstr[64:128, g * 64:(g + 1) * 64] = m1
    return (addm(okA), addm(okA0), addm(okB), addm(okB0),
            mulT(okA), mulT(okA0), mulT(okB), mulT(okB0), msum, mstr)


class Ctx:
    def __init__(self, nc, es):
        self.nc = nc
        self.es = es
        self.sems = {}
        self.lastw = {}
        self.marks = {}

    def sem(self, name):
        if name not in self.sems:
            h = self.es.enter_context(self.nc.semaphore(name))
            s = type("S", (), {})()
            s.h, s.n, s.name = h, 0, name
            self.sems[name] = s
        return self.sems[name]

    def W(self, eng, s, val):
        if val is None or val <= 0:
            return
        key = (eng, s.name)
        if self.lastw.get(key, -1) >= val:
            return
        getattr(self.nc, eng).wait_ge(s.h, val)
        self.lastw[key] = val

    def inc(self, instr, s, k=1):
        instr.then_inc(s.h, k)
        s.n += k
        return s.n

    def barrier(self, engines=("tensor", "vector", "scalar", "sync", "gpsimd"), skip=()):
        for e in engines:
            for s in self.sems.values():
                if s.name in skip:
                    continue
                self.W(e, s, s.n)


def build_nc():
    nc = bass.Bass("TRN2", target_bir_lowering=False)
    T, V, SC, SY, GP = nc.tensor, nc.vector, nc.scalar, nc.sync, nc.gpsimd

    xT = nc.dram_tensor("xT", [D, TOWN], F32, kind="ExternalInput").ap()
    wqT = nc.dram_tensor("wqT", [D, 128], F32, kind="ExternalInput").ap()
    wkT = nc.dram_tensor("wkT", [D, 128], F32, kind="ExternalInput").ap()
    wvT = nc.dram_tensor("wvT", [D, 128], F32, kind="ExternalInput").ap()
    woT = nc.dram_tensor("woT", [D, D], F32, kind="ExternalInput").ap()
    rw = nc.dram_tensor("rw", [D, NROUTED], F32, kind="ExternalInput").ap()
    rbc = nc.dram_tensor("rbc", [NROUTED, 1], F32, kind="ExternalInput").ap()
    w13d = nc.dram_tensor("w13d", [NEXP, 128, 2 * DC * HFF], F32, kind="ExternalInput").ap()
    w2d = nc.dram_tensor("w2d", [NEXP, 128, 4 * DC * 128], F32, kind="ExternalInput").ap()
    yT = nc.dram_tensor("yT", [D, TOWN], F32, kind="ExternalOutput").ap()

    if DEBUG:
        dbg_t = {
            "dbg_xln": nc.dram_tensor("dbg_xln", [128, DC * TT], BF16),
            "dbg_l": nc.dram_tensor("dbg_l", [NROUTED, TT], BF16),
            "dbg_g": nc.dram_tensor("dbg_g", [NROUTED, TT], BF16),
            "dbg_hh": nc.dram_tensor("dbg_hh", [128, NEXP * 4 * TT], BF16),
        }
    cc_xa_in = nc.dram_tensor("cc_xa_in", [NTILES, 128, DC * TT], BF16)
    cc_xa_out = nc.dram_tensor("cc_xa_out", [NTILES, N_CORES, 128, DC * TT], BF16,
                               addr_space="Shared")
    vperm_dr = nc.dram_tensor("vperm_dr", [S, 130], BF16)
    cc_o_in = nc.dram_tensor("cc_o_in", [B, 2, 64, S], BF16)
    cc_o_out = nc.dram_tensor("cc_o_out", [B, N_CORES, 2, 64, S], BF16, addr_space="Shared")

    (mA_np, mA0_np, mB_np, mB0_np, mAT_np, mA0T_np, mBT_np, mB0T_np,
     msum_np, mstr_np) = _masks_np()
    oneh_np = np.zeros((NROUTED, NROUTED * 128), dtype=BF)
    for e in range(NROUTED):
        oneh_np[e, e * 128:(e + 1) * 128] = 1.0
    mAT2 = np.concatenate([mAT_np, mAT_np], axis=1)
    mAT2_0 = np.concatenate([mA0T_np, mAT_np], axis=1)
    mBT2 = np.concatenate([mBT_np, mBT_np], axis=1)
    mBT2_0 = np.concatenate([mB0T_np, mBT_np], axis=1)
    consts = [
        ("c_ident", np.eye(128, dtype=BF)), ("c_ones", np.ones((128, 128), dtype=BF)),
        ("c_mAT2", mAT2), ("c_mAT2_0", mAT2_0), ("c_mBT2", mBT2), ("c_mBT2_0", mBT2_0),
        ("c_oneh", oneh_np),
        ("c_onesf", np.ones((128, 64), dtype=np.float32)),
        ("c_mA", mA_np), ("c_mA0", mA0_np), ("c_mB", mB_np), ("c_mB0", mB0_np),
        ("c_mAT", mAT_np), ("c_mA0T", mA0T_np), ("c_mBT", mBT_np), ("c_mB0T", mB0T_np),
        ("c_msum", msum_np), ("c_mstr", mstr_np),
    ]
    c_handles = {n: nc.inline_tensor(a, name=n) for n, a in consts}
    c_shapes = {n: a for n, a in consts}

    es_top = contextlib.ExitStack()
    with es_top:
        ctx = Ctx(nc, es_top)
        sb = lambda name, shape, dt: es_top.enter_context(nc.sbuf_tensor(name, shape, dt))

        x_res = sb("x_res", [128, DC * TOWN], F32)
        csb = {}
        for n, a in consts:
            dt = BF16 if a.dtype == BF else F32
            csb[n] = sb(n[2:], list(a.shape), dt)
        ident, ones = csb["c_ident"], csb["c_ones"]
        mA, mA0, mB, mB0 = csb["c_mA"], csb["c_mA0"], csb["c_mB"], csb["c_mB0"]
        mAT, mA0T, mBT, mB0T = csb["c_mAT"], csb["c_mA0T"], csb["c_mBT"], csb["c_mB0T"]
        msum, mstr = csb["c_msum"], csb["c_mstr"]
        oneh = csb["c_oneh"]
        onesf = csb["c_onesf"]
        mAT2c, mAT2c0, mBT2c, mBT2c0 = (csb["c_mAT2"], csb["c_mAT2_0"],
                                        csb["c_mBT2"], csb["c_mBT2_0"])
        rw_sb = sb("rw_sb", [128, DC * NROUTED], BF16)
        rb_sb = sb("rb_sb", [NROUTED, 1], F32)

        pb = [es_top.enter_context(nc.psum_tensor(f"pb{i}", [128, 512], F32)) for i in range(7)]
        pbT = es_top.enter_context(nc.psum_tensor("pbT", [128, 1024], BF16))

        et, ev, esc = ctx.sem("et"), ctx.sem("ev"), ctx.sem("esc")
        dsy = ctx.sem("dsy")
        dgp = ctx.sem("dgp")
        dw = [ctx.sem("dw0"), ctx.sem("dw1")]
        dwx = [ctx.sem("dwa"), ctx.sem("dwb")]
        dx = [ctx.sem("dx0"), ctx.sem("dx1")]
        dmr = ctx.sem("dmr")
        dvp = ctx.sem("dvp")
        dvs = ctx.sem("dvs")
        dco = ctx.sem("dco")
        dxa = ctx.sem("dxa")
        dcc = ctx.sem("dcc")
        dol = [ctx.sem("dol0"), ctx.sem("dol1")]
        dst = ctx.sem("dst")

        SY.dma_start(out=x_res[:, :].rearrange("p (c t) -> p c t", c=DC),
                     in_=xT.rearrange("(c p) t -> p c t", p=128)).then_inc(dsy.h, 16)
        dsy.n += 16
        for n, _ in consts:
            SY.dma_start(out=csb[n][:, :], in_=c_handles[n].ap()).then_inc(dsy.h, 16)
            dsy.n += 16
        SY.dma_start(out=rb_sb[:, :], in_=rbc).then_inc(dsy.h, 16)
        dsy.n += 16
        GP.dma_start(out=rw_sb[:, :].rearrange("p (c e) -> p c e", c=DC),
                     in_=rw.rearrange("(c p) e -> p c e", p=128)).then_inc(dgp.h, 16)
        dgp.n += 16
        base_sy = dsy.n
        base_gp = dgp.n

        def emit_ffn(tag):
            with contextlib.ExitStack() as es:
                fsb = lambda name, shape, dt: es.enter_context(
                    nc.sbuf_tensor(f"{tag}_{name}", shape, dt))
                xln = fsb("xln", [128, DC * TOWN], BF16)
                sq = fsb("sq", [128, TT], BF16)
                mbc = fsb("mbc", [128, TT], F32)
                ex2 = fsb("ex2", [128, TT], F32)
                rbcst = fsb("rbcst", [128, TT], F32)
                lnt = fsb("lnt", [128, TT], F32)
                l_sb = fsb("l_sb", [NROUTED, TOWN], BF16)
                topf = fsb("topf", [128, 48], F32)
                ge_sb = fsb("ge_sb", [128, NROUTED], F32)
                gbf = fsb("gbf", [128, NROUTED], BF16)
                g_sb = fsb("g_sb", [NROUTED, TOWN], BF16)
                silu_sb = fsb("silu_sb", [128, 2 * TT], BF16)
                hh = fsb("hh", [128, 16 * TT], BF16)
                w13 = [fsb("w13_0", [128, 2 * DC * HFF], BF16),
                       fsb("w13_1", [128, 2 * DC * HFF], BF16)]
                w2b = [fsb("w2b_0", [128, 4 * DC * 128], BF16),
                       fsb("w2b_1", [128, 4 * DC * 128], BF16)]

                w13_expect = [0, 0]
                w2_expect = [0, 0]

                def issue_w(e, slot, wait_et):
                    ctx.W("gpsimd", et, wait_et)
                    GP.dma_start(out=w13[slot][:, :], in_=w13d[e]).then_inc(dw[slot].h, 16)
                    dw[slot].n += 16
                    w13_expect[slot] = dw[slot].n
                    GP.dma_start(out=w2b[slot][:, :], in_=w2d[e]).then_inc(dwx[slot].h, 16)
                    dwx[slot].n += 16
                    w2_expect[slot] = dwx[slot].n

                issue_w(NROUTED, 0, 0)
                issue_w(NROUTED + 1, 1, 0)
                w13_loaded = {0: w13_expect[0], 1: w13_expect[1]}
                w2_loaded = {0: w2_expect[0], 1: w2_expect[1]}

                ctx.W("vector", dsy, base_sy)
                ctx.W("vector", et, et.n)
                for ts in range(4):
                    t0 = ts * TT
                    for dcm in range(DC):
                        xs = x_res[:, dcm * TOWN + t0: dcm * TOWN + t0 + TT]
                        ins = V.tensor_copy(xln[:, dcm * TOWN + t0:dcm * TOWN + t0 + TT], xs)
                    cast_ms = ctx.inc(ins, ev)
                    ctx.W("tensor", ev, cast_ms)
                    ctx.W("tensor", ev, ctx.marks.get(("lnap", ts), 0))
                    for dcm in range(DC):
                        ins = T.matmul(pb[4][:, 0:TT], ones[:, :],
                                       xln[:, dcm * TOWN + t0:dcm * TOWN + t0 + TT],
                                       start=(dcm == 0), stop=(dcm == DC - 1))
                    ctx.inc(ins, et)
                    sq_read_ms = None
                    for dcm in range(DC):
                        if sq_read_ms is not None:
                            ctx.W("vector", et, sq_read_ms)
                        sqv = ctx.inc(
                            V.tensor_mul(sq[:, :], x_res[:, dcm * TOWN + t0:dcm * TOWN + t0 + TT],
                                         x_res[:, dcm * TOWN + t0:dcm * TOWN + t0 + TT]), ev)
                        ctx.W("tensor", ev, sqv)
                        sq_read_ms = ctx.inc(
                            T.matmul(pb[5][:, 0:TT], ones[:, :], sq[:, :],
                                     start=(dcm == 0), stop=(dcm == DC - 1)), et)
                    sums_ms = sq_read_ms
                    ctx.W("vector", et, sums_ms)
                    V.tensor_scalar(mbc[:, :], pb[4][:, 0:TT], 1.0 / D, None, ALU.mult)
                    mb_ms = ctx.inc(V.tensor_scalar(ex2[:, :], pb[5][:, 0:TT], 1.0 / D, None,
                                                    ALU.mult), ev)
                    ctx.W("vector", ev, mb_ms)
                    v1 = ctx.inc(V.tensor_mul(lnt[:, :], mbc[:, :], mbc[:, :]), ev)
                    ctx.W("vector", ev, v1)
                    v2 = ctx.inc(V.scalar_tensor_tensor(lnt[:, :], ex2[:, :], 1e-5,
                                                        lnt[:, :], ALU.add, ALU.subtract), ev)
                    ctx.W("scalar", ev, v2)
                    sc1 = ctx.inc(SC.activation(lnt[:, :], lnt[:, :], ACT.Ln), esc)
                    ctx.W("scalar", esc, sc1)
                    sc2 = ctx.inc(SC.activation(rbcst[:, :], lnt[:, :], ACT.Exp, scale=-0.5), esc)
                    for dcm in range(DC):
                        xs = x_res[:, dcm * TOWN + t0: dcm * TOWN + t0 + TT]
                        ins = V.tensor_sub(xln[:, dcm * TOWN + t0:dcm * TOWN + t0 + TT],
                                           xs, mbc[:, :])
                    sub_ms = ctx.inc(ins, ev)
                    ctx.W("vector", ev, sub_ms)
                    ctx.W("vector", esc, sc2)
                    for dcm in range(DC):
                        ins = V.tensor_mul(xln[:, dcm * TOWN + t0:dcm * TOWN + t0 + TT],
                                           xln[:, dcm * TOWN + t0:dcm * TOWN + t0 + TT],
                                           rbcst[:, :])
                    xln_ts = ctx.inc(ins, ev)
                    ctx.marks[("xln", ts)] = xln_ts
                    ctx.W("tensor", ev, xln_ts)
                    ctx.W("tensor", dgp, base_gp)
                    for dcm in range(DC):
                        ins = T.matmul(pb[6][0:NROUTED, 0:TT],
                                       rw_sb[:, dcm * NROUTED:(dcm + 1) * NROUTED],
                                       xln[:, dcm * TOWN + t0:dcm * TOWN + t0 + TT],
                                       start=(dcm == 0), stop=(dcm == DC - 1))
                    rt_ms = ctx.inc(ins, et)
                    ctx.W("vector", et, rt_ms)
                    lg_ms = ctx.inc(V.tensor_scalar(l_sb[:, t0:t0 + TT], pb[6][0:NROUTED, 0:TT],
                                                    rb_sb[:, :], None, ALU.add), ev)
                    ctx.W("tensor", ev, lg_ms)
                    for g in range(4):
                        ins = T.transpose(pbT[:, g * 8:(g + 1) * 8],
                                          l_sb[:, t0 + g * 128:t0 + (g + 1) * 128],
                                          ident[0:8, 0:8])
                    tr_ms = ctx.inc(ins, et)
                    prev_gt = tr_ms
                    for g in range(4):
                        lt = pbT[:, g * 8:(g + 1) * 8]
                        ctx.W("vector", et, prev_gt)
                        a1 = ctx.inc(V.reduce_max(topf[:, 0:1], lt, axis=AX, negate=True), ev)
                        ctx.W("vector", ev, a1)
                        a2 = ctx.inc(V.tensor_scalar(topf[:, 8:16], lt, topf[:, 0:1], 0.0,
                                                     ALU.add, ALU.is_lt), ev)
                        ctx.W("vector", ev, a2)
                        a2b = ctx.inc(V.tensor_scalar(topf[:, 40:48], topf[:, 8:16], 1.0, 1.0e30,
                                                      ALU.subtract, ALU.mult), ev)
                        a3a = ctx.inc(V.tensor_mul(topf[:, 16:24], lt, topf[:, 8:16]), ev)
                        ctx.W("vector", ev, a3a)
                        a3 = ctx.inc(V.tensor_add(topf[:, 16:24], topf[:, 16:24],
                                                  topf[:, 40:48]), ev)
                        ctx.W("vector", ev, a3)
                        a4 = ctx.inc(V.reduce_max(topf[:, 1:2], topf[:, 16:24], axis=AX), ev)
                        ctx.W("scalar", ev, a1)
                        ctx.W("scalar", et, tr_ms)
                        e1 = ctx.inc(SC.activation(topf[:, 24:32], lt, ACT.Exp, bias=topf[:, 0:1]),
                                     esc)
                        ctx.W("vector", ev, a4)
                        a5 = ctx.inc(V.tensor_single_scalar(topf[:, 32:40], lt, topf[:, 1:2],
                                                            ALU.is_ge), ev)
                        ctx.W("vector", esc, e1)
                        ctx.W("vector", ev, a5)
                        a6 = ctx.inc(V.tensor_mul(ge_sb[:, :], topf[:, 24:32], topf[:, 32:40]), ev)
                        ctx.W("vector", ev, a6)
                        a7 = ctx.inc(V.reduce_sum(topf[:, 2:3], ge_sb[:, :], axis=AX), ev)
                        ctx.W("vector", ev, a7)
                        a8 = ctx.inc(V.reciprocal(topf[:, 3:4], topf[:, 2:3]), ev)
                        ctx.W("vector", ev, a8)
                        a9 = ctx.inc(V.tensor_scalar(gbf[:, :], ge_sb[:, :], topf[:, 3:4], None,
                                                     ALU.mult), ev)
                        ctx.W("tensor", ev, a9)
                        gt = ctx.inc(T.transpose(pbT[0:NROUTED, 512 + g * 128:512 + (g + 1) * 128],
                                                 gbf[:, :], ident[:, :]), et)
                        prev_gt = gt
                        ctx.W("vector", et, gt)
                        ctx.inc(V.tensor_copy(g_sb[:, t0 + g * 128:t0 + (g + 1) * 128],
                                              pbT[0:NROUTED, 512 + g * 128:512 + (g + 1) * 128]),
                                ev)
                    ctx.marks[("gts", ts)] = ev.n
                xln_ms = ev.n
                gates_ms = ev.n

                EORD = [NROUTED, NROUTED + 1] + list(range(NROUTED))
                for ei, e in enumerate(EORD):
                    slot = ei % 2
                    ctx.W("tensor", dw[slot], w13_loaded[slot])
                    for ts in range(4):
                        t0 = ts * TT
                        ctx.W("tensor", ev, ctx.marks.get(("xln", ts), 0))
                        if e < NROUTED:
                            ctx.W("tensor", ev, ctx.marks.get(("gts", ts), 0))
                            ctx.W("tensor", ev, ctx.marks.get(("gbread",), 0))
                            gbc = ctx.inc(T.matmul(pb[6][:, 0:TT],
                                                   oneh[:, e * 128:(e + 1) * 128],
                                                   g_sb[:, t0:t0 + TT],
                                                   start=True, stop=True), et)
                        for hc in range(4):
                            par = hc % 2
                            bh1, bh3 = pb[0 + 2 * par], pb[1 + 2 * par]
                            ctx.W("tensor", ev, ctx.marks.get(("s1ev", par), 0))
                            for dcm in range(DC):
                                T.matmul(bh1[:, 0:TT],
                                         w13[slot][:, dcm * HFF + hc * 128:
                                                   dcm * HFF + hc * 128 + 128],
                                         xln[:, dcm * TOWN + t0:dcm * TOWN + t0 + TT],
                                         start=(dcm == 0), stop=(dcm == DC - 1))
                            for dcm in range(DC):
                                ins = T.matmul(bh3[:, 0:TT],
                                               w13[slot][:, DC * HFF + dcm * HFF + hc * 128:
                                                         DC * HFF + dcm * HFF + hc * 128 + 128],
                                               xln[:, dcm * TOWN + t0:dcm * TOWN + t0 + TT],
                                               start=(dcm == 0), stop=(dcm == DC - 1))
                            mm_ms = ctx.inc(ins, et)
                            sl = silu_sb[:, par * TT:(par + 1) * TT]
                            ctx.W("scalar", et, mm_ms)
                            ctx.W("scalar", ev, ctx.marks.get(("s1ev", par), 0))
                            sl_ms = ctx.inc(SC.activation(sl, bh1[:, 0:TT], ACT.Silu), esc)
                            hho = hh[:, (hc * 4 + ts) * TT:(hc * 4 + ts + 1) * TT]
                            ctx.W("vector", esc, sl_ms)
                            if e < NROUTED:
                                tv = ctx.inc(V.tensor_mul(sl, sl, bh3[:, 0:TT]), ev)
                                ctx.W("vector", ev, tv)
                                ctx.W("vector", et, gbc)
                                s1e = ctx.inc(V.tensor_mul(hho, sl, pb[6][:, 0:TT]), ev)
                                ctx.marks[("gbread",)] = s1e
                            else:
                                s1e = ctx.inc(V.tensor_mul(hho, sl, bh3[:, 0:TT]), ev)
                            ctx.marks[("s1ev", par)] = s1e
                    hh_ms = ev.n
                    ctx.W("tensor", dwx[slot], w2_loaded[slot])
                    ctx.W("tensor", ev, hh_ms)
                    for dcm in range(DC):
                        for ts in range(4):
                            t0 = ts * TT
                            par = (dcm * 4 + ts) % 2
                            bank = pb[4 + par]
                            ctx.W("tensor", ev, ctx.marks.get(("s2ev", par), 0))
                            for hc in range(4):
                                ins = T.matmul(bank[:, 0:TT],
                                               w2b[slot][:, (hc * DC + dcm) * 128:
                                                         (hc * DC + dcm) * 128 + 128],
                                               hh[:, (hc * 4 + ts) * TT:(hc * 4 + ts + 1) * TT],
                                               start=(hc == 0), stop=(hc == 3))
                            o_ms = ctx.inc(ins, et)
                            ctx.W("vector", et, o_ms)
                            upd = ctx.inc(V.scalar_tensor_tensor(
                                x_res[:, dcm * TOWN + t0: dcm * TOWN + t0 + TT],
                                bank[:, 0:TT], 0.5,
                                x_res[:, dcm * TOWN + t0: dcm * TOWN + t0 + TT],
                                ALU.mult, ALU.add), ev)
                            ctx.marks[("s2ev", par)] = upd
                    if ei + 2 < NEXP:
                        issue_w(EORD[ei + 2], slot, et.n)
                        w13_loaded[slot] = w13_expect[slot]
                        w2_loaded[slot] = w2_expect[slot]
                ctx.barrier()

        def emit_ln2_and_ag1():
            with contextlib.ExitStack() as es:
                fsb = lambda name, shape, dt: es.enter_context(
                    nc.sbuf_tensor(f"ln2_{name}", shape, dt))
                xbf8 = fsb("xbf8", [128, DC * TT], BF16)
                sq = fsb("sq", [128, TT], BF16)
                xln = fsb("xln", [128, DC * TT], BF16)
                mbc = fsb("mbc", [128, TT], F32)
                ex2 = fsb("ex2", [128, TT], F32)
                rbcst = fsb("rbcst", [128, TT], F32)
                lnt = fsb("lnt", [128, TT], F32)
                for tile in range(NTILES):
                    t0 = tile * TT
                    ctx.W("vector", dsy, base_sy)
                    ctx.W("vector", dxa, dxa.n)
                    for dcm in range(DC):
                        xs = x_res[:, dcm * TOWN + t0: dcm * TOWN + t0 + TT]
                        ins = V.tensor_copy(xbf8[:, dcm * TT:(dcm + 1) * TT], xs)
                    cast_ms = ctx.inc(ins, ev)
                    ctx.W("tensor", ev, cast_ms)
                    for dcm in range(DC):
                        ins = T.matmul(pb[4][:, 0:TT], ones[:, :], xbf8[:, dcm * TT:(dcm + 1) * TT],
                                       start=(dcm == 0), stop=(dcm == DC - 1))
                    ctx.inc(ins, et)
                    sq_read_ms = None
                    for dcm in range(DC):
                        if sq_read_ms is not None:
                            ctx.W("vector", et, sq_read_ms)
                        sqv = ctx.inc(
                            V.tensor_mul(sq[:, :], x_res[:, dcm * TOWN + t0:dcm * TOWN + t0 + TT],
                                         x_res[:, dcm * TOWN + t0:dcm * TOWN + t0 + TT]), ev)
                        ctx.W("tensor", ev, sqv)
                        sq_read_ms = ctx.inc(
                            T.matmul(pb[5][:, 0:TT], ones[:, :], sq[:, :],
                                     start=(dcm == 0), stop=(dcm == DC - 1)), et)
                    sums_ms = sq_read_ms
                    ctx.W("vector", et, sums_ms)
                    V.tensor_scalar(mbc[:, :], pb[4][:, 0:TT], 1.0 / D, None, ALU.mult)
                    mb_ms = ctx.inc(V.tensor_scalar(ex2[:, :], pb[5][:, 0:TT], 1.0 / D, None,
                                                    ALU.mult), ev)
                    ctx.W("vector", ev, mb_ms)
                    v1 = ctx.inc(V.tensor_mul(lnt[:, :], mbc[:, :], mbc[:, :]), ev)
                    ctx.W("vector", ev, v1)
                    v2 = ctx.inc(V.scalar_tensor_tensor(lnt[:, :], ex2[:, :], 1e-5,
                                                        lnt[:, :], ALU.add, ALU.subtract), ev)
                    ctx.W("scalar", ev, v2)
                    scA = ctx.inc(SC.activation(lnt[:, :], lnt[:, :], ACT.Ln), esc)
                    ctx.W("scalar", esc, scA)
                    scB = ctx.inc(SC.activation(rbcst[:, :], lnt[:, :], ACT.Exp, scale=-0.5), esc)
                    for dcm in range(DC):
                        xs = x_res[:, dcm * TOWN + t0: dcm * TOWN + t0 + TT]
                        ins = V.tensor_sub(xbf8[:, dcm * TT:(dcm + 1) * TT], xs, mbc[:, :])
                    sub_ms = ctx.inc(ins, ev)
                    ctx.W("vector", ev, sub_ms)
                    ctx.W("vector", esc, scB)
                    for dcm in range(DC):
                        ins = V.tensor_mul(xln[:, dcm * TT:(dcm + 1) * TT],
                                           xbf8[:, dcm * TT:(dcm + 1) * TT], rbcst[:, :])
                    xln_ms = ctx.inc(ins, ev)
                    ctx.W("sync", ev, xln_ms)
                    SY.dma_start(out=cc_xa_in.ap()[tile], in_=xln[:, :]).then_inc(dxa.h, 16)
                    dxa.n += 16
                    ctx.W("gpsimd", dxa, dxa.n)
                    GP.collective_compute(
                        "AllGather", ALU.bypass, replica_groups=[list(range(N_CORES))],
                        ins=[cc_xa_in.ap()[tile].opt()], outs=[cc_xa_out.ap()[tile].opt()],
                    ).then_inc(dcc.h, 1)
                    dcc.n += 1
                ctx.barrier(skip=("dcc",))

        def emit_attention():
            with contextlib.ExitStack() as es:
                fsb = lambda name, shape, dt: es.enter_context(
                    nc.sbuf_tensor(f"at_{name}", shape, dt))
                wq_sb = fsb("wq", [128, 128], BF16)
                wk_sb = fsb("wk", [128, 128], BF16)
                wv_sb = fsb("wv", [128, 128], BF16)
                wq8 = fsb("wq8", [128, DC * 128], BF16)
                wk8 = fsb("wk8", [128, DC * 128], BF16)
                wv8 = fsb("wv8", [128, DC * 128], BF16)
                xa = [fsb("xa0", [128, DC * 512], BF16), fsb("xa1", [128, DC * 512], BF16)]
                qt_sb = fsb("qt", [128, S], BF16)
                kt_sb = fsb("kt", [128, S], BF16)
                v_tok = fsb("v_tok", [128, 32 * 144], BF16)
                vp_tok = fsb("vp_tok", [64, 64 * 72], BF16)
                vsum = fsb("vsum", [64, 72], BF16)
                ksum = fsb("ksum", [128, 64], BF16)
                kp_sb = fsb("kp", [64, S], BF16)
                qp_sb = fsb("qp", [64, S], BF16)
                o_acc = fsb("o_acc", [65, S], BF16)
                o_stage = fsb("o_stage", [64, S], BF16)
                zbuf = fsb("zbuf", [66, S], F32)
                pexp = fsb("pexp", [128, 512], BF16)
                pt_sb = fsb("pt", [128, 512], BF16)
                esb = fsb("esb", [128, 512], BF16)
                pts = fsb("pts", [128, 512], BF16)
                psum_ex = fsb("psum_ex", [64, 256], BF16)

                for w_dram, w_s in ((wqT, wq8), (wkT, wk8), (wvT, wv8)):
                    GP.dma_start(out=w_s[:, :].rearrange("p (c f) -> p c f", c=DC),
                                 in_=w_dram.rearrange("(c p) f -> p c f", p=128)).then_inc(dgp.h, 16)
                    dgp.n += 16
                wload = dgp.n
                ones_ms = ctx.inc(V.memset(
                    v_tok[:, :].rearrange("p (s w) -> p s w", w=144)[:, :, 64:65], 1.0), ev)
                ones_ms = ctx.inc(V.memset(
                    v_tok[:, :].rearrange("p (s w) -> p s w", w=144)[:, :, 136:137], 1.0), ev)

                for b in range(B):
                    for tg in range(8):
                        slot = tg % 2
                        shard = 2 * b + (tg // 4)
                        ctx.W("sync", dcc, (tg % 4) + 1)
                        ctx.W("sync", et, ctx.marks.get(("xaw", slot), 0))
                        SY.dma_start(
                            out=xa[slot][:, :],
                            in_=cc_xa_out.ap()[tg % 4, shard],
                        ).then_inc(dx[slot].h, 16)
                        dx[slot].n += 16
                        ctx.W("tensor", dx[slot], dx[slot].n)
                        ctx.W("tensor", dgp, wload)
                        ctx.W("tensor", ev, ctx.marks.get(("qkv_evac", tg % 2), 0))
                        ctx.W("tensor", esc, ctx.marks.get(("qkv_evac_sc", tg % 2), 0))
                        qbQ, qbK, qbV = pb[3 * (tg % 2)], pb[3 * (tg % 2) + 1], pb[3 * (tg % 2) + 2]
                        for dcm in range(DC):
                            T.matmul(qbQ[:, 0:512], wq8[:, dcm * 128:(dcm + 1) * 128],
                                     xa[slot][:, dcm * 512:(dcm + 1) * 512],
                                     start=(dcm == 0), stop=(dcm == DC - 1))
                        for dcm in range(DC):
                            T.matmul(qbK[:, 0:512], wk8[:, dcm * 128:(dcm + 1) * 128],
                                     xa[slot][:, dcm * 512:(dcm + 1) * 512],
                                     start=(dcm == 0), stop=(dcm == DC - 1))
                        for sub in range(4):
                            for dcm in range(DC):
                                ins = T.matmul(qbV[:, sub * 128:(sub + 1) * 128],
                                               xa[slot][:, dcm * 512 + sub * 128:
                                                        dcm * 512 + sub * 128 + 128],
                                               wv8[:, dcm * 128:(dcm + 1) * 128],
                                               start=(dcm == 0), stop=(dcm == DC - 1))
                        mm_ms = ctx.inc(ins, et)
                        ctx.marks[("xaw", slot)] = mm_ms
                        tcol = tg * 512
                        ctx.W("vector", et, mm_ms)
                        ctx.W("vector", ev, ones_ms)
                        ctx.W("scalar", et, mm_ms)
                        ctx.W("scalar", ev, ones_ms)
                        V.tensor_scalar(qt_sb[:, tcol:tcol + 512], qbQ[:, 0:512], 0.125,
                                        None, ALU.mult)
                        sc_ins = SC.copy(kt_sb[:, tcol:tcol + 512], qbK[:, 0:512])
                        for sub in range(4):
                            kt_i = tg * 4 + sub
                            V.tensor_copy(v_tok[:, kt_i * 144:kt_i * 144 + 64],
                                          qbV[0:128, sub * 128:sub * 128 + 64])
                            ins = V.tensor_copy(v_tok[:, kt_i * 144 + 72:kt_i * 144 + 136],
                                                qbV[0:128, sub * 128 + 64:sub * 128 + 128])
                        ev_ms = ctx.inc(ins, ev)
                        sc_ms = ctx.inc(sc_ins, esc)
                        ctx.marks[("qkv_evac", tg % 2)] = ev_ms
                        ctx.marks[("qkv_evac_sc", tg % 2)] = sc_ms
                    qkv_ms = ev.n
                    ctx.marks[("qkvsc",)] = esc.n

                    for h in range(2):
                        p0 = 0 if h == 0 else 64
                        tp = None if h == 0 else (64, 0)
                        mmulT = (mAT, mA0T) if h == 0 else (mBT, mB0T)

                        if h == 0 and ATT["strided"] >= 1:
                            ctx.W("vector", ev, qkv_ms)
                            ctx.W("vector", esc, ctx.marks.get(("qkvsc",), 0))
                            ctx.W("vector", et, et.n)
                            kpv = ctx.inc(V.tensor_copy(
                                kp_sb[:, :].rearrange("p (r bj) -> p r bj", r=64),
                                kt_sb[0:64, :].rearrange("p (bj r) -> p r bj", bj=64)), ev)
                            qpv = ctx.inc(V.tensor_copy(
                                qp_sb[:, :].rearrange("p (r bj) -> p r bj", r=64),
                                qt_sb[0:64, :].rearrange("p (bj r) -> p r bj", bj=64)), ev)
                            ctx.W("sync", ev, qkv_ms)
                            ctx.W("sync", esc, ctx.marks.get(("qkvsc",), 0))
                            ctx.W("sync", et, et.n)
                            ctx.W("sync", dvp, dvp.n)
                            SY.dma_start(
                                out=vperm_dr.ap().rearrange("(s p) f -> p s f", p=128)[:, :, 0:65],
                                in_=v_tok[:, :].rearrange("p (s w) -> p s w", w=144)[:, :, 0:65],
                            ).then_inc(dvp.h, 16)
                            SY.dma_start(
                                out=vperm_dr.ap().rearrange("(s p) f -> p s f", p=128)[:, :, 65:130],
                                in_=v_tok[:, :].rearrange("p (s w) -> p s w", w=144)[:, :, 72:137],
                            ).then_inc(dvp.h, 16)
                            dvp.n += 32
                            ctx.W("sync", dvp, dvp.n)
                            SY.dma_start(
                                out=vp_tok[:, :].rearrange("p (s w) -> p s w", w=72)[:, :, 0:65],
                                in_=vperm_dr.ap()
                                    .rearrange("(bj r) f -> bj r f", r=64)[:, :, 0:65],
                            ).then_inc(dvp.h, 16)
                            dvp.n += 16
                        elif h == 1 and ATT["summary"]:
                            ctx.W("vector", ev, qkv_ms)
                            ctx.W("vector", esc, ctx.marks.get(("qkvsc",), 0))
                            ctx.W("vector", et, et.n)
                            ksv = ctx.inc(V.tensor_copy(
                                ksum[64:128, :],
                                kt_sb[64:128, :].rearrange("p (c r) -> p c r", r=64)[:, :, 63]), ev)
                            ctx.W("sync", ev, qkv_ms)
                            ctx.W("sync", esc, ctx.marks.get(("qkvsc",), 0))
                            ctx.W("sync", et, et.n)
                            ctx.W("sync", dvp, dvp.n)
                            SY.dma_start(
                                out=vsum[:, 0:65],
                                in_=vperm_dr.ap()
                                    .rearrange("(c u) f -> c u f", u=64)[:, 63, 65:130],
                            ).then_inc(dvs.h, 16)
                            dvs.n += 16

                        if not ATT["passb"]:
                            continue
                        ctx.W("tensor", ev, qkv_ms)
                        ctx.W("tensor", esc, ctx.marks.get(("qkvsc",), 0))
                        if h == 1:
                            ctx.W("tensor", dvs, dvs.n)
                            ctx.W("tensor", ev, ev.n)
                        mk2 = (mAT2c, mAT2c0) if h == 0 else (mBT2c, mBT2c0)
                        for qp2 in range(NQT // 2):
                            bank = pb[2 + (qp2 % 2)]
                            bo = pb[4 + (qp2 % 2)]
                            ctx.W("tensor", esc, ctx.marks.get(("pbexp", qp2 % 2), 0))
                            for qh in range(2):
                                qt = 2 * qp2 + qh
                                kc0 = 0 if qt == 0 else (qt - 1) * 128
                                for half in range(2):
                                    ins = T.matmul(
                                        bank[:, qh * 256 + half * 128:qh * 256 + half * 128 + 128],
                                        kt_sb[p0:p0 + 64,
                                              kc0 + half * 128:kc0 + half * 128 + 128],
                                        qt_sb[p0:p0 + 64, qt * 128:(qt + 1) * 128],
                                        start=True, stop=True, tile_position=tp)
                            if h == 1 and ATT["summary"]:
                                ctx.W("tensor", esc, ctx.marks.get(("sumexS",), 0))
                                for qh in range(2):
                                    qt = 2 * qp2 + qh
                                    ins = T.matmul(pb[6][0:64, qh * 128:(qh + 1) * 128],
                                                   ksum[64:128, :],
                                                   qt_sb[64:128, qt * 128:(qt + 1) * 128],
                                                   start=True, stop=True, tile_position=(64, 0))
                            pbm = ctx.inc(ins, et)
                            ctx.W("scalar", et, pbm)
                            ctx.W("scalar", ev, ctx.marks.get(("pbpt", qp2 % 2), 0))
                            ex1 = ctx.inc(SC.activation(pexp[:, :], bank[:, 0:512], ACT.Exp), esc)
                            ctx.marks[("pbexp", qp2 % 2)] = ex1
                            ctx.W("vector", esc, ex1)
                            mkT = mk2[1] if qp2 == 0 else mk2[0]
                            pt1 = ctx.inc(V.tensor_mul(pt_sb[:, :], pexp[:, :], mkT[:, :]), ev)
                            ctx.marks[("pbpt", qp2 % 2)] = pt1
                            if h == 1 and ATT["summary"]:
                                ctx.W("scalar", ev, ctx.marks.get(("sumex",), 0))
                                sx = ctx.inc(SC.activation(psum_ex[:, :], pb[6][0:64, 0:256],
                                                           ACT.Exp), esc)
                                ctx.marks[("sumexS",)] = sx
                                ctx.W("vector", esc, sx)
                                ctx.marks[("sumex",)] = ctx.inc(V.tensor_mul(
                                    psum_ex[:, :], psum_ex[:, :],
                                    msum[:, qp2 * 256:(qp2 + 1) * 256]), ev)
                            ctx.W("tensor", ev, pt1)
                            if h == 1 and ATT["summary"]:
                                ctx.W("tensor", ev, ctx.marks.get(("sumex",), 0))
                            ctx.W("tensor", ev, ctx.marks.get(("poev", qp2 % 2), 0))
                            for qh in range(2):
                                qt = 2 * qp2 + qh
                                kt0 = 0 if qt == 0 else qt - 1
                                for half in range(2):
                                    ktile = kt0 + half
                                    ins = T.matmul(
                                        bo[0:65, qh * 128:(qh + 1) * 128],
                                        v_tok[:, ktile * 144 + h * 72:ktile * 144 + h * 72 + 65],
                                        pt_sb[:, qh * 256 + half * 128:
                                              qh * 256 + half * 128 + 128],
                                        start=(half == 0),
                                        stop=(half == 1 and (h == 0 or not ATT["summary"])))
                                if h == 1 and ATT["summary"]:
                                    ins = T.matmul(bo[0:65, qh * 128:(qh + 1) * 128],
                                                   vsum[:, 0:65],
                                                   psum_ex[:, qh * 128:(qh + 1) * 128],
                                                   start=False, stop=True)
                            pv_ms = ctx.inc(ins, et)
                            ctx.W("vector", et, pv_ms)
                            ctx.marks[("poev", qp2 % 2)] = ctx.inc(
                                V.tensor_copy(o_acc[:, qp2 * 256:(qp2 + 1) * 256],
                                              bo[0:65, 0:256]), ev)

                        if h == 0 and ATT["strided"] >= 2:
                            ctx.W("tensor", ev, ev.n)
                            ctx.W("tensor", esc, esc.n)
                            ctx.W("tensor", dvp, dvp.n)
                            for rg in range(8):
                                bs = pb[2 + (rg % 2)]
                                bos = pb[4 + (rg % 2)]
                                ctx.W("tensor", esc, ctx.marks.get(("strex", rg % 2), 0))
                                ctx.W("tensor", ev, ctx.marks.get(("stradd", rg % 2), 0))
                                for rr in range(8):
                                    r = rg * 8 + rr
                                    ins = T.matmul(bs[0:64, rr * 64:(rr + 1) * 64],
                                                   kp_sb[:, r * 64:(r + 1) * 64],
                                                   qp_sb[:, r * 64:(r + 1) * 64],
                                                   start=True, stop=True)
                                sm = ctx.inc(ins, et)
                                ctx.W("scalar", et, sm)
                                ctx.W("scalar", ev, ctx.marks.get(("strpt", rg % 2), 0))
                                se = ctx.inc(SC.activation(esb[0:64, :], bs[0:64, 0:512],
                                                           ACT.Exp), esc)
                                ctx.marks[("strex", rg % 2)] = se
                                ctx.W("vector", esc, se)
                                sp = ctx.inc(V.tensor_mul(pts[0:64, :], esb[0:64, :],
                                                          mstr[0:64, :]), ev)
                                ctx.marks[("strpt", rg % 2)] = sp
                                ctx.W("tensor", ev, sp)
                                if ATT["strided"] < 3:
                                    ctx.marks[("stradd", rg % 2)] = sp
                                    continue
                                for rr in range(8):
                                    r = rg * 8 + rr
                                    ins = T.matmul(bos[0:65, rr * 64:(rr + 1) * 64],
                                                   vp_tok[:, r * 72:r * 72 + 65],
                                                   pts[0:64, rr * 64:(rr + 1) * 64],
                                                   start=True, stop=True)
                                spv = ctx.inc(ins, et)
                                ctx.W("vector", et, spv)
                                oacc_v = o_acc[:, :].rearrange("p (qi r) -> p r qi",
                                                               r=64)[:, rg * 8:(rg + 1) * 8, :]
                                ctx.marks[("stradd", rg % 2)] = ctx.inc(V.tensor_add(
                                    oacc_v, oacc_v,
                                    bos[0:65, 0:512].rearrange("p (rr qi) -> p rr qi", rr=8)), ev)

                        if not ATT["norm"]:
                            continue
                        ctx.W("vector", ev, ev.n)
                        nz0 = ctx.inc(V.tensor_copy(zbuf[64:65, :], o_acc[64:65, :]), ev)
                        ctx.W("scalar", ev, nz0)
                        z1 = ctx.inc(SC.activation(zbuf[64:65, :], zbuf[64:65, :], ACT.Ln), esc)
                        ctx.W("scalar", esc, z1)
                        z2 = ctx.inc(SC.activation(zbuf[64:65, :], zbuf[64:65, :], ACT.Exp,
                                                   scale=-1.0), esc)
                        ctx.W("tensor", esc, z2)
                        ctx.W("vector", dco, dco.n)
                        for g in range(8):
                            bz = pb[2 + (g % 2)]
                            ctx.W("tensor", ev, ctx.marks.get(("zev", g % 2), 0))
                            zm = ctx.inc(T.matmul(bz[0:64, 0:512], onesf[64:65, 0:64],
                                                  zbuf[64:65, g * 512:(g + 1) * 512],
                                                  start=True, stop=True), et)
                            ctx.W("vector", et, zm)
                            ctx.marks[("zev", g % 2)] = ctx.inc(
                                V.tensor_mul(o_stage[:, g * 512:(g + 1) * 512],
                                             o_acc[0:64, g * 512:(g + 1) * 512],
                                             bz[0:64, 0:512]), ev)
                        ctx.W("sync", ev, ev.n)
                        SY.dma_start(out=cc_o_in.ap()[b, h], in_=o_stage[:, :]).then_inc(dco.h, 16)
                        dco.n += 16
                    if ATT["ag2"]:
                        ctx.W("gpsimd", dco, dco.n)
                        GP.collective_compute(
                            "AllGather", ALU.bypass, replica_groups=[list(range(N_CORES))],
                            ins=[cc_o_in.ap()[b].opt()], outs=[cc_o_out.ap()[b].opt()],
                        ).then_inc(dcc.h, 1)
                        dcc.n += 1
                    ctx.barrier(engines=("tensor", "vector", "scalar"), skip=("dcc",))
                ctx.barrier(skip=("dcc",))

        def emit_wo():
            with contextlib.ExitStack() as es:
                fsb = lambda name, shape, dt: es.enter_context(
                    nc.sbuf_tensor(f"wo_{name}", shape, dt))
                wo_sb = fsb("wo", [128, DC * D], BF16)
                og = [fsb("og0", [128, DC * 512], BF16), fsb("og1", [128, DC * 512], BF16)]
                GP.dma_start(out=wo_sb[:, :].rearrange("p (c f) -> p c f", c=DC),
                             in_=woT.rearrange("(c p) f -> p c f", p=128)).then_inc(dgp.h, 16)
                dgp.n += 16
                wo_ld = dgp.n
                pid = SY.partition_id()
                row0 = SY.compute_val((pid // 2) * (N_CORES * 2 * 64))
                colb = SY.compute_val((pid % 2) * TOWN)
                flat = cc_o_out.ap().rearrange("b n h d t -> (b n h d) t")
                ctx.W("sync", dcc, dcc.n)
                for ts in range(4):
                    slot = ts % 2
                    ctx.W("sync", et, ctx.marks.get(("ogw", slot), 0))
                    col = SY.compute_val(colb + ts * 512)
                    SY.dma_start(
                        out=og[slot][:, :].rearrange("p (c t) -> p c t", c=DC),
                        in_=flat[bass.ds(row0, 1024), bass.ds(col, 512)]
                            .rearrange("(c p) t -> p c t", p=128),
                    ).then_inc(dol[slot].h, 16)
                    dol[slot].n += 16
                    ctx.W("tensor", dol[slot], dol[slot].n)
                    ctx.W("tensor", dgp, wo_ld)
                    t0 = ts * 512
                    for oc in range(DC):
                        bank = pb[4 + (oc % 2)]
                        ctx.W("tensor", ev, ctx.marks.get(("woev", oc % 2), 0))
                        for ic in range(DC):
                            ins = T.matmul(bank[:, 0:512],
                                           wo_sb[:, ic * D + oc * 128:ic * D + oc * 128 + 128],
                                           og[slot][:, ic * 512:(ic + 1) * 512],
                                           start=(ic == 0), stop=(ic == DC - 1))
                        wm = ctx.inc(ins, et)
                        ctx.marks[("ogw", slot)] = wm
                        ctx.W("vector", et, wm)
                        ctx.marks[("woev", oc % 2)] = ctx.inc(V.tensor_add(
                            x_res[:, oc * TOWN + t0: oc * TOWN + t0 + 512],
                            x_res[:, oc * TOWN + t0: oc * TOWN + t0 + 512],
                            bank[:, 0:512]), ev)
                ctx.barrier()

        if PHASES.get("ffn1"):
            emit_ffn("f1")
        if PHASES.get("attn"):
            emit_ln2_and_ag1()
            emit_attention()
            if ATT["wo"]:
                emit_wo()
        if PHASES.get("ffn2"):
            emit_ffn("f2")

        ctx.barrier()
        SY.dma_start(out=yT.rearrange("(c p) t -> p c t", p=128),
                     in_=x_res[:, :].rearrange("p (c t) -> p c t", c=DC)).then_inc(dst.h, 16)
        dst.n += 16
        ctx.W("sync", dst, dst.n)

    return nc


_NC = None


def _in_maps(inputs):
    x = np.asarray(inputs["L0_attn_x"], dtype=np.float32).reshape(B * S, D)
    Wq = np.asarray(inputs["L0_attn_Wq"], dtype=np.float32)
    Wk = np.asarray(inputs["L0_attn_Wk"], dtype=np.float32)
    Wv = np.asarray(inputs["L0_attn_Wv"], dtype=np.float32)
    Wo = np.asarray(inputs["L0_attn_Wo"], dtype=np.float32)
    rw_ = np.ascontiguousarray(np.asarray(inputs["L0_ffn_router_w"], dtype=np.float32))
    rb_ = np.asarray(inputs["L0_ffn_router_b"], dtype=np.float32).reshape(NROUTED, 1)
    w1_ = np.asarray(inputs["L0_ffn_routed_w1"], dtype=np.float32)
    w3_ = np.asarray(inputs["L0_ffn_routed_w3"], dtype=np.float32)
    w2_ = np.asarray(inputs["L0_ffn_routed_w2"], dtype=np.float32)
    s1_ = np.asarray(inputs["L0_ffn_shared_w1"], dtype=np.float32)
    s3_ = np.asarray(inputs["L0_ffn_shared_w3"], dtype=np.float32)
    s2_ = np.asarray(inputs["L0_ffn_shared_w2"], dtype=np.float32)
    def _w13_part(wa):
        return wa.reshape(-1, DC, 128, HFF).transpose(0, 2, 1, 3).reshape(-1, 128, DC * HFF)
    w13cat = np.concatenate([
        np.concatenate([_w13_part(w1_), _w13_part(w3_)], axis=2),
        np.concatenate([_w13_part(s1_), _w13_part(s3_)], axis=2)], axis=0)
    w13cat = np.ascontiguousarray(w13cat)
    def _w2_part(wa):
        return wa.reshape(-1, 4, 128, DC, 128).transpose(0, 2, 1, 3, 4)
    w2cat = np.concatenate([_w2_part(w2_), _w2_part(s2_)], axis=0)
    w2cat = np.ascontiguousarray(w2cat.reshape(NEXP, 128, 4 * DC * 128))
    WqT, WkT, WvT, WoT = Wq.T, Wk.T, Wv.T, Wo.T
    perm = np.concatenate([np.r_[64 * c:64 * c + 64, 64 * (c + 8):64 * (c + 8) + 64]
                           for c in range(N_CORES)])
    WoTp = np.ascontiguousarray(WoT[perm, :])
    maps = []
    for c in range(N_CORES):
        cols = np.r_[64 * c:64 * c + 64, 64 * (c + 8):64 * (c + 8) + 64]
        maps.append({
            "xT": np.ascontiguousarray(x[c * TOWN:(c + 1) * TOWN].T),
            "wqT": np.ascontiguousarray(WqT[:, cols]),
            "wkT": np.ascontiguousarray(WkT[:, cols]),
            "wvT": np.ascontiguousarray(WvT[:, cols]),
            "woT": WoTp,
            "rw": rw_, "rbc": rb_, "w13d": w13cat, "w2d": w2cat,
        })
    return maps


def kernel(**inputs):
    global _NC
    if _NC is None:
        _NC = build_nc()
    maps = _in_maps(inputs)
    res = run_bass_kernel_spmd(_NC, maps, core_ids=list(range(N_CORES)))
    out = np.empty((B * S, D), dtype=np.float32)
    for c in range(N_CORES):
        out[c * TOWN:(c + 1) * TOWN] = np.asarray(res.results[c]["yT"]).T
    return out.reshape(B, S, D)
